# version 17
# baseline (speedup 1.0000x reference)
import os
import numpy as np
import ml_dtypes
import concourse.bass as bass
import concourse.mybir as mybir
from concourse.bass_utils import run_bass_kernel_spmd

F32 = mybir.dt.float32
BF16 = mybir.dt.bfloat16
AF = mybir.ActivationFunctionType
ALU = mybir.AluOpType
BF = ml_dtypes.bfloat16

T = 4096
D = 1024
NCH = 8
NT = T // 512
NC = T // 128

ENGS = ("sync", "vector", "scalar", "tensor", "gpsimd")


class Prog:

    def __init__(self, nc):
        self.nc = nc
        self.sems = {e: nc.alloc_semaphore(f"S_{e}") for e in ENGS}
        self.fsems = {e: nc.alloc_semaphore(f"F_{e}") for e in ENGS}
        self.n_flush = {e: 0 for e in ENGS}
        self.dsems = {}
        self.rec = {e: [] for e in ENGS}
        self.n_ops = {e: 0 for e in ENGS}
        self.n_dma = {}
        self.need_inc = {e: set() for e in ENGS}
        self.last_wait = {}
        self.drain_mark = {e: 0 for e in ENGS}
        self.regions = {}
        self.dma_flush = {}
        self.dma_flush_tok = {}

    def _region(self, key):
        r = self.regions.get(key)
        if r is None:
            r = [None, []]
            self.regions[key] = r
        return r

    def claim(self, dst_keys, src_keys):
        toks = []
        for k in src_keys:
            s = self.regions.get(k)
            if s:
                if s[0] is not None:
                    toks.append(s[0])
                toks.extend(s[1])
        for k in dst_keys:
            self._region(k)[1].extend(toks)

    def claim_prefix(self, dst_keys, prefixes):
        src = [k for k in self.regions
               if isinstance(k, tuple) and k and k[0] in prefixes]
        self.claim(dst_keys, src)

    def _dchan(self, eng, chan):
        q = (eng, chan)
        if q not in self.dsems:
            self.dsems[q] = self.nc.alloc_semaphore(f"D_{eng}_{chan}")
            self.n_dma[q] = 0
        return q

    def _flush_dma(self, q):
        tgt = self.n_dma[q] * 16
        if self.dma_flush.get(q, 0) >= tgt:
            return self.dma_flush_tok[q]
        eng = q[0]
        self.n_flush[eng] += 1
        self.rec[eng].append(("dwaitinc", q, tgt, self.n_flush[eng]))
        tok = ("f", eng, self.n_flush[eng])
        self.dma_flush[q] = tgt
        self.dma_flush_tok[q] = tok
        return tok

    def _want(self, eng, tok):
        kind, peng, idx = tok
        if kind == "d":
            tok2 = self._flush_dma(peng)
            if eng != peng[0]:
                self._want(eng, tok2)
            return
        if kind == "f":
            if eng != peng:
                key = (eng, "f", peng)
                if self.last_wait.get(key, 0) < idx:
                    self.rec[eng].append(("fwait", peng, idx))
                    self.last_wait[key] = idx
            return
        if peng == eng:
            if eng != "tensor" and self.drain_mark[eng] < idx:
                self.rec[eng].append(("drain",))
                self.drain_mark[eng] = self.n_ops[eng]
        else:
            key = (eng, "o", peng)
            if self.last_wait.get(key, 0) < idx:
                self.rec[eng].append(("owait", peng, idx))
                self.last_wait[key] = idx
                self.need_inc[peng].add(idx)

    def scope(self, name):
        for e in ENGS:
            self.rec[e].append(("scope", name))

    def op(self, eng, fn, reads=(), writes=(), dma=False, chan=0):
        deps = []
        for k in reads:
            r = self._region(k)
            if r[0] is not None:
                deps.append(r[0])
        for k in writes:
            r = self._region(k)
            deps.extend(r[1])
            if r[0] is not None:
                deps.append(r[0])
        for t in deps:
            self._want(eng, t)
        if dma:
            q = self._dchan(eng, chan)
            self.n_dma[q] += 1
            self.rec[eng].append(("dma", fn, q))
            tok = ("d", q, self.n_dma[q] * 16)
        else:
            self.n_ops[eng] += 1
            self.rec[eng].append(("op", fn, self.n_ops[eng]))
            tok = ("o", eng, self.n_ops[eng])
        for k in reads:
            self._region(k)[1].append(tok)
        for k in writes:
            r = self._region(k)
            r[0] = tok
            r[1] = []
        return tok

    def finalize(self):
        inc_rank = {}
        for e in ENGS:
            idxs = sorted(self.need_inc[e])
            inc_rank[e] = {ix: r + 1 for r, ix in enumerate(idxs)}

        scope_state = {}

        def body(en):
            def run(eng):
                for item in self.rec[en]:
                    k = item[0]
                    if k == "scope":
                        if scope_state.get(en):
                            self.nc.pop_named_scope(scope_state[en])
                        self.nc.push_named_scope(item[1])
                        scope_state[en] = item[1]
                        continue
                    if k == "op":
                        ins = item[1](eng)
                        rank = inc_rank[en].get(item[2])
                        if rank is not None:
                            ins.then_inc(self.sems[en], 1)
                    elif k == "dma":
                        item[1](eng).then_inc(self.dsems[item[2]], 16)
                    elif k == "owait":
                        eng.wait_ge(self.sems[item[1]], inc_rank[item[1]][item[2]])
                    elif k == "dwaitinc":
                        ins = eng.wait_ge(self.dsems[item[1]], item[2])
                        ins.then_inc(self.fsems[en], 1)
                    elif k == "fwait":
                        eng.wait_ge(self.fsems[item[1]], item[2])
                    else:
                        eng.drain()
                if scope_state.get(en):
                    self.nc.pop_named_scope(scope_state[en])
                    scope_state[en] = None
            return run

        with self.nc.Block() as block:
            block.sync(body("sync"))
            block.vector(body("vector"))
            block.scalar(body("scalar"))
            block.tensor(body("tensor"))
            block.gpsimd(body("gpsimd"))


def make_sel_consts():
    selr = np.zeros((4, 128, 256), np.float32)
    selt = np.zeros((4, 128, 256), np.float32)
    for hp in range(4):
        for i in range(16):
            for j in range(16):
                selr[hp, 32 * hp + 1 + i, 16 * i + j] = 0.5
                selt[hp, 32 * hp + 1 + j, 16 * i + j] = 1.0
    return selr, selt


def pad_qk_feat_weight(wt):
    out = np.zeros((1024, 512), wt.dtype)
    for h in range(16):
        c0 = 128 * (h // 4) + 32 * (h % 4) + 1
        out[:, c0:c0 + 16] = wt[:, 16 * h:16 * h + 16]
    return out


def perm_halfsplit(nheads=16):
    p = np.empty(64 * nheads, np.int64)
    half = 32 * nheads
    for h in range(nheads):
        base = 128 * (h // 4) + 32 * (h % 4)
        for i in range(32):
            p[base + i] = 64 * h + i
            p[half + base + i] = 64 * h + 32 + i
    return p


def perm_interleave(nheads=16):
    p = np.empty(64 * nheads, np.int64)
    half = 32 * nheads
    for h in range(nheads):
        base = 128 * (h // 4) + 32 * (h % 4)
        for i in range(32):
            p[base + i] = 64 * h + 2 * i
            p[half + base + i] = 64 * h + 2 * i + 1
    return p


def rope_table(base, s=T):
    inv = 1.0 / (float(base) ** (np.arange(32, dtype=np.float64) / 32.0))
    ang = np.arange(s, dtype=np.float64)[None, :] * inv[:, None]
    cos = np.cos(ang).astype(np.float32)
    sin = np.sin(ang).astype(np.float32)
    return np.tile(cos, (4, 1)), np.tile(sin, (4, 1))


def build_program(stop="all"):
    nc = bass.Bass("TRN2", target_bir_lowering=False, dynamic_dma_scratch_size=2048)
    P = Prog(nc)

    def din(name, shape):
        return nc.dram_tensor(name, list(shape), F32, kind="ExternalInput")

    x0T = din("x0T", (D, T))
    L0A_WqTp = din("L0A_WqTp", (D, 512))
    L0A_WkTp = din("L0A_WkTp", (D, 512))
    L0A_WkT = din("L0A_WkT", (D, 256))
    L0A_WvT = din("L0A_WvT", (D, D))
    L0A_WoT = din("L0A_WoT", (D, D))
    L0A_g = din("L0A_g", (64, 1))
    F_sc1 = din("F_sc1", (128, 8))
    F_wqs = din("F_wqs", (8, 2, 128, 512))
    F_wk = din("F_wk", (D, D))
    F_wv = din("F_wv", (D, D))
    F_wo = din("F_wo", (D, D))
    F_sc2 = din("F_sc2", (128, 8))
    F_rw = din("F_rw", (D, 8))
    F_gum = din("F_gum", (T, 8))
    F_w1 = din("F_w1", (8, D, D))
    F_w2 = din("F_w2", (8, D, D))
    F_wout = din("F_wout", (8, D, D))
    F_b1 = din("F_b1", (8, D))
    F_b2 = din("F_b2", (8, D))
    F_bout = din("F_bout", (8, D))
    G_n1 = din("G_n1", (128, 8))
    G_wq = din("G_wq", (D, D))
    G_kvd = din("G_kvd", (D, 256))
    G_kup = din("G_kup", (256, 256))
    G_vup = din("G_vup", (256, 256))
    G_wo = din("G_wo", (D, D))
    G_n2 = din("G_n2", (128, 8))
    L_wq = din("L_wq", (D, D))
    L_wk = din("L_wk", (D, 256))
    L_wv = din("L_wv", (D, 256))
    L_wo = din("L_wo", (D, D))
    U1 = din("U1", (D, 256))
    V1 = din("V1", (256, 4096))
    B1F = din("B1F", (128, 32))
    U2 = din("U2", (4096, 256))
    V2 = din("V2", (256, D))
    B2F = din("B2F", (128, 8))
    outT = nc.dram_tensor("outT", [D, T], F32, kind="ExternalOutput")

    x1sp = nc.dram_tensor("x1sp", [D, T], BF16, kind="Internal")
    hsp = nc.dram_tensor("hsp", [D, T], BF16, kind="Internal")
    xmsp = nc.dram_tensor("xmsp", [D, T], BF16, kind="Internal")
    ysp = nc.dram_tensor("ysp", [D, T], BF16, kind="Internal")
    wqsb = nc.dram_tensor("wqsb", [8, 2, 128, 512], BF16, kind="Internal")
    wkb = nc.dram_tensor("wkb", [D, D], BF16, kind="Internal")
    wvb = nc.dram_tensor("wvb", [D, D], BF16, kind="Internal")
    wob = nc.dram_tensor("wob", [D, D], BF16, kind="Internal")
    w1b = nc.dram_tensor("w1b", [8, D, D], BF16, kind="Internal")
    w2b = nc.dram_tensor("w2b", [8, D, D], BF16, kind="Internal")
    woutb = nc.dram_tensor("woutb", [8, D, D], BF16, kind="Internal")

    def inl(name, arr, dtype=BF):
        return nc.inline_tensor(np.ascontiguousarray(arr.astype(dtype)), name)

    selr_np, selt_np = make_sel_consts()
    C_SELR = [inl(f"c_selr{i}", selr_np[i]) for i in range(4)]
    C_SELT = [inl(f"c_selt{i}", selt_np[i]) for i in range(4)]
    C_ONESCOL = inl("c_onescol", np.ones((128, 1), np.float32))
    C_ONESROW_F = inl("c_onesrow_f", np.ones((1, 128), np.float32), np.float32)
    C_ONESROW = inl("c_onesrow", np.ones((1, 128), np.float32))
    pat = np.zeros((1, 128), np.float32)
    pat[0, [0, 32, 64, 96]] = 1.0
    C_SLOTPAT = inl("c_slotpat", pat)
    tri = (np.arange(128)[:, None] <= np.arange(128)[None, :]).astype(np.float32)
    C_TRIU = inl("c_triu", tri)
    C_MGT = inl("c_mgt", 1.0 - tri)
    bd64t = np.zeros((2, 128), np.float32)
    bd64t[0, :64] = 1.0
    bd64t[1, 64:] = 1.0
    C_BD64T_F = inl("c_bd64t_f", bd64t, np.float32)
    C_BD64 = inl("c_bd64", bd64t.T)
    bd32t = np.zeros((4, 128), np.float32)
    for a in range(4):
        bd32t[a, 32 * a:32 * a + 32] = 1.0
    C_BD32T_F = inl("c_bd32t_f", bd32t, np.float32)
    C_BD32 = inl("c_bd32", bd32t.T)
    C_IDENT = inl("c_ident", np.eye(128, dtype=np.float32))
    C_ONESF128 = inl("c_onesf128", np.ones((128, 128), np.float32), np.float32)
    sel8 = np.zeros((8, 1024), np.float32)
    for ex_ in range(8):
        sel8[ex_, 128 * ex_:128 * (ex_ + 1)] = 1.0
    C_SEL8 = inl("c_sel8", sel8)
    cl, sl = rope_table(1e4)
    cg, sg_ = rope_table(1e6)
    C_COSL = inl("c_cosl", cl, np.float32)
    C_SINL = inl("c_sinl", sl, np.float32)
    C_COSG = inl("c_cosg", cg, np.float32)
    C_SING = inl("c_sing", sg_, np.float32)

    sbA = nc.alloc_sbuf_tensor
    ps = nc.alloc_psum_tensor

    XB = sbA("XB", [128, NCH * T], BF16)
    ARA = sbA("ARA", [128, 32768], BF16)
    ARB = sbA("ARB", [128, 33280], BF16)
    LNB = sbA("LNB", [128, 1024], F32)
    WS = sbA("WS", [128, 512], F32)
    STAT = sbA("STAT", [1, 1024], F32)
    XSB = sbA("XSB", [128, 512], BF16)
    XSQ = sbA("XSQ", [128, 512], BF16)
    PT0 = sbA("PT0", [128, 512], BF16)
    PT1 = sbA("PT1", [128, 512], BF16)
    RKBF = sbA("RKBF", [128, 1024], BF16)
    WQS = sbA("WQS", [128, 256], BF16)
    ROPC = sbA("ROPC", [128, 512], F32)
    ROPS = sbA("ROPS", [128, 512], F32)
    CONSTS = sbA("CONSTS", [128, 12], F32)
    G128 = sbA("G128", [128, 1], F32)
    SC1 = sbA("SC1", [128, 8], F32)
    SC2 = sbA("SC2", [128, 8], F32)
    N1W = sbA("N1W", [128, 8], F32)
    N2W = sbA("N2W", [128, 8], F32)
    BB1 = sbA("BB1", [128, 8], F32)
    BB2 = sbA("BB2", [128, 8], F32)
    B1FS = sbA("B1FS", [128, 32], F32)
    B2FS = sbA("B2FS", [128, 8], F32)
    ONESROW = sbA("ONESROW", [1, 128], BF16)
    ONESCOL = sbA("ONESCOL", [128, 1], BF16)
    ONESROW_F = sbA("ONESROW_F", [1, 128], F32)
    SLOTPAT = sbA("SLOTPAT", [1, 128], BF16)
    TRIU = sbA("TRIU", [128, 128], BF16)
    MGT = sbA("MGT", [128, 128], BF16)
    BD64T_F = sbA("BD64T_F", [2, 128], F32)
    BD64 = sbA("BD64", [128, 2], BF16)
    BD32T_F = sbA("BD32T_F", [4, 128], F32)
    BD32 = sbA("BD32", [128, 4], BF16)
    IDENT = sbA("IDENT", [128, 128], BF16)
    BCV = sbA("BCV", [128, 512], F32)
    ONESF128 = sbA("ONESF128", [128, 128], F32)
    GUM = sbA("GUM", [128, 8], F32)
    SEL8 = sbA("SEL8", [8, 1024], BF16)
    YSTG = XSB
    WTOK = sbA("WTOK", [128, 24], F32)
    WTOKB = sbA("WTOKB", [128, 8], BF16)

    P_QF = ps("P_QF", [128, 512], F32)
    P_KF = ps("P_KF", [128, 512], F32)
    P_KV = ps("P_KV", [128, 512], F32)
    P_V = ps("P_V", [128, 512], F32)
    P_SEL = ps("P_SEL", [128, 512], F32)
    P_S = ps("P_S", [128, 512], F32)
    P_Y = ps("P_Y", [128, 512], F32)
    P_W = ps("P_W", [128, 512], F32)

    def carve(arena, col0, cols, dtype=None):
        ap = arena[:, col0:col0 + cols]
        if dtype is not None:
            ap = ap.bitcast(dtype)
        return ap

    WTG = XB[0:8, 0:4096]
    W_QP = carve(ARA, 0, 4096)
    W_KP = carve(ARA, 4096, 4096)
    W_KT = carve(ARA, 8192, 2048)
    W_VT = carve(ARA, 10240, 8192)
    W_OT = carve(ARA, 18432, 8192)
    S2B = carve(ARA, 26624, 2048)
    YB = carve(ARA, 28672, 4096)
    S2M = carve(ARB, 0, 4096, F32)
    XSUM1 = carve(ARB, 4096, 8192, F32)
    SELR = carve(ARB, 17056, 1024)
    SELT = carve(ARB, 18080, 1024)
    SS2 = carve(ARB, 19104, 256, F32)
    PP2 = carve(ARB, 19360, 128)
    PM2 = carve(ARB, 19488, 128)
    QF22 = carve(ARB, 19616, 256)
    SB12 = carve(ARB, 19872, 256)
    KF22 = carve(ARB, 20128, 256)
    QF = carve(ARB, 12288, 512)
    KF = carve(ARB, 12800, 512)
    VTOK = carve(ARB, 13312, 1024)
    KTOKP = carve(ARB, 14336, 288)
    QF2 = carve(ARB, 14624, 256)
    SB1 = carve(ARB, 14880, 256)
    KF2 = carve(ARB, 15136, 256)
    PP = carve(ARB, 15392, 128)
    PM = carve(ARB, 15520, 128)
    SS = carve(ARB, 15648, 256, F32)
    S01M = carve(ARB, 15904, 512, F32)
    S01B = carve(ARB, 16416, 256)
    YSQ = carve(ARB, 16672, 128)
    BSB = carve(ARB, 16800, 256, F32)

    def dma(eng, dst, src, reads=(), writes=(), chan=0):
        return P.op(eng, lambda e, dst=dst, src=src: e.dma_start(out=dst, in_=src),
                    reads=reads, writes=writes, dma=True, chan=chan)

    def mm(out, lhsT, rhs, start, stop, reads, writes, tp=None):
        def fn(e, out=out, lhsT=lhsT, rhs=rhs, start=start, stop=stop, tp=tp):
            kw = {"tile_position": tp} if tp is not None else {}
            return e.matmul(out, lhsT, rhs, start=start, stop=stop, **kw)
        return P.op("tensor", fn, reads=reads, writes=writes)

    def vop(fn, reads, writes, eng="vector"):
        return P.op(eng, fn, reads=reads, writes=writes)

    for i in range(4):
        dma("sync", SELR[:, 256 * i:256 * (i + 1)], C_SELR[i].ap(), writes=[("SELR",)])
        dma("sync", SELT[:, 256 * i:256 * (i + 1)], C_SELT[i].ap(), writes=[("SELT",)])
    for dst, src, key in [
            (ONESROW, C_ONESROW, "ONESROW"), (ONESCOL, C_ONESCOL, "ONESCOL"),
            (ONESROW_F, C_ONESROW_F, "ONESROW_F"), (SLOTPAT, C_SLOTPAT, "SLOTPAT"),
            (TRIU, C_TRIU, "TRIU"), (MGT, C_MGT, "MGT"),
            (BD64T_F, C_BD64T_F, "BD64T_F"), (BD64, C_BD64, "BD64"),
            (BD32T_F, C_BD32T_F, "BD32T_F"), (BD32, C_BD32, "BD32"),
            (IDENT, C_IDENT, "IDENT"), (ONESF128, C_ONESF128, "ONESF128"),
            (SEL8, C_SEL8, "SEL8")]:
        dma("sync", dst[:, :], src.ap(), writes=[(key,)])
    dma("sync", G128[0:64, :], L0A_g.ap(), writes=[("G128",)])
    dma("sync", G128[64:128, :], L0A_g.ap(), writes=[("G128",)])
    dma("sync", SC1[:, :], F_sc1.ap(), writes=[("SC1",)])
    dma("sync", SC2[:, :], F_sc2.ap(), writes=[("SC2",)])
    dma("sync", N1W[:, :], G_n1.ap(), writes=[("N1W",)])
    dma("sync", N2W[:, :], G_n2.ap(), writes=[("N2W",)])
    dma("sync", B1FS[:, :], B1F.ap(), writes=[("B1FS",)])
    dma("sync", B2FS[:, :], B2F.ap(), writes=[("B2FS",)])
    vop(lambda e: e.memset(CONSTS[:, 0:1], 1.0), (), [("CONSTS",)])
    vop(lambda e: e.memset(CONSTS[:, 1:2], 1e-5), (), [("CONSTS",)])
    vop(lambda e: e.memset(CONSTS[:, 2:3], 1e-6), (), [("CONSTS",)])
    vop(lambda e: e.memset(CONSTS[:, 3:4], 0.125), (), [("CONSTS",)])
    vop(lambda e: e.memset(CONSTS[:, 4:5], -44.0), (), [("CONSTS",)])
    vop(lambda e: e.memset(CONSTS[:, 5:6], -8.0), (), [("CONSTS",)])
    vop(lambda e: e.memset(CONSTS[:, 6:7], 1.0 / D), (), [("CONSTS",)])
    vop(lambda e: e.memset(CONSTS[:, 7:8], 1.0 / 64), (), [("CONSTS",)])
    vop(lambda e: e.memset(CONSTS[:, 8:9], 64.0 * 1e-5), (), [("CONSTS",)])

    for g in range(NCH):
        for qq in range(8):
            wk = ("WS",)
            dma("sync", WS[:, :], x0T.ap()[128 * g:128 * g + 128,
                                           512 * qq:512 * qq + 512],
                reads=[wk], writes=[wk])
            vop(lambda e, g=g, qq=qq: e.tensor_copy(
                out=XB[:, T * g + 512 * qq:T * g + 512 * qq + 512], in_=WS[:, :]),
                [wk], [("XB", g, qq)])

    def load_w(dst_fn, src_ap, cols, name, scale_fn=None, scale_imm=None,
               cast_scale=None, nch=NCH):
        nhalf = (cols + 511) // 512
        for g in range(nch):
            for hh in range(nhalf):
                c0 = 512 * hh
                cw = min(512, cols - c0)
                dma("sync", WS[:, 0:cw], src_ap[128 * g:128 * g + 128, c0:c0 + cw],
                    reads=[("WS",)], writes=[("WS",)], chan="w")
                dst = dst_fn(g)[:, c0:c0 + cw]
                if scale_fn is not None:
                    vop(lambda e, dst=dst, cw=cw, g=g: e.tensor_scalar(
                        out=dst, in0=WS[:, 0:cw], scalar1=scale_fn(g),
                        scalar2=scale_imm, op0=ALU.mult,
                        op1=ALU.mult if scale_imm is not None else ALU.bypass),
                        [("WS",), ("G128",), ("SC1",), ("SC2",), ("N1W",), ("N2W",)],
                        [(name, g)])
                elif cast_scale is not None:
                    vop(lambda e, dst=dst, cw=cw: e.tensor_scalar(
                        out=dst, in0=WS[:, 0:cw], scalar1=cast_scale, scalar2=None,
                        op0=ALU.mult), [("WS",)], [(name, g)])
                else:
                    vop(lambda e, dst=dst, cw=cw: e.tensor_copy(
                        out=dst, in_=WS[:, 0:cw]), [("WS",)], [(name, g)])

    PCPEND = []
    PCI = [0]

    def pc_add(pcname, src_ap, dst_dram_ap, rows, cols, ex=None):
        for g in range(rows // 128):
            for hh in range((cols + 511) // 512):
                c0 = 512 * hh
                cw = min(512, cols - c0)
                key = (("PCW", pcname, g, hh) if ex is None
                       else ("PCW", pcname, ex, g, hh))
                for q0 in range(0, cw, 256):
                    qw = min(256, cw - q0)
                    PCPEND.append(
                        (src_ap[128 * g:128 * g + 128, c0 + q0:c0 + q0 + qw],
                         dst_dram_ap[128 * g:128 * g + 128, c0 + q0:c0 + q0 + qw],
                         key, qw))

    def pump(n, w):
        for _ in range(n):
            if not PCPEND:
                return
            src, dst, key, qw = PCPEND.pop(0)
            i = PCI[0]
            PCI[0] += 1
            if w == "p1":
                fs, fk = ((PT0[:, :].bitcast(F32), ("PT", 0)) if i % 2 == 0
                          else (PT1[:, :].bitcast(F32), ("PT", 1)))
            else:
                fs, fk = ((WS[:, 0:256], ("WS",)) if i % 2 == 0
                          else (WS[:, 256:512], ("WSB",)))
            os_, ok = ((XSQ[:, 0:256], ("XSQ",)) if i % 2 == 0
                       else (XSQ[:, 256:512], ("XSQB",)))
            dma("sync", fs[:, 0:qw], src, reads=[fk], writes=[fk], chan="pc")
            P.op("gpsimd", lambda e, fs=fs, os_=os_, qw=qw: e.tensor_copy(
                out=os_[:, 0:qw], in_=fs[:, 0:qw]), reads=[fk, ok], writes=[ok])
            dma("sync", dst, os_[:, 0:qw], reads=[ok], writes=[key],
                chan="pc2")

    def load_w_b16(dst_fn, src_ap, cols, name, pcname, ex=None, nch=NCH):
        nhalf = (cols + 511) // 512
        for g in range(nch):
            for hh in range(nhalf):
                c0 = 512 * hh
                cw = min(512, cols - c0)
                key = (("PCW", pcname, g, hh) if ex is None
                       else ("PCW", pcname, ex, g, hh))
                dma("sync", dst_fn(g)[:, c0:c0 + cw],
                    src_ap[128 * g:128 * g + 128, c0:c0 + cw],
                    reads=[key, (name, g)], writes=[(name, g)],
                    chan="w" if (g + hh) % 2 == 0 else "w2")

    load_w(lambda g: W_QP[:, 512 * g:512 * (g + 1)], L0A_WqTp.ap(), 512, "W_QP",
           cast_scale=0.5)
    load_w(lambda g: W_KP[:, 512 * g:512 * (g + 1)], L0A_WkTp.ap(), 512, "W_KP",
           cast_scale=0.5)
    load_w(lambda g: W_KT[:, 256 * g:256 * (g + 1)], L0A_WkT.ap(), 256, "W_KT",
           cast_scale=0.5)
    load_w(lambda g: W_VT[:, 1024 * g:1024 * (g + 1)], L0A_WvT.ap(), 1024, "W_VT")
    load_w(lambda g: W_OT[:, 1024 * g:1024 * (g + 1)], L0A_WoT.ap(), 1024, "W_OT",
           scale_fn=lambda g: G128[:, 0:1], scale_imm=8.0)

    vop(lambda e: e.memset(S2M[:, :], 0.0), (),
        [("S2M", hh) for hh in range(16)])
    vop(lambda e: e.memset(S01M[:, :], 0.0), (),
        [("S01M", hh) for hh in range(16)])
    P.op("gpsimd", lambda e: e.memset(KTOKP[:, :], 1.0), reads=(),
         writes=[("KTOKP", "ones"), ("KTOKP",)])

    def stats_sumsq(src_f32_ap_fn, src_keys, n=NCH, via_f32=True, also_sum=False):
        for g in range(n):
            ap, keys = src_f32_ap_fn(g)
            if also_sum:
                if via_f32:
                    vop(lambda e, ap=ap: e.tensor_copy(out=XSB[:, :], in_=ap),
                        keys, [("XSB",)])
                    sap = XSB[:, :]
                    skey = [("XSB",)]
                else:
                    sap, skey = ap, keys
                mm(P_QF[0:1, 0:512], ONESCOL[:, :], sap,
                   start=(g == 0), stop=(g == n - 1),
                   reads=skey + [("ONESCOL",)], writes=[("P_QF",)])
            P.op("scalar", lambda e, ap=ap: e.activation(
                out=XSQ[:, :], in_=ap, func=AF.Square),
                reads=keys, writes=[("XSQ",)])
            mm(P_KF[0:1, 0:512], ONESCOL[:, :], XSQ[:, :],
               start=(g == 0), stop=(g == n - 1),
               reads=[("XSQ",), ("ONESCOL",)], writes=[("P_KF",)])

    def bcast_f32(stat_slice, stat_key, pbank, pkey, lnb_off):
        mm(pbank[:, :], ONESROW_F[:, :], stat_slice, start=True, stop=True,
           reads=[stat_key, ("ONESROW_F",)], writes=[pkey])
        P.op("scalar", lambda e: e.copy(out=LNB[:, lnb_off:lnb_off + 512],
                                        in_=pbank[:, :]),
             reads=[pkey], writes=[("LNB", lnb_off)])

    def ln_finish(eps_col, out_writer, src_fn, n=NCH):
        vop(lambda e: e.tensor_scalar(out=STAT[:, 0:512], in0=P_QF[0:1, 0:512],
                                      scalar1=1.0 / D, scalar2=None, op0=ALU.mult),
            [("P_QF",)], [("STAT", "m")])
        vop(lambda e: e.tensor_scalar(out=STAT[:, 512:1024], in0=P_KF[0:1, 0:512],
                                      scalar1=1.0 / D, scalar2=None, op0=ALU.mult),
            [("P_KF",)], [("STAT", "A")])
        vop(lambda e: e.tensor_mul(out=BCV[0:1, 0:512], in0=STAT[:, 0:512],
                                   in1=STAT[:, 0:512]),
            [("STAT", "m")], [("BCV", "a")])
        vop(lambda e: e.tensor_sub(out=STAT[:, 512:1024], in0=STAT[:, 512:1024],
                                   in1=BCV[0:1, 0:512]),
            [("STAT", "A"), ("BCV", "a")], [("STAT", "A")])
        P.op("scalar", lambda e: e.activation(
            out=BCV[0:1, 0:512], in_=STAT[:, 512:1024], func=AF.Sqrt,
            bias=eps_col, scale=1.0),
            reads=[("STAT", "A"), ("CONSTS",)], writes=[("BCV", "a")])
        vop(lambda e: e.reciprocal(out=STAT[:, 512:1024], in_=BCV[0:1, 0:512]),
            [("BCV", "a")], [("STAT", "A")])
        bcast_f32(STAT[0:1, 0:512], ("STAT", "m"), P_KV, ("P_KV",), 0)
        bcast_f32(STAT[0:1, 512:1024], ("STAT", "A"), P_V, ("P_V",), 512)
        for g in range(n):
            ap, keys = src_fn(g)
            vop(lambda e, ap=ap: e.tensor_sub(out=WS[:, :], in0=ap,
                                              in1=LNB[:, 0:512]),
                keys + [("LNB", 0)], [("WS",)])
            out_writer(g)

    def rms_r(eps_col, n=NCH, scale=1.0 / D):
        vop(lambda e: e.tensor_scalar(out=STAT[:, 512:1024], in0=P_KF[0:1, 0:512],
                                      scalar1=scale, scalar2=None, op0=ALU.mult),
            [("P_KF",)], [("STAT", "A")])
        P.op("scalar", lambda e: e.activation(
            out=BCV[0:1, 0:512], in_=STAT[:, 512:1024], func=AF.Sqrt,
            bias=eps_col, scale=1.0),
            reads=[("STAT", "A"), ("CONSTS",)], writes=[("BCV", "a")])
        vop(lambda e: e.reciprocal(out=STAT[:, 512:1024], in_=BCV[0:1, 0:512]),
            [("BCV", "a")], [("STAT", "A")])
        bcast_f32(STAT[0:1, 512:1024], ("STAT", "A"), P_V, ("P_V",), 512)

    for _wb in range(8):
        for _h2 in range(2):
            for _q0 in (0, 256):
                PCPEND.append((F_wqs.ap()[_wb, _h2][:, _q0:_q0 + 256],
                               wqsb.ap()[_wb, _h2][:, _q0:_q0 + 256],
                               ("PCW", "wqs", _wb, _h2), 256))
    pc_add("wk", F_wk.ap(), wkb.ap(), D, D)
    pc_add("wv", F_wv.ap(), wvb.ap(), D, D)
    pc_add("wo", F_wo.ap(), wob.ap(), D, D)

    P.scope("P1_l0attn")
    EPS_RMS0 = 1e-5
    for j in range(NT):
        for g in range(NCH):
            dma("scalar", XSUM1[:, 512 * g:512 * (g + 1)],
                x0T.ap()[128 * g:128 * g + 128, 512 * j:512 * (j + 1)],
                reads=[("XSUM1", g)], writes=[("XSUM1", g)], chan="x0")
        for cc in range(4):
            c = 4 * j + cc
            t0 = 128 * c
            xkeys = [("XB", g, c // 4) for g in range(NCH)]
            P.op("gpsimd", lambda e: e.tensor_copy(out=S2B[:, :], in_=S2M[:, :]),
                 reads=[("S2M", hh) for hh in range(16)], writes=[("S2B",)])
            P.op("gpsimd", lambda e: e.tensor_copy(out=S01B[:, :], in_=S01M[:, :]),
                 reads=[("S01M", hh) for hh in range(16)], writes=[("S01B",)])
            for m in range(4):
                for g in range(NCH):
                    mm(P_QF[:, 128 * m:128 * (m + 1)],
                       W_QP[:, 512 * g + 128 * m:512 * g + 128 * m + 128],
                       XB[:, T * g + t0:T * g + t0 + 128],
                       start=(g == 0), stop=False,
                       reads=[("W_QP", g), xkeys[g]], writes=[("P_QF",)])
                mm(P_QF[:, 128 * m:128 * (m + 1)], SLOTPAT[:, :], ONESROW[:, :],
                   start=False, stop=True,
                   reads=[("SLOTPAT",), ("ONESROW",)], writes=[("P_QF",)])
                for g in range(NCH):
                    mm(P_KF[:, 128 * m:128 * (m + 1)],
                       W_KP[:, 512 * g + 128 * m:512 * g + 128 * m + 128],
                       XB[:, T * g + t0:T * g + t0 + 128],
                       start=(g == 0), stop=(g == NCH - 1),
                       reads=[("W_KP", g), xkeys[g]], writes=[("P_KF",)])
            for g in range(NCH):
                mm(P_KV[:, 0:256], XB[:, T * g + t0:T * g + t0 + 128],
                   W_KT[:, 256 * g:256 * (g + 1)],
                   start=(g == 0), stop=(g == NCH - 1),
                   reads=[("W_KT", g), xkeys[g]], writes=[("P_KV",)])
            for hf in range(2):
                for g in range(NCH):
                    mm(P_V[:, :], XB[:, T * g + t0:T * g + t0 + 128],
                       W_VT[:, 1024 * g + 512 * hf:1024 * g + 512 * hf + 512],
                       start=(g == 0), stop=(g == NCH - 1),
                       reads=[("W_VT", g), xkeys[g]], writes=[("P_V",)])
                vop(lambda e, hf=hf: e.tensor_copy(
                    out=VTOK[:, 512 * hf:512 * (hf + 1)], in_=P_V[:, :]),
                    [("P_V",)], [("VTOK",)])
            vop(lambda e: e.tensor_copy(out=QF[:, :], in_=P_QF[:, :]),
                [("P_QF",)], [("QF",)])
            vop(lambda e: e.tensor_copy(out=KF[:, :], in_=P_KF[:, :]),
                [("P_KF",)], [("KF",)])
            vop(lambda e: e.tensor_copy(
                out=KTOKP[:, :].rearrange("p (h s) -> p h s", s=18)[:, :, 1:17],
                in_=P_KV[:, 0:256].rearrange("p (h s) -> p h s", s=16)),
                [("P_KV",)], [("KTOKP",)])

            def hctx(h):
                par = h % 2
                return dict(
                    h=h, tl=h // 4, b=32 * (h % 4), yb=64 * par, par=par,
                    SSp=(SS, ("SS",)) if par == 0 else (SS2, ("SS2",)),
                    PPp=(PP, ("PP",)) if par == 0 else (PP2, ("PP2",)),
                    PMp=(PM, ("PM",)) if par == 0 else (PM2, ("PM2",)),
                    QF2p=(QF2, ("QF2",)) if par == 0 else (QF22, ("QF22",)),
                    SB1p=(SB1, ("SB1",)) if par == 0 else (SB12, ("SB12",)),
                    KF2p=(KF2, ("KF2",)) if par == 0 else (KF22, ("KF22",)),
                    sB=(P_S, ("P_S",)) if par == 0 else (P_KV, ("P_KV",)),
                    uB=(P_W, ("P_W",)) if par == 0 else (P_V, ("P_V",)),
                    selB=(P_SEL, ("P_SEL",)) if par == 0 else (P_QF, ("P_QF",)),
                    yB=(P_Y, ("P_Y",)) if (h // 2) % 2 == 0 else (P_KF, ("P_KF",)))

            def head_front(h):
                cx = hctx(h)
                tl, b = cx["tl"], cx["b"]
                sB, sbk = cx["sB"]
                SSp, ssk = cx["SSp"]
                PPp, ppk = cx["PPp"]
                PMp, pmk = cx["PMp"]
                selB, selk = cx["selB"]
                SB1p, sb1k = cx["SB1p"]
                QF2p, qf2k = cx["QF2p"]
                KF2p, kf2k = cx["KF2p"]
                qtile = QF[:, 128 * tl:128 * (tl + 1)]
                mm(sB[0:128, 0:128], KF[b:b + 17, 128 * tl:128 * tl + 128],
                   QF[b:b + 17, 128 * tl:128 * tl + 128], start=True, stop=True,
                   reads=[("KF",), ("QF",)], writes=[sbk], tp=(b, 0))
                P.op("scalar", lambda e, sB=sB, SSp=SSp: e.activation(
                    out=SSp[:, :], in_=sB[0:128, 0:128], func=AF.Square,
                    bias=CONSTS[:, 0:1], scale=1.0),
                    reads=[sbk, ("CONSTS",)], writes=[ssk])
                P.op("gpsimd", lambda e, SSp=SSp, PPp=PPp: e.tensor_scalar(
                    out=PPp[:, :], in0=SSp[:, :], scalar1=0.5, scalar2=0.5,
                    op0=ALU.mult, op1=ALU.add), reads=[ssk], writes=[ppk])
                P.op("gpsimd", lambda e, PPp=PPp, PMp=PMp: e.tensor_mul(
                    out=PMp[:, :], in0=PPp[:, :], in1=TRIU[:, :]),
                    reads=[ppk, ("TRIU",)], writes=[pmk])
                for ch in range(2):
                    mm(selB[:, 128 * ch:128 * (ch + 1)],
                       SELR[:, 256 * (h % 4) + 128 * ch:256 * (h % 4) + 128 * (ch + 1)],
                       qtile, start=True, stop=True,
                       reads=[("SELR",), ("QF",)], writes=[selk])
                    mm(selB[:, 256 + 128 * ch:256 + 128 * (ch + 1)],
                       SELT[:, 256 * (h % 4) + 128 * ch:256 * (h % 4) + 128 * (ch + 1)],
                       qtile, start=True, stop=True,
                       reads=[("SELT",), ("QF",)], writes=[selk])
                P.op("scalar", lambda e, selB=selB, SB1p=SB1p: e.copy(
                    out=SB1p[:, :], in_=selB[:, 0:256]),
                    reads=[selk], writes=[sb1k])
                vop(lambda e, selB=selB, SB1p=SB1p, QF2p=QF2p: e.tensor_mul(
                    out=QF2p[:, :], in0=selB[:, 256:512], in1=SB1p[:, :]),
                    [selk, sb1k], [qf2k])
                kslot = KTOKP[:, :].rearrange("p (hh s) -> p hh s", s=18)[:, h, 1:17]
                P.op("gpsimd", lambda e, kslot=kslot, KF2p=KF2p: e.tensor_mul(
                    out=KF2p[:, :].rearrange("p (i jj) -> p i jj", i=16),
                    in0=kslot.unsqueeze(2).broadcast_to((128, 16, 16)),
                    in1=kslot.unsqueeze(1).broadcast_to((128, 16, 16))),
                    reads=[("KTOKP",), ("KTOKP", "ones")], writes=[kf2k])

            def head_back(h, cc):
                cx = hctx(h)
                tl, b, yb = cx["tl"], cx["b"], cx["yb"]
                PMp, pmk = cx["PMp"]
                QF2p, qf2k = cx["QF2p"]
                KF2p, kf2k = cx["KF2p"]
                uB, ubk = cx["uB"]
                yB, ybk = cx["yB"]
                mm(uB[0:17, 0:64], KTOKP[:, 18 * h:18 * h + 17],
                   VTOK[:, 64 * h:64 * h + 64], start=True, stop=True,
                   reads=[("KTOKP",), ("KTOKP", "ones"), ("VTOK",)],
                   writes=[ubk])
                mm(yB[yb:yb + 64, 0:128], VTOK[:, 64 * h:64 * h + 64], PMp[:, :],
                   start=True, stop=False,
                   reads=[("VTOK",), pmk], writes=[ybk], tp=(0, yb))
                mm(yB[yb:yb + 64, 0:128], S01B[b:b + 17, 64 * tl:64 * tl + 64],
                   QF[b:b + 17, 128 * tl:128 * tl + 128], start=False, stop=False,
                   reads=[("S01B",), ("QF",)], writes=[ybk], tp=(b, yb))
                for ch in range(2):
                    mm(uB[:, 64 + 64 * ch:64 + 64 * (ch + 1)],
                       KF2p[:, 128 * ch:128 * (ch + 1)],
                       VTOK[:, 64 * h:64 * h + 64], start=True, stop=True,
                       reads=[kf2k, ("VTOK",)], writes=[ubk])
                for ch in range(2):
                    mm(yB[yb:yb + 64, 0:128],
                       S2B[:, 64 * (2 * h + ch):64 * (2 * h + ch) + 64],
                       QF2p[:, 128 * ch:128 * (ch + 1)],
                       start=False, stop=(ch == 1),
                       reads=[("S2B",), qf2k], writes=[ybk], tp=(0, yb))
                vop(lambda e, b=b, tl=tl, uB=uB: e.scalar_tensor_tensor(
                    out=S01M[b:b + 17, 64 * tl:64 * tl + 64], in0=uB[0:17, 0:64],
                    scalar=1.0, in1=S01M[b:b + 17, 64 * tl:64 * tl + 64],
                    op0=ALU.mult, op1=ALU.add),
                    [ubk], [("S01M", h)])
                for ch in range(2):
                    cix = 64 * (2 * h + ch)
                    vop(lambda e, cix=cix, ch=ch, uB=uB: e.scalar_tensor_tensor(
                        out=S2M[:, cix:cix + 64],
                        in0=uB[:, 64 + 64 * ch:64 + 64 * (ch + 1)],
                        scalar=1.0, in1=S2M[:, cix:cix + 64],
                        op0=ALU.mult, op1=ALU.add),
                        [ubk], [("S2M", h)])

                if h % 2 == 1:
                    gp = h // 2
                    P.op("scalar", lambda e, yB=yB: e.activation(
                        out=YSQ[:, :], in_=yB[:, 0:128], func=AF.Square),
                        reads=[ybk], writes=[("YSQ",)])
                    mm(yB[0:2, 256:384], BD64[:, :], YSQ[:, :], start=True,
                       stop=True, reads=[("YSQ",), ("BD64",)], writes=[ybk])
                    vop(lambda e, yB=yB: e.tensor_scalar(
                        out=BCV[0:2, 0:128], in0=yB[0:2, 256:384],
                        scalar1=64.0 * EPS_RMS0, scalar2=None, op0=ALU.add),
                        [ybk], [("BCV", "a")])
                    P.op("scalar", lambda e: e.activation(
                        out=WS[0:2, 0:128], in_=BCV[0:2, 0:128], func=AF.Sqrt),
                        reads=[("BCV", "a")], writes=[("WS",)])
                    vop(lambda e: e.reciprocal(out=BCV[0:2, 128:256],
                                               in_=WS[0:2, 0:128]),
                        [("WS",)], [("BCV", "a")])
                    mm(yB[:, 384:512], BD64T_F[:, :], BCV[0:2, 128:256],
                       start=True, stop=True,
                       reads=[("BCV", "a"), ("BD64T_F",)], writes=[ybk])
                    P.op("scalar", lambda e, yB=yB: e.copy(out=BSB[:, :],
                                                           in_=yB[:, 384:512]),
                         reads=[ybk], writes=[("BSB",)])
                    vop(lambda e, gp=gp, cc=cc, yB=yB: e.tensor_mul(
                        out=YB[:, 512 * gp + 128 * cc:512 * gp + 128 * cc + 128],
                        in0=yB[:, 0:128], in1=BSB[:, :]),
                        [ybk, ("BSB",)], [("YB", gp, cc)])

            head_front(0)
            head_front(1)
            for h in range(16):
                head_back(h, cc)
                if h + 2 < 16:
                    head_front(h + 2)
                if h % 3 == 2:
                    pump(1, "p1")
            pump(1, "p1")

        for o in range(NCH):
            for g in range(NCH):
                mm(P_W[:, :], W_OT[:, 1024 * g + 128 * o:1024 * g + 128 * o + 128],
                   YB[:, 512 * g:512 * (g + 1)],
                   start=(g == 0), stop=(g == NCH - 1),
                   reads=[("W_OT", g)] + [("YB", g, c4) for c4 in range(4)],
                   writes=[("P_W",)])
            vop(lambda e, o=o: e.scalar_tensor_tensor(
                out=XSUM1[:, 512 * o:512 * (o + 1)], in0=P_W[:, :], scalar=1.0,
                in1=XSUM1[:, 512 * o:512 * (o + 1)], op0=ALU.mult, op1=ALU.add),
                [("P_W",)], [("XSUM1", o)])
        stats_sumsq(lambda g: (XSUM1[:, 512 * g:512 * (g + 1)], [("XSUM1", g)]),
                    None, also_sum=True)
        ln_finish(CONSTS[0:1, 1:2],
                  out_writer=lambda g, j=j: vop(
                      lambda e, g=g: e.tensor_mul(
                          out=XB[:, T * g + 512 * j:T * g + 512 * (j + 1)],
                          in0=WS[:, :], in1=LNB[:, 512:1024]),
                      [("WS",), ("LNB", 512)], [("XB", g, j)]),
                  src_fn=lambda g: (XSUM1[:, 512 * g:512 * (g + 1)],
                                    [("XSUM1", g)]))

    def debug_stream_out(keys_fn):
        for g in range(NCH):
            for q in range(8):
                vop(lambda e, g=g, q=q: e.tensor_copy(
                    out=WS[:, :], in_=XB[:, T * g + 512 * q:T * g + 512 * (q + 1)]),
                    keys_fn(g, q) + [("WS",)], [("WS",)])
                dma("sync", outT.ap()[128 * g:128 * g + 128, 512 * q:512 * (q + 1)],
                    WS[:, :], reads=[("WS",)], writes=[("OUT", g, q)], chan="out")
        P.op("sync", lambda e: e.nop(nofuse=True),
             reads=[("OUT", g, q) for g in range(NCH) for q in range(8)], writes=())

    def debug_hbm_out(src_sp, prefix):
        for g in range(NCH):
            for q in range(8):
                dma("sync", XSB[:, :],
                    src_sp.ap()[128 * g:128 * g + 128, 512 * q:512 * (q + 1)],
                    reads=[(prefix, g, q), ("XSB",)], writes=[("XSB",)], chan="dbg")
                vop(lambda e: e.tensor_copy(out=WS[:, :], in_=XSB[:, :]),
                    [("XSB",), ("WS",)], [("WS",)])
                dma("sync", outT.ap()[128 * g:128 * g + 128, 512 * q:512 * (q + 1)],
                    WS[:, :], reads=[("WS",)], writes=[("OUT", g, q)], chan="out")
        P.op("sync", lambda e: e.nop(nofuse=True),
             reads=[("OUT", g, q) for g in range(NCH) for q in range(8)], writes=())

    if stop == "x1":
        debug_stream_out(lambda g, q: [("XB", g, q)])
        P.finalize()
        return nc

    assert not PCPEND, f"group A pre-casts not drained: {len(PCPEND)}"
    P.scope("P2_rms1")
    P.claim_prefix([("XSQ",)], {"XSQB"})
    RMS1X = ARA[:, :]
    P.claim([("RMS1X", g, j) for g in range(NCH) for j in range(NT)],
            [("W_QP", g) for g in range(NCH)] + [("W_KP", g) for g in range(NCH)]
            + [("W_KT", g) for g in range(NCH)] + [("W_VT", g) for g in range(NCH)]
            + [("W_OT", g) for g in range(NCH)] + [("S2B",)]
            + [("YB", g, c) for g in range(NCH) for c in range(4)])
    for j in range(NT):
        for g in range(NCH):
            dma("scalar", x1sp.ap()[128 * g:128 * g + 128, 512 * j:512 * (j + 1)],
                XB[:, T * g + 512 * j:T * g + 512 * (j + 1)],
                reads=[("XB", g, j)], writes=[("x1sp", g, j)], chan="sp")
        stats_sumsq(lambda g, j=j: (XB[:, T * g + 512 * j:T * g + 512 * (j + 1)],
                                    [("XB", g, j)]), None, via_f32=False)
        rms_r(CONSTS[0:1, 2:3])
        for g in range(NCH):
            vop(lambda e, g=g, j=j: e.scalar_tensor_tensor(
                out=RMS1X[:, T * g + 512 * j:T * g + 512 * (j + 1)],
                in0=XB[:, T * g + 512 * j:T * g + 512 * (j + 1)],
                scalar=SC1[:, g:g + 1], in1=LNB[:, 512:1024],
                op0=ALU.mult, op1=ALU.mult),
                [("XB", g, j), ("SC1",), ("LNB", 512)], [("RMS1X", g, j)])


    BANKS = [P_QF, P_KF, P_KV, P_V, P_SEL, P_S, P_Y, P_W]
    BKEYS = [("P_QF",), ("P_KF",), ("P_KV",), ("P_V",), ("P_SEL",), ("P_S",),
             ("P_Y",), ("P_W",)]

    def _rope_pair(evbank, evkey, odbank, odkey):
        vop(lambda e: e.scalar_tensor_tensor(
            out=XSB[:, :], in0=evbank[:, :], scalar=1.0, in1=ROPC[:, :],
            op0=ALU.mult, op1=ALU.mult), [evkey, ("ROPC",)], [("XSB",)])
        vop(lambda e: e.scalar_tensor_tensor(
            out=XSQ[:, :], in0=odbank[:, :], scalar=1.0, in1=ROPS[:, :],
            op0=ALU.mult, op1=ALU.mult), [odkey, ("ROPS",)], [("XSQ",)])
        vop(lambda e: e.tensor_sub(out=RKBF[:, 0:512], in0=XSB[:, :],
                                   in1=XSQ[:, :]),
            [("XSB",), ("XSQ",)], [("RKBF", 0)])
        vop(lambda e: e.scalar_tensor_tensor(
            out=XSB[:, :], in0=evbank[:, :], scalar=1.0, in1=ROPS[:, :],
            op0=ALU.mult, op1=ALU.mult), [evkey, ("ROPS",)], [("XSB",)])
        vop(lambda e: e.scalar_tensor_tensor(
            out=XSQ[:, :], in0=odbank[:, :], scalar=1.0, in1=ROPC[:, :],
            op0=ALU.mult, op1=ALU.mult), [odkey, ("ROPC",)], [("XSQ",)])
        vop(lambda e: e.tensor_add(out=RKBF[:, 512:1024], in0=XSB[:, :],
                                   in1=XSQ[:, :]),
            [("XSB",), ("XSQ",)], [("RKBF", 1)])

    def _qknorm_rkbf(stat_bank, stat_key, bc_bank, bc_key):
        for part in range(2):
            P.op("scalar", lambda e, part=part: e.activation(
                out=XSQ[:, :], in_=RKBF[:, 512 * part:512 * part + 512],
                func=AF.Square),
                reads=[("RKBF", part)], writes=[("XSQ",)])
            mm(stat_bank[0:4, 0:512], BD32[:, :], XSQ[:, :],
               start=(part == 0), stop=(part == 1),
               reads=[("XSQ",), ("BD32",)], writes=[stat_key])
        vop(lambda e: e.tensor_scalar(
            out=BCV[0:4, 0:512], in0=stat_bank[0:4, 0:512], scalar1=1.0 / 64,
            scalar2=None, op0=ALU.mult), [stat_key], [("BCV", "a")])
        P.op("scalar", lambda e: e.activation(
            out=WS[0:4, 0:512], in_=BCV[0:4, 0:512], func=AF.Sqrt,
            bias=CONSTS[0:4, 2:3], scale=1.0),
            reads=[("BCV", "a"), ("CONSTS",)], writes=[("WS",)])
        vop(lambda e: e.reciprocal(out=BCV[0:4, 0:512], in_=WS[0:4, 0:512]),
            [("WS",)], [("BCV", "a")])
        mm(bc_bank[:, :], BD32T_F[:, :], BCV[0:4, 0:512], start=True, stop=True,
           reads=[("BCV", "a"), ("BD32T_F",)], writes=[bc_key])
        P.op("scalar", lambda e, bc_bank=bc_bank: e.copy(out=BCV[:, :],
                                                         in_=bc_bank[:, :]),
             reads=[bc_key], writes=[("BCV", "a")])
        for part in range(2):
            vop(lambda e, part=part: e.tensor_mul(
                out=RKBF[:, 512 * part:512 * part + 512],
                in0=RKBF[:, 512 * part:512 * part + 512], in1=BCV[:, :]),
                [("RKBF", part), ("BCV", "a")], [("RKBF", part)])

    def rope_proj_pass(src_arena, src_key_fn, nch, wslot_fn, wname, npair,
                       ctab, stab, qknorm, cache_write_fn):
        def tab(j):
            dma("sync", ROPC[:, :], ctab.ap()[:, 512 * j:512 * (j + 1)],
                reads=[("ROPC",)], writes=[("ROPC",)], chan="w")
            dma("sync", ROPS[:, :], stab.ap()[:, 512 * j:512 * (j + 1)],
                reads=[("ROPS",)], writes=[("ROPS",)], chan="w")

        def proj(j, pr):
            for part, pbank, pkey in ((0, P_QF, ("P_QF",)), (1, P_KF, ("P_KF",))):
                for g in range(nch):
                    mm(pbank[:, :], wslot_fn(g, part, pr),
                       src_arena[:, T * g + 512 * j:T * g + 512 * (j + 1)],
                       start=(g == 0), stop=(g == nch - 1),
                       reads=[(wname, g)] + src_key_fn(g, j), writes=[pkey])

        seq = [(j, pr) for j in range(NT) for pr in range(npair)]
        tab(0)
        proj(0, 0)
        for idx, (j, pr) in enumerate(seq):
            _rope_pair(P_QF, ("P_QF",), P_KF, ("P_KF",))
            if idx + 1 < len(seq):
                jn, prn = seq[idx + 1]
                if prn == 0:
                    tab(jn)
                proj(jn, prn)
            if qknorm:
                _qknorm_rkbf(P_S, ("P_S",), P_W, ("P_W",))
            for part in range(2):
                cache_write_fn(pr, part, j,
                               RKBF[:, 512 * part:512 * part + 512],
                               [("RKBF", part)])

    def rope_proj_psum(src_arena, src_key_fn, nch, wslot_fn, wname, npair,
                       ctab, stab, qknorm, cache_write_fn):
        def tab(j):
            dma("sync", ROPC[:, :], ctab.ap()[:, 512 * j:512 * (j + 1)],
                reads=[("ROPC",)], writes=[("ROPC",)], chan="w")
            dma("sync", ROPS[:, :], stab.ap()[:, 512 * j:512 * (j + 1)],
                reads=[("ROPS",)], writes=[("ROPS",)], chan="w")

        def proj(j, pr):
            for part in range(2):
                bi = 2 * pr + part
                for g in range(nch):
                    mm(BANKS[bi][:, :], wslot_fn(g, part, pr),
                       src_arena[:, T * g + 512 * j:T * g + 512 * (j + 1)],
                       start=(g == 0), stop=(g == nch - 1),
                       reads=[(wname, g)] + src_key_fn(g, j),
                       writes=[BKEYS[bi]])

        tab(0)
        for pr in range(npair):
            proj(0, pr)
        for j in range(NT):
            for pr in range(npair):
                _rope_pair(BANKS[2 * pr], BKEYS[2 * pr],
                           BANKS[2 * pr + 1], BKEYS[2 * pr + 1])
                if qknorm:
                    _qknorm_rkbf(BANKS[2 * pr], BKEYS[2 * pr],
                                 BANKS[2 * pr + 1], BKEYS[2 * pr + 1])
                if pr == npair - 1 and j + 1 < NT:
                    tab(j + 1)
                if j + 1 < NT:
                    proj(j + 1, pr)
                for part in range(2):
                    cache_write_fn(pr, part, j,
                                   RKBF[:, 512 * part:512 * part + 512],
                                   [("RKBF", part)])

    def rope_proj_psum_stream(src_arena, src_key_fn, nch, w_stream,
                              npair, ctab, stab, qknorm, cache_write_fn):
        sl = 0
        for j in range(NT):
            dma("sync", ROPC[:, :], ctab.ap()[:, 512 * j:512 * (j + 1)],
                reads=[("ROPC",)], writes=[("ROPC",)], chan="w")
            dma("sync", ROPS[:, :], stab.ap()[:, 512 * j:512 * (j + 1)],
                reads=[("ROPS",)], writes=[("ROPS",)], chan="w")
            for pr in range(npair):
                for part in range(2):
                    bi = 2 * pr + part
                    wb = npair * part + pr
                    for h2 in range(2):
                        par = sl % 2
                        sl += 1
                        PTs, ptk = ((PT0, ("PT", 0)) if par == 0
                                    else (PT1, ("PT", 1)))
                        dma("sync", PTs[:, :], w_stream.ap()[wb, h2],
                            reads=[("PCW", "wqs", wb, h2), ptk], writes=[ptk],
                            chan="w" if par == 0 else "w2")
                        for i in range(4):
                            g = 4 * h2 + i
                            mm(BANKS[bi][:, :], PTs[:, 128 * i:128 * i + 128],
                               src_arena[:, T * g + 512 * j:T * g + 512 * (j + 1)],
                               start=(g == 0), stop=(g == nch - 1),
                               reads=[ptk] + src_key_fn(g, j),
                               writes=[BKEYS[bi]])
            for pr in range(npair):
                _rope_pair(BANKS[2 * pr], BKEYS[2 * pr],
                           BANKS[2 * pr + 1], BKEYS[2 * pr + 1])
                if qknorm:
                    _qknorm_rkbf(BANKS[2 * pr], BKEYS[2 * pr],
                                 BANKS[2 * pr + 1], BKEYS[2 * pr + 1])
                for part in range(2):
                    cache_write_fn(pr, part, j,
                                   RKBF[:, 512 * part:512 * part + 512],
                                   [("RKBF", part)])

    RGQ = [("sync", "rg"), ("scalar", "rg"), ("sync", "rg2"), ("scalar", "rg2")]

    def kc_writer(arena, col0, name, dup, extra_w_fn=None):
        def w(pr, part, j, src_ap, keys):
            qi = 0
            for a in range(4):
                h = 4 * pr + a
                extra = extra_w_fn(h, part, j) if extra_w_fn else []
                if dup:
                    for rep in range(2):
                        r0 = 64 * rep + 32 * part
                        dst = arena[r0:r0 + 32,
                                    col0 + 4096 * h + 512 * j:col0 + 4096 * h + 512 * j + 512]
                        eng, ch = RGQ[qi % 4]
                        qi += 1
                        dma(eng, dst, src_ap[32 * a:32 * a + 32, :],
                            reads=keys, writes=[(name, h, part, j, rep)] + extra,
                            chan=ch)
                else:
                    pair = h // 2
                    r0 = 64 * (h % 2) + 32 * part
                    dst = arena[r0:r0 + 32,
                                col0 + 4096 * pair + 512 * j:col0 + 4096 * pair + 512 * j + 512]
                    eng, ch = RGQ[qi % 4]
                    qi += 1
                    dma(eng, dst, src_ap[32 * a:32 * a + 32, :],
                        reads=keys, writes=[(name, h, part, j)] + extra, chan=ch)
        return w

    def v_pass(src_arena, src_key_fn, nch, w_dram, scale_fn, nkv, vc_col0,
               vc_name, src_b16=None, pcname=None, merged=False):
        banks = [P_QF, P_KF, P_KV, P_V, P_SEL, P_S, P_Y, P_W]
        bkeys = [("P_QF",), ("P_KF",), ("P_KV",), ("P_V",), ("P_SEL",), ("P_S",),
                 ("P_Y",), ("P_W",)]
        width = 64 * nkv
        nhalf = (width + 511) // 512
        ngrp = 8 // nhalf
        si = 0
        for cg in range(NC // ngrp):
            for g in range(nch):
                for hh in range(nhalf):
                    cw = min(512, width - 512 * hh)
                    if src_b16 is not None:
                        slot, skey = ((YSTG, ("XSB",)) if si % 2 == 0
                                      else (XSQ, ("XSQ",)))
                        si += 1
                        dma("sync", slot[:, 0:cw],
                            src_b16[128 * g:128 * g + 128, 512 * hh:512 * hh + cw],
                            reads=[("PCW", pcname, g, hh), skey], writes=[skey],
                            chan="w" if si % 2 == 0 else "w2")
                    else:
                        slot, skey = YSTG, ("XSB",)
                        dma("sync", WS[:, 0:cw],
                            w_dram.ap()[128 * g:128 * g + 128, 512 * hh:512 * hh + cw],
                            reads=[("WS",)], writes=[("WS",)], chan="w")
                        if scale_fn is not None:
                            vop(lambda e, cw=cw, g=g: e.tensor_scalar(
                                out=YSTG[:, 0:cw], in0=WS[:, 0:cw], scalar1=scale_fn(g),
                                scalar2=None, op0=ALU.mult),
                                [("WS",), ("SC1",), ("SC2",), ("N1W",), ("N2W",)],
                                [("XSB",)])
                        else:
                            vop(lambda e, cw=cw: e.tensor_copy(
                                out=YSTG[:, 0:cw], in_=WS[:, 0:cw]),
                                [("WS",)], [("XSB",)])
                    for ci in range(ngrp):
                        c = cg * ngrp + ci
                        bi = ci * nhalf + hh
                        mm(banks[bi][:, 0:cw],
                           src_arena[:, T * g + 128 * c:T * g + 128 * c + 128],
                           slot[:, 0:cw],
                           start=(g == 0), stop=(g == nch - 1),
                           reads=[skey] + src_key_fn(g, c // 4),
                           writes=[bkeys[bi]])
            for ci in range(ngrp):
                c = cg * ngrp + ci
                for hh in range(nhalf):
                    cw = min(512, width - 512 * hh)
                    bi = ci * nhalf + hh
                    if merged:
                        d0 = vc_col0 + 1040 * c + 520 * hh
                        vop(lambda e, d0=d0, bi=bi: e.tensor_copy(
                            out=ARB[:, d0:d0 + 520].rearrange(
                                "p (b x) -> p b x", x=65)[:, :, 0:64],
                            in_=banks[bi][:, 0:512].rearrange(
                                "p (b x) -> p b x", x=64)),
                            [bkeys[bi]], [(vc_name, c)])
                    else:
                        vop(lambda e, c=c, hh=hh, cw=cw, bi=bi: e.tensor_copy(
                            out=ARB[:, vc_col0 + width * c + 512 * hh:
                                    vc_col0 + width * c + 512 * hh + cw],
                            in_=banks[bi][:, 0:cw]),
                            [bkeys[bi]], [(vc_name, c)])

    def attn_pass(qc_arena, kc_arena, kc_col0, kc_name, kc_dup, vc_col0, vc_name,
                  nkv, bias_col, window, tag, merged=False):
        width = 64 * nkv

        def krange(jq):
            lo = 0 if window is None else max(0, 4 * jq - 2)
            return list(range(lo, 4 * jq + 4))

        for jq in range(NT):
            pend = [None]
            for h in range(16):
                kv = h * nkv // 16
                par = h % 2
                pair = h // 2
                if par == 0:
                    PYB, pyk = P_Y, ("P_Y",)
                    y0, y1, drow, dbase = 0, 64, 64, 64
                else:
                    PYB, pyk = P_SEL, ("P_SEL",)
                    y0, y1, drow, dbase = 64, 128, 0, 0
                kts = krange(jq)
                nk = len(kts)

                def kparams(ki):
                    kt = kts[ki]
                    off = 128 * kt - 512 * jq
                    qlo = max(0, off)
                    qhi = 512 if window is None else min(512, off + 384)
                    if ki == 0:
                        qlo = 0
                    return kt, off, qlo, qhi, ki == 0, ki == nk - 1

                def emit_score(ki):
                    kt, off, qlo, qhi, first, last = kparams(ki)
                    if kc_dup:
                        lh = kc_arena[64 * par:64 * par + 64,
                                      kc_col0 + 4096 * kv + 128 * kt:
                                      kc_col0 + 4096 * kv + 128 * kt + 128]
                        lkeys = [(kc_name, kv, pp, kt // 4, par) for pp in range(2)]
                    else:
                        lh = kc_arena[64 * par:64 * par + 64,
                                      kc_col0 + 4096 * (kv // 2) + 128 * kt:
                                      kc_col0 + 4096 * (kv // 2) + 128 * kt + 128]
                        lkeys = [(kc_name, kv, pp, kt // 4) for pp in range(2)]
                    rh = qc_arena[64 * par:64 * par + 64,
                                  4096 * pair + 512 * jq + qlo:
                                  4096 * pair + 512 * jq + qhi]
                    rkeys = [("QC" + tag, h, pp, jq) for pp in range(2)]
                    sb_, sk_ = (P_KV, ("P_KV",)) if ki % 2 == 0 else (P_QF, ("P_QF",))
                    mm(sb_[:, qlo:qhi], lh, rh, start=True, stop=True,
                       reads=lkeys + rkeys, writes=[sk_], tp=(64 * par, 0))

                def emit_tail(ki):
                    kt, off, qlo, qhi, first, last = kparams(ki)
                    sb_, sk_ = (P_KV, ("P_KV",)) if ki % 2 == 0 else (P_QF, ("P_QF",))
                    PT = PT0 if ki % 2 == 0 else PT1
                    ptk = ("PT", ki % 2)
                    P.op("scalar", lambda e, sb_=sb_, PT=PT, qlo=qlo, qhi=qhi: e.activation(
                        out=PT[:, qlo:qhi], in_=sb_[:, qlo:qhi], func=AF.Exp,
                        bias=bias_col, scale=CONSTS[:, 3:4]),
                        reads=[sk_, ("CONSTS",)], writes=[ptk])
                    if off >= 0:
                        vop(lambda e, PT=PT, o=off: e.tensor_mul(
                            out=PT[:, o:o + 128], in0=PT[:, o:o + 128],
                            in1=TRIU[:, :]), [ptk, ("TRIU",)], [ptk])
                    if window is not None and off + 384 <= 512:
                        vop(lambda e, PT=PT, o=qhi - 128: e.tensor_mul(
                            out=PT[:, o:o + 128], in0=PT[:, o:o + 128],
                            in1=MGT[:, :]), [ptk, ("MGT",)], [ptk])
                    if first and qhi < 512:
                        vop(lambda e, PT=PT, qhi=qhi: e.memset(PT[:, qhi:512], 0.0),
                            [ptk], [ptk])
                        qhi = 512
                    if merged:
                        vc0 = vc_col0 + 1040 * kt + 65 * kv
                        if par == 0:
                            mm(PYB[0:65, qlo:qhi], ARB[:, vc0:vc0 + 65],
                               PT[:, qlo:qhi], start=first, stop=last,
                               reads=[(vc_name, kt), ptk], writes=[pyk],
                               tp=(0, 0))
                        else:
                            mm(PYB[y0:y1, qlo:qhi], ARB[:, vc0:vc0 + 64],
                               PT[:, qlo:qhi], start=first, stop=last,
                               reads=[(vc_name, kt), ptk], writes=[pyk],
                               tp=(0, 64))
                            mm(P_S[0:1, qlo:qhi], ONESCOL[:, :], PT[:, qlo:qhi],
                               start=first, stop=last,
                               reads=[("ONESCOL",), ptk], writes=[("P_S",)],
                               tp=(0, 0))
                    else:
                        vsl = ARB[:, vc_col0 + width * kt + 64 * kv:
                                  vc_col0 + width * kt + 64 * kv + 64]
                        mm(PYB[y0:y1, qlo:qhi], vsl, PT[:, qlo:qhi],
                           start=first, stop=last,
                           reads=[(vc_name, kt), ptk], writes=[pyk], tp=(0, 64 * par))
                        mm(P_S[drow:drow + 1, qlo:qhi], ONESCOL[:, :], PT[:, qlo:qhi],
                           start=first, stop=last,
                           reads=[("ONESCOL",), ptk], writes=[("P_S",)], tp=(0, dbase))

                emit_score(0)
                for ki in range(nk):
                    if ki + 1 < nk:
                        emit_score(ki + 1)
                    emit_tail(ki)

                def norm(par=par, pyk=pyk, pair=pair):
                    if par == 0:
                        dsrc, dkey = ((P_Y, pyk) if merged else (P_S, ("P_S",)))
                        vop(lambda e, dsrc=dsrc: e.reciprocal(
                            out=LNB[64:65, 0:512], in_=dsrc[64:65, 0:512]),
                            [dkey], [("LNB", "d0")])
                        mm(P_W[:, :], ONESF128[64:65, :], LNB[64:65, 0:512],
                           start=True, stop=True,
                           reads=[("LNB", "d0"), ("ONESF128",)], writes=[("P_W",)],
                           tp=(64, 0))
                        P.op("scalar", lambda e: e.copy(out=BCV[:, :], in_=P_W[:, :]),
                             reads=[("P_W",)], writes=[("BCV", "a")])
                        vop(lambda e: e.tensor_mul(out=YSTG[0:64, :],
                                                   in0=P_Y[0:64, 0:512],
                                                   in1=BCV[0:64, :]),
                            [pyk, ("BCV", "a")], [("XSB",)])
                    else:
                        vop(lambda e: e.reciprocal(out=LNB[0:1, 512:1024],
                                                   in_=P_S[0:1, 0:512]),
                            [("P_S",)], [("LNB", "d1")])
                        mm(P_V[:, :], ONESROW_F[:, :], LNB[0:1, 512:1024],
                           start=True, stop=True,
                           reads=[("LNB", "d1"), ("ONESROW_F",)], writes=[("P_V",)])
                        P.op("scalar", lambda e: e.copy(out=LNB[:, 0:512],
                                                        in_=P_V[:, :]),
                             reads=[("P_V",)], writes=[("LNB", 0)])
                        vop(lambda e: e.tensor_mul(out=YSTG[64:128, :],
                                                   in0=P_SEL[64:128, 0:512],
                                                   in1=LNB[64:128, 0:512]),
                            [pyk, ("LNB", 0)], [("XSB",)])
                        dma("scalar",
                            ysp.ap()[128 * pair:128 * pair + 128,
                                     512 * jq:512 * (jq + 1)], YSTG[:, :],
                            reads=[("XSB",)],
                            writes=[("ysp", tag, pair, jq)], chan="ysp")

                if pend[0] is not None:
                    pend[0]()
                pend[0] = norm
                pump(6, "att")
            pend[0]()

    P.scope("P3a_k")
    W3C = 20480
    P.claim_prefix([("W3", g) for g in range(NCH)], {"SELR", "SELT", "BSB"})
    P.claim_prefix([("KC", h, p, j) for h in range(16) for p in range(2)
                    for j in range(NT)], {"XB"})
    load_w_b16(lambda g: ARB[:, W3C + 1024 * g:W3C + 1024 * (g + 1)], wkb.ap(),
               1024, "W3", "wk")

    def w3_slot(g, part, pr, npair=4, col0=W3C):
        c = col0 + 1024 * g + 128 * (npair * part + pr)
        return ARB[:, c:c + 128]

    rope_proj_pass(RMS1X, lambda g, j: [("RMS1X", g, j)], NCH, w3_slot, "W3",
                   4, C_COSL, C_SINL, False, kc_writer(XB, 0, "KC", False))
    P.scope("P3b_v")
    P.claim_prefix([("VC3", c) for c in range(NC)],
                   {"QF", "KF", "VTOK", "KTOKP", "QF2", "SB1", "KF2", "PP", "PM",
                    "SS", "S01M", "S01B", "YSQ", "BSB", "SELR", "SELT", "W3",
                    "S2M", "XSUM1"})
    vop(lambda e: e.memset(ARB[:, 0:33280], 1.0), (),
        [("VC3", c) for c in range(NC)])
    v_pass(RMS1X, lambda g, j: [("RMS1X", g, j)], NCH, None, None, 16, 0,
           "VC3", src_b16=wvb.ap(), pcname="wv", merged=True)
    P.scope("P3c_q")
    rope_proj_psum_stream(RMS1X, lambda g, j: [("RMS1X", g, j)], NCH, wqsb,
                          4, C_COSL, C_SINL, False,
                          kc_writer(ARA, 0, "QCl0", False,
                                    extra_w_fn=lambda h, p, j: [("RMS1X", h // 2, j)]))
    for _ex in range(8):
        pc_add("w1", F_w1.ap()[_ex], w1b.ap()[_ex], D, D, ex=_ex)
        pc_add("w2", F_w2.ap()[_ex], w2b.ap()[_ex], D, D, ex=_ex)
        pc_add("wout", F_wout.ap()[_ex], woutb.ap()[_ex], D, D, ex=_ex)
    P.scope("P3d_attn")
    attn_pass(ARA, XB, 0, "KC", False, 0, "VC3", 16, CONSTS[:, 4:5], None, "l0",
              merged=True)
    assert not PCPEND, f"group B pre-casts not drained: {len(PCPEND)}"
    P.scope("P3e_wo")
    P.claim_prefix([("Wl0o", g) for g in range(NCH)]
                   + [("YT", g) for g in range(NCH)]
                   + [("XT", g) for g in range(NCH)]
                   + [("HT", g) for g in range(NCH)], {"VC3"})
    load_w_b16(lambda g: ARB[:, 1024 * g:1024 * (g + 1)], wob.ap(), 1024,
               "Wl0o", "wo")
    for j in range(NT):
        for g in range(NCH):
            dma("sync", ARB[:, 8192 + 512 * g:8192 + 512 * (g + 1)],
                ysp.ap()[128 * g:128 * g + 128, 512 * j:512 * (j + 1)],
                reads=[("ysp", "l0", g, j), ("YT", g)], writes=[("YT", g)],
                chan="xt")
            dma("scalar", ARB[:, 12288 + 512 * g:12288 + 512 * (g + 1)],
                x1sp.ap()[128 * g:128 * g + 128, 512 * j:512 * (j + 1)],
                reads=[("x1sp", g, j), ("XT", g)], writes=[("XT", g)], chan="xt2")
        for o in range(NCH):
            for g in range(NCH):
                mm(P_W[:, :], ARB[:, 1024 * g + 128 * o:1024 * g + 128 * o + 128],
                   ARB[:, 8192 + 512 * g:8192 + 512 * (g + 1)],
                   start=(g == 0), stop=(g == NCH - 1),
                   reads=[("Wl0o", g), ("YT", g)], writes=[("P_W",)])
            vop(lambda e, o=o: e.tensor_add(
                out=ARB[:, 16384 + 512 * o:16384 + 512 * (o + 1)], in0=P_W[:, :],
                in1=ARB[:, 12288 + 512 * o:12288 + 512 * (o + 1)]),
                [("P_W",), ("XT", o)], [("HT", o)])
        for g in range(NCH):
            dma("scalar", hsp.ap()[128 * g:128 * g + 128, 512 * j:512 * (j + 1)],
                ARB[:, 16384 + 512 * g:16384 + 512 * (g + 1)],
                reads=[("HT", g)], writes=[("hsp", g, j)], chan="sp")

    if stop == "h":
        debug_hbm_out(hsp, "hsp")
        P.finalize()
        return nc

    P.scope("P4_router")
    P.claim_prefix([("WS",)], {"WSB"})
    WTG8 = XB[0:8, 28672:32768]
    P.claim_prefix([("HT4", g) for g in range(NCH)]
                   + [("RWS", g) for g in range(NCH)], {"QCl0"})
    P.claim_prefix([("WTG8", j, tc) for j in range(NT) for tc in range(4)],
                   {"KC"})
    P.claim_prefix([("R2X", g, j) for g in range(7) for j in range(NT)], {"KC"})
    P.claim_prefix([("R2X", 7, j) for j in range(NT)], {"VC3"})

    def r2x_slot(g, j):
        if g < 7:
            return XB[:, 4096 * g + 512 * j:4096 * g + 512 * (j + 1)]
        return ARB[:, 24576 + 512 * j:24576 + 512 * (j + 1)]
    load_w(lambda g: ARA[:, 8192 + 8 * g:8192 + 8 * (g + 1)], F_rw.ap(), 8,
           "RWS")
    for j in range(NT):
        for g in range(NCH):
            dma("scalar", ARA[:, 512 * g:512 * (g + 1)],
                hsp.ap()[128 * g:128 * g + 128, 512 * j:512 * (j + 1)],
                reads=[("hsp", g, j), ("HT4", g)], writes=[("HT4", g)], chan="xt")
        stats_sumsq(lambda g: (ARA[:, 512 * g:512 * (g + 1)], [("HT4", g)]),
                    None, via_f32=False)
        rms_r(CONSTS[0:1, 2:3])
        for g in range(NCH):
            vop(lambda e, g=g, j=j: e.scalar_tensor_tensor(
                out=r2x_slot(g, j),
                in0=ARA[:, 512 * g:512 * (g + 1)], scalar=SC2[:, g:g + 1],
                in1=LNB[:, 512:1024], op0=ALU.mult, op1=ALU.mult),
                [("HT4", g), ("SC2",), ("LNB", 512)], [("R2X", g, j)])
        for tc in range(4):
            for g in range(NCH):
                mm(P_S[:, 0:8], r2x_slot(g, j)[:, 128 * tc:128 * tc + 128],
                   ARA[:, 8192 + 8 * g:8192 + 8 * (g + 1)],
                   start=(g == 0), stop=(g == NCH - 1),
                   reads=[("R2X", g, j), ("RWS", g)], writes=[("P_S",)])
            dma("sync", GUM[:, :],
                F_gum.ap()[512 * j + 128 * tc:512 * j + 128 * tc + 128, :],
                reads=[("GUM",)], writes=[("GUM",)], chan="w")
            vop(lambda e: e.tensor_add(out=WTOK[:, 0:8], in0=P_S[:, 0:8],
                                       in1=GUM[:, :]),
                [("P_S",), ("GUM",)], [("WTOK",)])
            vop(lambda e: e.reduce_max(out=WTOK[:, 8:9], in_=WTOK[:, 0:8],
                                       axis=mybir.AxisListType.X, negate=True),
                [("WTOK",)], [("WTOK", "nm")])
            P.op("scalar", lambda e: e.activation(
                out=WTOK[:, 16:24], in_=WTOK[:, 0:8], func=AF.Exp,
                bias=WTOK[:, 8:9], scale=1.0),
                reads=[("WTOK",), ("WTOK", "nm")], writes=[("WTOK", "e")])
            vop(lambda e: e.reduce_sum(out=WTOK[:, 9:10], in_=WTOK[:, 16:24],
                                       axis=mybir.AxisListType.X),
                [("WTOK", "e")], [("WTOK", "s")])
            vop(lambda e: e.reciprocal(out=WTOK[:, 10:11], in_=WTOK[:, 9:10]),
                [("WTOK", "s")], [("WTOK", "rs")])
            vop(lambda e: e.reduce_max(out=WTOK[:, 11:12], in_=WTOK[:, 16:24],
                                       axis=mybir.AxisListType.X),
                [("WTOK", "e")], [("WTOK", "m1")])
            vop(lambda e: e.tensor_scalar(out=WTOK[:, 0:8], in0=WTOK[:, 16:24],
                                          scalar1=WTOK[:, 11:12], scalar2=None,
                                          op0=ALU.is_ge),
                [("WTOK", "e"), ("WTOK", "m1")], [("WTOK",)])
            vop(lambda e: e.tensor_scalar(out=WTOK[:, 0:8], in0=WTOK[:, 0:8],
                                          scalar1=-1.0, scalar2=1.0,
                                          op0=ALU.mult, op1=ALU.add),
                [("WTOK",)], [("WTOK",)])
            vop(lambda e: e.tensor_mul(out=WTOK[:, 0:8], in0=WTOK[:, 0:8],
                                       in1=WTOK[:, 16:24]),
                [("WTOK",), ("WTOK", "e")], [("WTOK",)])
            vop(lambda e: e.reduce_max(out=WTOK[:, 12:13], in_=WTOK[:, 0:8],
                                       axis=mybir.AxisListType.X),
                [("WTOK",)], [("WTOK", "m2")])
            vop(lambda e: e.tensor_scalar(out=WTOK[:, 0:8], in0=WTOK[:, 16:24],
                                          scalar1=WTOK[:, 12:13], scalar2=None,
                                          op0=ALU.is_ge),
                [("WTOK", "e"), ("WTOK", "m2")], [("WTOK",)])
            vop(lambda e: e.tensor_mul(out=WTOK[:, 0:8], in0=WTOK[:, 0:8],
                                       in1=WTOK[:, 16:24]),
                [("WTOK",), ("WTOK", "e")], [("WTOK",)])
            vop(lambda e: e.tensor_scalar(out=WTOK[:, 0:8], in0=WTOK[:, 0:8],
                                          scalar1=WTOK[:, 10:11], scalar2=None,
                                          op0=ALU.mult),
                [("WTOK",), ("WTOK", "rs")], [("WTOK",)])
            vop(lambda e: e.tensor_copy(out=WTOKB[:, :], in_=WTOK[:, 0:8]),
                [("WTOK",)], [("WTOKB",)])
            P.op("tensor", lambda e: e.transpose(
                P_W[0:8, 0:64].bitcast(BF16), WTOKB[:, :], IDENT[:, :]),
                reads=[("WTOKB",), ("IDENT",)], writes=[("P_W",)])
            vop(lambda e, j=j, tc=tc: e.tensor_copy(
                out=WTG8[:, 512 * j + 128 * tc:512 * j + 128 * tc + 128],
                in_=P_W[0:8, 0:64].bitcast(BF16)),
                [("P_W",)], [("WTG8", j, tc)])

    P.scope("P6_moe")
    P.claim_prefix([("XSQ",)], {"XSQB"})
    ACC = ARA
    BOUTT = ARB[0:8, 32256:33280]
    P.claim_prefix([("ACC", g, j) for g in range(NCH) for j in range(NT)],
                   {"HT4", "R2B", "RWS"})
    P.claim_prefix([("BOUTT",)], {"VC3"})
    P.claim_prefix([("W1e", g) for g in range(NCH)]
                   + [("W2e", g) for g in range(NCH)]
                   + [("WOUTe", g) for g in range(NCH)],
                   {"Wl0o", "YT", "XT", "HT", "VC3"})
    for hh in range(2):
        dma("sync", WS[0:8, 0:512], F_bout.ap()[:, 512 * hh:512 * (hh + 1)],
            reads=[("WS",)], writes=[("WS",)], chan="w")
        vop(lambda e, hh=hh: e.tensor_copy(
            out=BOUTT[:, 512 * hh:512 * (hh + 1)], in_=WS[0:8, 0:512]),
            [("WS",)], [("BOUTT",)])
    for j in range(NT):
        for o in range(NCH):
            mm(P_W[:, :], BOUTT[:, 128 * o:128 * (o + 1)],
               WTG8[:, 512 * j:512 * (j + 1)],
               start=True, stop=True,
               reads=[("BOUTT",)] + [("WTG8", j, tc) for tc in range(4)],
               writes=[("P_W",)])
            P.op("scalar", lambda e, o=o, j=j: e.copy(
                out=ACC[:, 4096 * o + 512 * j:4096 * o + 512 * (j + 1)],
                in_=P_W[:, :]),
                reads=[("P_W",)], writes=[("ACC", o, j)])
    P.claim_prefix([("HE", g) for g in range(NCH)], {"VC3", "BOUTT"})
    pendw = [None]
    for ex in range(8):
        if pendw[0] is not None:
            pendw[0]()
            pendw[0] = None
        load_w_b16(lambda g: ARB[:, 1024 * g:1024 * (g + 1)], w1b.ap()[ex],
                   1024, "W1e", "w1", ex=ex)
        load_w_b16(lambda g: ARB[:, 8192 + 1024 * g:8192 + 1024 * (g + 1)],
                   w2b.ap()[ex], 1024, "W2e", "w2", ex=ex)
        load_w_b16(lambda g: ARB[:, 16384 + 1024 * g:16384 + 1024 * (g + 1)],
                   woutb.ap()[ex], 1024, "WOUTe", "wout", ex=ex)
        for o in range(NCH):
            dma("sync", BB1[:, o:o + 1],
                F_b1.ap()[ex:ex + 1, 128 * o:128 * o + 128],
                reads=[("BB1",)], writes=[("BB1",)], chan="w")
            dma("sync", BB2[:, o:o + 1],
                F_b2.ap()[ex:ex + 1, 128 * o:128 * o + 128],
                reads=[("BB2",)], writes=[("BB2",)], chan="w")
        for j in range(NT):
            mm(P_S[:, :], SEL8[:, 128 * ex:128 * (ex + 1)],
               WTG8[:, 512 * j:512 * (j + 1)],
               start=True, stop=True,
               reads=[("WTG8", j, tc) for tc in range(4)] + [("SEL8",)],
               writes=[("P_S",)])
            P.op("scalar", lambda e: e.copy(out=BCV[:, :], in_=P_S[:, :]),
                 reads=[("P_S",)], writes=[("BCV", "a")])
            for o in range(NCH):
                bh1, kh1 = (P_QF, ("P_QF",)) if o % 2 == 0 else (P_KV, ("P_KV",))
                bh2, kh2 = (P_KF, ("P_KF",)) if o % 2 == 0 else (P_V, ("P_V",))
                sl1, ks1 = (XSB, ("XSB",)) if o % 2 == 0 else (XSQ, ("XSQ",))
                sl2, ks2 = (PT0, ("PT", 0)) if o % 2 == 0 else (PT1, ("PT", 1))
                for g in range(NCH):
                    mm(bh1[:, :],
                       ARB[:, 1024 * g + 128 * o:1024 * g + 128 * o + 128],
                       r2x_slot(g, j),
                       start=(g == 0), stop=(g == NCH - 1),
                       reads=[("W1e", g), ("R2X", g, j)], writes=[kh1])
                for g in range(NCH):
                    mm(bh2[:, :],
                       ARB[:, 8192 + 1024 * g + 128 * o:8192 + 1024 * g + 128 * o + 128],
                       r2x_slot(g, j),
                       start=(g == 0), stop=(g == NCH - 1),
                       reads=[("W2e", g), ("R2X", g, j)], writes=[kh2])
                if o == 0 and pendw[0] is not None:
                    pendw[0]()
                    pendw[0] = None
                P.op("scalar", lambda e, o=o, bh1=bh1, sl1=sl1: e.activation(
                    out=sl1[:, :], in_=bh1[:, :], func=AF.Silu,
                    bias=BB1[:, o:o + 1], scale=1.0),
                    reads=[kh1, ("BB1",)], writes=[ks1])
                vop(lambda e, o=o, bh2=bh2, sl1=sl1, sl2=sl2: e.scalar_tensor_tensor(
                    out=sl2[:, :], in0=bh2[:, :], scalar=BB2[:, o:o + 1],
                    in1=sl1[:, :], op0=ALU.add, op1=ALU.mult),
                    [kh2, ("BB2",), ks1], [ks2])
                vop(lambda e, o=o, sl2=sl2: e.tensor_mul(
                    out=ARB[:, 28672 + 512 * o:28672 + 512 * (o + 1)],
                    in0=sl2[:, :], in1=BCV[:, :]),
                    [ks2, ("BCV", "a")], [("HE", o)])
            def wout_block(j=j):
                for o in range(NCH):
                    bw, kw = (P_W, ("P_W",)) if o % 2 == 0 else (P_SEL, ("P_SEL",))
                    for g in range(NCH):
                        mm(bw[:, :],
                           ARB[:, 16384 + 1024 * g + 128 * o:16384 + 1024 * g + 128 * o + 128],
                           ARB[:, 28672 + 512 * g:28672 + 512 * (g + 1)],
                           start=(g == 0), stop=(g == NCH - 1),
                           reads=[("WOUTe", g), ("HE", g)], writes=[kw])
                    vop(lambda e, o=o, j=j, bw=bw: e.scalar_tensor_tensor(
                        out=ACC[:, 4096 * o + 512 * j:4096 * o + 512 * (j + 1)],
                        in0=bw[:, :], scalar=1.0,
                        in1=ACC[:, 4096 * o + 512 * j:4096 * o + 512 * (j + 1)],
                        op0=ALU.mult, op1=ALU.add),
                        [kw], [("ACC", o, j)])

            pendw[0] = wout_block
    pendw[0]()
    pendw[0] = None

    P.scope("P7_ln")
    XSUM7 = carve(ARB, 24576, 8192, F32)
    P.claim_prefix([("XSUM7", g) for g in range(NCH)], {"R2X", "HE"})
    P.claim_prefix([("XB2", g, j) for g in range(NCH) for j in range(NT)],
                   {"KC", "WTG8", "R2X", "XB"})
    for j in range(NT):
        for g in range(NCH):
            dma("scalar", PT0[:, :],
                x1sp.ap()[128 * g:128 * g + 128, 512 * j:512 * (j + 1)],
                reads=[("x1sp", g, j), ("PT", 0)], writes=[("PT", 0)], chan="xt")
            dma("scalar", PT1[:, :],
                hsp.ap()[128 * g:128 * g + 128, 512 * j:512 * (j + 1)],
                reads=[("hsp", g, j), ("PT", 1)], writes=[("PT", 1)], chan="xt2")
            vop(lambda e, g=g, j=j: e.tensor_add(
                out=XSUM7[:, 512 * g:512 * (g + 1)],
                in0=ACC[:, 4096 * g + 512 * j:4096 * g + 512 * (j + 1)],
                in1=PT0[:, :]),
                [("ACC", g, j), ("PT", 0)], [("XSUM7", g)])
            vop(lambda e, g=g: e.scalar_tensor_tensor(
                out=XSUM7[:, 512 * g:512 * (g + 1)], in0=PT1[:, :], scalar=1.0,
                in1=XSUM7[:, 512 * g:512 * (g + 1)], op0=ALU.mult, op1=ALU.add),
                [("PT", 1)], [("XSUM7", g)])
        stats_sumsq(lambda g: (XSUM7[:, 512 * g:512 * (g + 1)], [("XSUM7", g)]),
                    None, also_sum=True, via_f32=True)
        ln_finish(CONSTS[0:1, 1:2],
                  src_fn=lambda g: (XSUM7[:, 512 * g:512 * (g + 1)],
                                    [("XSUM7", g)]),
                  out_writer=lambda g, j=j: vop(
                      lambda e, g=g: e.tensor_mul(
                          out=XB[:, T * g + 512 * j:T * g + 512 * (j + 1)],
                          in0=WS[:, :], in1=LNB[:, 512:1024]),
                      [("WS",), ("LNB", 512)], [("XB2", g, j)]))

    if stop == "x2":
        debug_stream_out(lambda g, q: [("XB2", g, q)])
        P.finalize()
        return nc

    P.scope("P8_glob")
    N1X = ARA
    P.claim_prefix([("N1X", g, j) for g in range(NCH) for j in range(NT)],
                   {"ACC", "HT4", "R2B", "RWS"})
    for j in range(NT):
        stats_sumsq(lambda g, j=j: (XB[:, T * g + 512 * j:T * g + 512 * (j + 1)],
                                    [("XB2", g, j)]), None, via_f32=False)
        rms_r(CONSTS[0:1, 2:3])
        for g in range(NCH):
            vop(lambda e, g=g, j=j: e.scalar_tensor_tensor(
                out=N1X[:, T * g + 512 * j:T * g + 512 * (j + 1)],
                in0=XB[:, T * g + 512 * j:T * g + 512 * (j + 1)],
                scalar=N1W[:, g:g + 1], in1=LNB[:, 512:1024],
                op0=ALU.mult, op1=ALU.mult),
                [("XB2", g, j), ("N1W",), ("LNB", 512)], [("N1X", g, j)])
    P.claim_prefix([("WKVD", g) for g in range(NCH)] + [("WKUP", g) for g in range(2)],
                   {"XSUM7"})
    P.claim_prefix([("LATB", m, j) for m in range(2) for j in range(NT)],
                   {"XSUM7"})
    load_w(lambda g: ARB[:, 16384 + 256 * g:16384 + 256 * (g + 1)], G_kvd.ap(),
           256, "WKVD")
    for j in range(NT):
        for m2 in range(2):
            lb, lk = (P_QF, ("P_QF",)) if m2 == 0 else (P_KF, ("P_KF",))
            for g in range(NCH):
                mm(lb[:, :],
                   ARB[:, 16384 + 256 * g + 128 * m2:16384 + 256 * g + 128 * m2 + 128],
                   N1X[:, T * g + 512 * j:T * g + 512 * (j + 1)],
                   start=(g == 0), stop=(g == NCH - 1),
                   reads=[("WKVD", g), ("N1X", g, j)], writes=[lk])
            vop(lambda e, m2=m2, j=j, lb=lb: e.tensor_copy(
                out=ARB[:, 24576 + 4096 * m2 + 512 * j:24576 + 4096 * m2 + 512 * (j + 1)],
                in_=lb[:, :]),
                [lk], [("LATB", m2, j)])
    LATB = ARB[:, 24576:32768]
    load_w(lambda g: ARB[:, 18432 + 256 * g:18432 + 256 * (g + 1)], G_kup.ap(),
           256, "WKUP", nch=2)
    P.claim_prefix([("KCg", h, p, j, r) for h in range(4) for p in range(2)
                    for j in range(NT) for r in range(2)],
                   {"W1e", "W2e", "WOUTe", "BOUTT", "VC3"})
    rope_proj_pass(LATB, lambda g, j: [("LATB", g, j)], 2,
                   lambda g, part, pr: ARB[:, 18432 + 256 * g + 128 * part:
                                           18432 + 256 * g + 128 * part + 128],
                   "WKUP", 1, C_COSG, C_SING, True,
                   kc_writer(ARB, 0, "KCg", True))
    P.claim_prefix([("VCg", c) for c in range(NC)], {"WKVD", "WKUP"})
    v_pass(LATB, lambda g, j: [("LATB", g, j)], 2, G_vup, None, 4, 16384, "VCg")
    P.claim_prefix([("W8", g) for g in range(NCH)], {"LATB"})
    load_w(lambda g: ARB[:, 24576 + 1024 * g:24576 + 1024 * (g + 1)], G_wq.ap(),
           1024, "W8")
    rope_proj_psum(N1X, lambda g, j: [("N1X", g, j)], NCH,
                   lambda g, part, pr: ARB[:, 24576 + 1024 * g + 128 * (4 * part + pr):
                                           24576 + 1024 * g + 128 * (4 * part + pr) + 128],
                   "W8", 4, C_COSG, C_SING, True,
                   kc_writer(ARA, 0, "QCg", False,
                             extra_w_fn=lambda h, p, j: [("N1X", h // 2, j)]))
    attn_pass(ARA, ARB, 0, "KCg", True, 16384, "VCg", 4, CONSTS[:, 5:6], None, "g")
    P.claim_prefix([("W8o", g) for g in range(NCH)], {"W8"})
    P.claim_prefix([("YT8", g) for g in range(NCH)], {"VCg"})
    P.claim_prefix([("XMT8", g) for g in range(NCH)], {"VCg"})
    load_w(lambda g: ARB[:, 24576 + 1024 * g:24576 + 1024 * (g + 1)], G_wo.ap(),
           1024, "W8o")
    for j in range(NT):
        for g in range(NCH):
            dma("sync", ARB[:, 16384 + 512 * g:16384 + 512 * (g + 1)],
                ysp.ap()[128 * g:128 * g + 128, 512 * j:512 * (j + 1)],
                reads=[("ysp", "g", g, j), ("YT8", g)], writes=[("YT8", g)],
                chan="xt")
        for o in range(NCH):
            for g in range(NCH):
                mm(P_W[:, :],
                   ARB[:, 24576 + 1024 * g + 128 * o:24576 + 1024 * g + 128 * o + 128],
                   ARB[:, 16384 + 512 * g:16384 + 512 * (g + 1)],
                   start=(g == 0), stop=(g == NCH - 1),
                   reads=[("W8o", g), ("YT8", g)], writes=[("P_W",)])
            vop(lambda e, o=o, j=j: e.tensor_add(
                out=ARB[:, 20480 + 512 * o:20480 + 512 * (o + 1)], in0=P_W[:, :],
                in1=XB[:, T * o + 512 * j:T * o + 512 * (j + 1)]),
                [("P_W",), ("XB2", o, j)], [("XMT8", o)])
        for g in range(NCH):
            dma("scalar", xmsp.ap()[128 * g:128 * g + 128, 512 * j:512 * (j + 1)],
                ARB[:, 20480 + 512 * g:20480 + 512 * (g + 1)],
                reads=[("XMT8", g)], writes=[("xmsp", g, j)], chan="sp")

    if stop == "xm":
        debug_hbm_out(xmsp, "xmsp")
        P.finalize()
        return nc

    P.scope("P9_loc")
    N2X = ARA
    P.claim_prefix([("N2X", g, j) for g in range(NCH) for j in range(NT)],
                   {"QCg", "N1X"})
    P.claim_prefix([("XMT9", g, p) for g in range(NCH) for p in range(2)],
                   {"W8o"})

    def xmt9(g, j):
        c0 = 24576 + 4096 * (j % 2) + 512 * g
        return ARB[:, c0:c0 + 512]

    for g in range(NCH):
        dma("scalar", xmt9(g, 0),
            xmsp.ap()[128 * g:128 * g + 128, 0:512],
            reads=[("xmsp", g, 0), ("XMT9", g, 0)], writes=[("XMT9", g, 0)],
            chan="xt")
    for j in range(NT):
        if j + 1 < NT:
            for g in range(NCH):
                dma("scalar", xmt9(g, j + 1),
                    xmsp.ap()[128 * g:128 * g + 128, 512 * (j + 1):512 * (j + 2)],
                    reads=[("xmsp", g, j + 1), ("XMT9", g, (j + 1) % 2)],
                    writes=[("XMT9", g, (j + 1) % 2)], chan="xt")
        stats_sumsq(lambda g, j=j: (xmt9(g, j), [("XMT9", g, j % 2)]),
                    None, via_f32=False)
        rms_r(CONSTS[0:1, 2:3])
        for g in range(NCH):
            vop(lambda e, g=g, j=j: e.scalar_tensor_tensor(
                out=N2X[:, T * g + 512 * j:T * g + 512 * (j + 1)],
                in0=xmt9(g, j),
                scalar=N2W[:, g:g + 1], in1=LNB[:, 512:1024],
                op0=ALU.mult, op1=ALU.mult),
                [("XMT9", g, j % 2), ("N2W",), ("LNB", 512)], [("N2X", g, j)])
    P.claim_prefix([("WLK", g) for g in range(NCH)], {"YT8", "XMT8", "VCg"})
    load_w(lambda g: ARB[:, 16384 + 256 * g:16384 + 256 * (g + 1)], L_wk.ap(),
           256, "WLK")
    P.claim_prefix([("KCl", h, p, j, r) for h in range(4) for p in range(2)
                    for j in range(NT) for r in range(2)], {"KCg"})
    rope_proj_pass(N2X, lambda g, j: [("N2X", g, j)], NCH,
                   lambda g, part, pr: ARB[:, 16384 + 256 * g + 128 * part:
                                           16384 + 256 * g + 128 * part + 128],
                   "WLK", 1, C_COSL, C_SINL, True,
                   kc_writer(ARB, 0, "KCl", True))
    P.claim_prefix([("VCl", c) for c in range(NC)], {"WLK"})
    v_pass(N2X, lambda g, j: [("N2X", g, j)], NCH, L_wv, None, 4, 16384,
           "VCl")
    P.claim_prefix([("W9", g) for g in range(NCH)], {"XMT9"})
    load_w(lambda g: ARB[:, 24576 + 1024 * g:24576 + 1024 * (g + 1)], L_wq.ap(),
           1024, "W9")
    rope_proj_psum(N2X, lambda g, j: [("N2X", g, j)], NCH,
                   lambda g, part, pr: ARB[:, 24576 + 1024 * g + 128 * (4 * part + pr):
                                           24576 + 1024 * g + 128 * (4 * part + pr) + 128],
                   "W9", 4, C_COSL, C_SINL, True,
                   kc_writer(ARA, 0, "QCl", False,
                             extra_w_fn=lambda h, p, j: [("N2X", h // 2, j)]))
    attn_pass(ARA, ARB, 0, "KCl", True, 16384, "VCl", 4, CONSTS[:, 5:6], 256, "l")
    P.claim_prefix([("W9o", g) for g in range(NCH)], {"W9"})
    P.claim_prefix([("YT9", g) for g in range(NCH)], {"VCl"})
    P.claim_prefix([("XMT", g) for g in range(NCH)], {"VCl"})
    P.claim_prefix([("XSUM9", g) for g in range(NCH)], {"KCl"})
    XSUM9 = carve(ARB, 0, 8192, F32)
    load_w(lambda g: ARB[:, 24576 + 1024 * g:24576 + 1024 * (g + 1)], L_wo.ap(),
           1024, "W9o")
    for j in range(NT):
        for g in range(NCH):
            dma("sync", ARB[:, 16384 + 512 * g:16384 + 512 * (g + 1)],
                ysp.ap()[128 * g:128 * g + 128, 512 * j:512 * (j + 1)],
                reads=[("ysp", "l", g, j), ("YT9", g)], writes=[("YT9", g)],
                chan="xt")
            dma("scalar", ARB[:, 20480 + 512 * g:20480 + 512 * (g + 1)],
                xmsp.ap()[128 * g:128 * g + 128, 512 * j:512 * (j + 1)],
                reads=[("xmsp", g, j), ("XMT", g)], writes=[("XMT", g)],
                chan="xt2")
        for o in range(NCH):
            for g in range(NCH):
                mm(P_W[:, :],
                   ARB[:, 24576 + 1024 * g + 128 * o:24576 + 1024 * g + 128 * o + 128],
                   ARB[:, 16384 + 512 * g:16384 + 512 * (g + 1)],
                   start=(g == 0), stop=(g == NCH - 1),
                   reads=[("W9o", g), ("YT9", g)], writes=[("P_W",)])
            vop(lambda e, o=o, j=j: e.tensor_add(
                out=XSUM9[:, 512 * o:512 * (o + 1)], in0=P_W[:, :],
                in1=XB[:, T * o + 512 * j:T * o + 512 * (j + 1)]),
                [("P_W",), ("XB2", o, j)], [("XSUM9", o)])
            vop(lambda e, o=o: e.scalar_tensor_tensor(
                out=XSUM9[:, 512 * o:512 * (o + 1)],
                in0=ARB[:, 20480 + 512 * o:20480 + 512 * (o + 1)], scalar=1.0,
                in1=XSUM9[:, 512 * o:512 * (o + 1)], op0=ALU.mult, op1=ALU.add),
                [("XMT", o)], [("XSUM9", o)])
        stats_sumsq(lambda g: (XSUM9[:, 512 * g:512 * (g + 1)], [("XSUM9", g)]),
                    None, also_sum=True, via_f32=True)
        ln_finish(CONSTS[0:1, 1:2],
                  src_fn=lambda g: (XSUM9[:, 512 * g:512 * (g + 1)],
                                    [("XSUM9", g)]),
                  out_writer=lambda g, j=j: vop(
                      lambda e, g=g: e.tensor_mul(
                          out=XB[:, T * g + 512 * j:T * g + 512 * (j + 1)],
                          in0=WS[:, :], in1=LNB[:, 512:1024]),
                      [("WS",), ("LNB", 512)], [("XB3", g, j)]))

    if stop == "x3":
        debug_stream_out(lambda g, q: [("XB3", g, q)])
        P.finalize()
        return nc

    P.scope("P10_ffn")
    P.claim_prefix([("U2A", g) for g in range(32)] + [("V1A", g) for g in range(2)]
                   + [("U1A", g) for g in range(NCH)] + [("V2A", g) for g in range(2)]
                   + [("XSUM10", g) for g in range(NCH)],
                   {"N2X", "QCl"})
    U2A = carve(ARA, 0, 8192)
    V1A = carve(ARA, 8192, 8192)
    U1A = carve(ARA, 16384, 2048)
    V2A = carve(ARA, 18432, 2048)
    XSUM10 = carve(ARA, 20480, 8192, F32)
    load_w(lambda g: U2A[:, 256 * g:256 * (g + 1)], U2.ap(), 256, "U2A", nch=32)
    load_w(lambda g: V1A[:, 4096 * g:4096 * (g + 1)], V1.ap(), 4096, "V1A", nch=2)
    load_w(lambda g: U1A[:, 256 * g:256 * (g + 1)], U1.ap(), 256, "U1A")
    load_w(lambda g: V2A[:, 1024 * g:1024 * (g + 1)], V2.ap(), 1024, "V2A", nch=2)
    for j in range(NT):
        for m2, dstb, dkey in ((0, XSB, ("XSB",)), (1, XSQ, ("XSQ",))):
            for g in range(NCH):
                mm(P_QF[:, :],
                   U1A[:, 256 * g + 128 * m2:256 * g + 128 * m2 + 128],
                   XB[:, T * g + 512 * j:T * g + 512 * (j + 1)],
                   start=(g == 0), stop=(g == NCH - 1),
                   reads=[("U1A", g), ("XB3", g, j)], writes=[("P_QF",)])
            vop(lambda e, dstb=dstb: e.tensor_copy(out=dstb[:, :], in_=P_QF[:, :]),
                [("P_QF",)], [dkey])
        def v1mm(ff):
            hb, hk = (P_KV, ("P_KV",)) if ff % 2 == 0 else (P_QF, ("P_QF",))
            mm(hb[:, :], V1A[:, 128 * ff:128 * ff + 128], XSB[:, :],
               start=True, stop=False,
               reads=[("V1A", 0), ("XSB",)], writes=[hk])
            mm(hb[:, :], V1A[:, 4096 + 128 * ff:4096 + 128 * ff + 128], XSQ[:, :],
               start=False, stop=True,
               reads=[("V1A", 1), ("XSQ",)], writes=[hk])

        v1mm(0)
        for ff in range(32):
            hb, hk = (P_KV, ("P_KV",)) if ff % 2 == 0 else (P_QF, ("P_QF",))
            pt, ptk = (PT0, ("PT", 0)) if ff % 2 == 0 else (PT1, ("PT", 1))
            if ff + 1 < 32:
                v1mm(ff + 1)
            P.op("scalar", lambda e, ff=ff, hb=hb, pt=pt: e.activation(
                out=pt[:, :], in_=hb[:, :], func=AF.Gelu_apprx_tanh,
                bias=B1FS[:, ff:ff + 1], scale=1.0),
                reads=[hk, ("B1FS",)], writes=[ptk])
            mm(P_S[:, :], U2A[:, 256 * ff:256 * ff + 128], pt[:, :],
               start=(ff == 0), stop=(ff == 31),
               reads=[("U2A", ff), ptk], writes=[("P_S",)])
            mm(P_Y[:, :], U2A[:, 256 * ff + 128:256 * ff + 256], pt[:, :],
               start=(ff == 0), stop=(ff == 31),
               reads=[("U2A", ff), ptk], writes=[("P_Y",)])
        vop(lambda e: e.tensor_copy(out=PT1[:, :], in_=P_S[:, :]),
            [("P_S",)], [("PT", 1)])
        vop(lambda e: e.tensor_copy(out=XSB[:, :], in_=P_Y[:, :]),
            [("P_Y",)], [("XSB",)])
        for o in range(NCH):
            mm(P_W[:, :], V2A[:, 128 * o:128 * o + 128], PT1[:, :],
               start=True, stop=False,
               reads=[("V2A", 0), ("PT", 1)], writes=[("P_W",)])
            mm(P_W[:, :], V2A[:, 1024 + 128 * o:1024 + 128 * o + 128], XSB[:, :],
               start=False, stop=True,
               reads=[("V2A", 1), ("XSB",)], writes=[("P_W",)])
            vop(lambda e, o=o, j=j: e.scalar_tensor_tensor(
                out=XSUM10[:, 512 * o:512 * (o + 1)], in0=P_W[:, :],
                scalar=B2FS[:, o:o + 1],
                in1=XB[:, T * o + 512 * j:T * o + 512 * (j + 1)],
                op0=ALU.add, op1=ALU.add),
                [("P_W",), ("B2FS",), ("XB3", o, j)], [("XSUM10", o)])
        stats_sumsq(lambda g: (XSUM10[:, 512 * g:512 * (g + 1)], [("XSUM10", g)]),
                    None, also_sum=True, via_f32=True)
        ln_finish(CONSTS[0:1, 1:2],
                  src_fn=lambda g: (XSUM10[:, 512 * g:512 * (g + 1)],
                                    [("XSUM10", g)]),
                  out_writer=lambda g, j=j: (
                      vop(lambda e: e.tensor_mul(out=BCV[:, :], in0=WS[:, :],
                                                 in1=LNB[:, 512:1024]),
                          [("WS",), ("LNB", 512)], [("BCV", "a")]),
                      dma("sync", outT.ap()[128 * g:128 * g + 128,
                                            512 * j:512 * (j + 1)], BCV[:, :],
                          reads=[("BCV", "a")], writes=[("OUT", g, j)],
                          chan="out")))
    P.op("sync", lambda e: e.nop(nofuse=True),
         reads=[("OUT", g, j) for g in range(NCH) for j in range(NT)], writes=())

    P.finalize()
    return nc


def _prep_core_inputs(core, I):
    m = {}
    m["x0T"] = np.ascontiguousarray(I["L0_attn_hidden_states"][core].T)
    m["L0A_WqTp"] = pad_qk_feat_weight(np.ascontiguousarray(I["L0_attn_Wq"].T))
    m["L0A_WkTp"] = pad_qk_feat_weight(np.ascontiguousarray(I["L0_attn_Wk"].T))
    m["L0A_WkT"] = np.ascontiguousarray(I["L0_attn_Wk"].T)
    m["L0A_WvT"] = np.ascontiguousarray(I["L0_attn_Wv"].T)
    m["L0A_WoT"] = np.ascontiguousarray(I["L0_attn_Wo"].T)
    m["L0A_g"] = np.ascontiguousarray(I["L0_attn_g_weight"].reshape(64, 1))
    ph = perm_halfsplit()
    pi = perm_interleave()
    pi4 = np.concatenate([pi[:128] - 0, pi[512:640]])
    p4 = np.empty(256, np.int64)
    for a in range(4):
        for i in range(32):
            p4[32 * a + i] = 64 * a + 2 * i
            p4[128 + 32 * a + i] = 64 * a + 2 * i + 1
    m["F_sc1"] = np.ascontiguousarray(I["L0_ffn_scale1"].reshape(8, 128).T)
    wqp = I["L0_ffn_wq"][:, ph]
    blk = wqp.reshape(8, 128, 8, 128)
    S = blk.transpose(2, 0, 1, 3).reshape(8, 2, 4, 128, 128)
    m["F_wqs"] = np.ascontiguousarray(
        S.transpose(0, 1, 3, 2, 4).reshape(8, 2, 128, 512))
    m["F_wk"] = np.ascontiguousarray(I["L0_ffn_wk"][:, ph])
    m["F_wv"] = np.ascontiguousarray(I["L0_ffn_wv"])
    m["F_wo"] = np.ascontiguousarray(I["L0_ffn_wo"])
    m["F_sc2"] = np.ascontiguousarray(I["L0_ffn_scale2"].reshape(8, 128).T)
    m["F_rw"] = np.ascontiguousarray(I["L0_ffn_router_w"])
    m["F_gum"] = _gumbel_noise()[core]
    m["F_w1"] = np.ascontiguousarray(I["L0_ffn_w1"])
    m["F_w2"] = np.ascontiguousarray(I["L0_ffn_w2"])
    m["F_wout"] = np.ascontiguousarray(I["L0_ffn_wout"])
    m["F_b1"] = np.ascontiguousarray(I["L0_ffn_b1"])
    m["F_b2"] = np.ascontiguousarray(I["L0_ffn_b2"])
    m["F_bout"] = np.ascontiguousarray(I["L0_ffn_bout"])
    m["G_n1"] = np.ascontiguousarray(I["L1_attn_norm1_w"].reshape(8, 128).T)
    m["G_wq"] = np.ascontiguousarray(I["L1_attn_g_wq"][:, pi])
    m["G_kvd"] = np.ascontiguousarray(I["L1_attn_g_kv_down"])
    m["G_kup"] = np.ascontiguousarray(I["L1_attn_g_k_up"][:, p4])
    m["G_vup"] = np.ascontiguousarray(I["L1_attn_g_v_up"])
    m["G_wo"] = np.ascontiguousarray(I["L1_attn_g_wo"])
    m["G_n2"] = np.ascontiguousarray(I["L1_attn_norm2_w"].reshape(8, 128).T)
    m["L_wq"] = np.ascontiguousarray(I["L1_attn_l_wq"][:, pi])
    m["L_wk"] = np.ascontiguousarray(I["L1_attn_l_wk"][:, p4])
    m["L_wv"] = np.ascontiguousarray(I["L1_attn_l_wv"])
    m["L_wo"] = np.ascontiguousarray(I["L1_attn_l_wo"])
    m["U1"] = np.ascontiguousarray(I["L1_ffn_U1"])
    m["V1"] = np.ascontiguousarray(I["L1_ffn_V1"])
    m["B1F"] = np.ascontiguousarray(I["L1_ffn_b1"].reshape(32, 128).T)
    m["U2"] = np.ascontiguousarray(I["L1_ffn_U2"])
    m["V2"] = np.ascontiguousarray(I["L1_ffn_V2"])
    m["B2F"] = np.ascontiguousarray(I["L1_ffn_b2"].reshape(8, 128).T)
    return {k: np.asarray(v, dtype=np.float32) for k, v in m.items()}


_GUMBEL = None


def _gumbel_noise():
    global _GUMBEL
    if _GUMBEL is None:
        import jax
        cpus = jax.devices("cpu")
        with jax.default_device(cpus[0]):
            g = jax.random.gumbel(jax.random.key(42), (8, T, 8)) * 0.05
        _GUMBEL = np.asarray(g, dtype=np.float32)
    return _GUMBEL


def kernel(**inputs):
    I = {k: np.asarray(v) for k, v in inputs.items()}
    nc = build_program(stop=os.environ.get("BASS_MK_STOP", "all"))
    in_maps = [_prep_core_inputs(c, I) for c in range(8)]
    res = run_bass_kernel_spmd(nc, in_maps, core_ids=list(range(8)))
    return np.stack([np.ascontiguousarray(r["outT"].T) for r in res.results])


# version 19 (final)
# speedup vs baseline: 1.0097x; 1.0097x over previous
import os
import numpy as np
import ml_dtypes
import concourse.bass as bass
import concourse.mybir as mybir
from concourse.bass_utils import run_bass_kernel_spmd

F32 = mybir.dt.float32
BF16 = mybir.dt.bfloat16
AF = mybir.ActivationFunctionType
ALU = mybir.AluOpType
BF = ml_dtypes.bfloat16

T = 4096
D = 1024
NCH = 8
NT = T // 512
NC = T // 128

ENGS = ("sync", "vector", "scalar", "tensor", "gpsimd")


class Prog:

    def __init__(self, nc):
        self.nc = nc
        self.sems = {e: nc.alloc_semaphore(f"S_{e}") for e in ENGS}
        self.fsems = {e: nc.alloc_semaphore(f"F_{e}") for e in ENGS}
        self.n_flush = {e: 0 for e in ENGS}
        self.dsems = {}
        self.rec = {e: [] for e in ENGS}
        self.n_ops = {e: 0 for e in ENGS}
        self.n_dma = {}
        self.need_inc = {e: set() for e in ENGS}
        self.last_wait = {}
        self.drain_mark = {e: 0 for e in ENGS}
        self.regions = {}
        self.dma_flush = {}
        self.dma_flush_tok = {}

    def _region(self, key):
        r = self.regions.get(key)
        if r is None:
            r = [None, []]
            self.regions[key] = r
        return r

    def claim(self, dst_keys, src_keys):
        toks = []
        for k in src_keys:
            s = self.regions.get(k)
            if s:
                if s[0] is not None:
                    toks.append(s[0])
                toks.extend(s[1])
        for k in dst_keys:
            self._region(k)[1].extend(toks)

    def claim_prefix(self, dst_keys, prefixes):
        src = [k for k in self.regions
               if isinstance(k, tuple) and k and k[0] in prefixes]
        self.claim(dst_keys, src)

    def _dchan(self, eng, chan):
        q = (eng, chan)
        if q not in self.dsems:
            self.dsems[q] = self.nc.alloc_semaphore(f"D_{eng}_{chan}")
            self.n_dma[q] = 0
        return q

    def _flush_dma(self, q):
        tgt = self.n_dma[q] * 16
        if self.dma_flush.get(q, 0) >= tgt:
            return self.dma_flush_tok[q]
        eng = q[0]
        self.n_flush[eng] += 1
        self.rec[eng].append(("dwaitinc", q, tgt, self.n_flush[eng]))
        tok = ("f", eng, self.n_flush[eng])
        self.dma_flush[q] = tgt
        self.dma_flush_tok[q] = tok
        return tok

    def _want(self, eng, tok):
        kind, peng, idx = tok
        if kind == "d":
            tok2 = self._flush_dma(peng)
            if eng != peng[0]:
                self._want(eng, tok2)
            return
        if kind == "f":
            if eng != peng:
                key = (eng, "f", peng)
                if self.last_wait.get(key, 0) < idx:
                    self.rec[eng].append(("fwait", peng, idx))
                    self.last_wait[key] = idx
            return
        if peng == eng:
            if eng != "tensor" and self.drain_mark[eng] < idx:
                self.rec[eng].append(("drain",))
                self.drain_mark[eng] = self.n_ops[eng]
        else:
            key = (eng, "o", peng)
            if self.last_wait.get(key, 0) < idx:
                self.rec[eng].append(("owait", peng, idx))
                self.last_wait[key] = idx
                self.need_inc[peng].add(idx)

    def scope(self, name):
        for e in ENGS:
            self.rec[e].append(("scope", name))

    def op(self, eng, fn, reads=(), writes=(), dma=False, chan=0):
        deps = []
        for k in reads:
            r = self._region(k)
            if r[0] is not None:
                deps.append(r[0])
        for k in writes:
            r = self._region(k)
            deps.extend(r[1])
            if r[0] is not None:
                deps.append(r[0])
        for t in deps:
            self._want(eng, t)
        if dma:
            q = self._dchan(eng, chan)
            self.n_dma[q] += 1
            self.rec[eng].append(("dma", fn, q))
            tok = ("d", q, self.n_dma[q] * 16)
        else:
            self.n_ops[eng] += 1
            self.rec[eng].append(("op", fn, self.n_ops[eng]))
            tok = ("o", eng, self.n_ops[eng])
        for k in reads:
            self._region(k)[1].append(tok)
        for k in writes:
            r = self._region(k)
            r[0] = tok
            r[1] = []
        return tok

    def finalize(self):
        inc_rank = {}
        for e in ENGS:
            idxs = sorted(self.need_inc[e])
            inc_rank[e] = {ix: r + 1 for r, ix in enumerate(idxs)}

        scope_state = {}

        def body(en):
            def run(eng):
                for item in self.rec[en]:
                    k = item[0]
                    if k == "scope":
                        if scope_state.get(en):
                            self.nc.pop_named_scope(scope_state[en])
                        self.nc.push_named_scope(item[1])
                        scope_state[en] = item[1]
                        continue
                    if k == "op":
                        ins = item[1](eng)
                        rank = inc_rank[en].get(item[2])
                        if rank is not None:
                            ins.then_inc(self.sems[en], 1)
                    elif k == "dma":
                        item[1](eng).then_inc(self.dsems[item[2]], 16)
                    elif k == "owait":
                        eng.wait_ge(self.sems[item[1]], inc_rank[item[1]][item[2]])
                    elif k == "dwaitinc":
                        ins = eng.wait_ge(self.dsems[item[1]], item[2])
                        ins.then_inc(self.fsems[en], 1)
                    elif k == "fwait":
                        eng.wait_ge(self.fsems[item[1]], item[2])
                    else:
                        eng.drain()
                if scope_state.get(en):
                    self.nc.pop_named_scope(scope_state[en])
                    scope_state[en] = None
            return run

        with self.nc.Block() as block:
            block.sync(body("sync"))
            block.vector(body("vector"))
            block.scalar(body("scalar"))
            block.tensor(body("tensor"))
            block.gpsimd(body("gpsimd"))


def make_sel_consts():
    selr = np.zeros((4, 128, 256), np.float32)
    selt = np.zeros((4, 128, 256), np.float32)
    for hp in range(4):
        for i in range(16):
            for j in range(16):
                selr[hp, 32 * hp + 1 + i, 16 * i + j] = 0.5
                selt[hp, 32 * hp + 1 + j, 16 * i + j] = 1.0
    return selr, selt


def pad_qk_feat_weight(wt):
    out = np.zeros((1024, 512), wt.dtype)
    for h in range(16):
        c0 = 128 * (h // 4) + 32 * (h % 4) + 1
        out[:, c0:c0 + 16] = wt[:, 16 * h:16 * h + 16]
    return out


def perm_halfsplit(nheads=16):
    p = np.empty(64 * nheads, np.int64)
    half = 32 * nheads
    for h in range(nheads):
        base = 128 * (h // 4) + 32 * (h % 4)
        for i in range(32):
            p[base + i] = 64 * h + i
            p[half + base + i] = 64 * h + 32 + i
    return p


def perm_interleave(nheads=16):
    p = np.empty(64 * nheads, np.int64)
    half = 32 * nheads
    for h in range(nheads):
        base = 128 * (h // 4) + 32 * (h % 4)
        for i in range(32):
            p[base + i] = 64 * h + 2 * i
            p[half + base + i] = 64 * h + 2 * i + 1
    return p


def rope_table(base, s=T):
    inv = 1.0 / (float(base) ** (np.arange(32, dtype=np.float64) / 32.0))
    ang = np.arange(s, dtype=np.float64)[None, :] * inv[:, None]
    cos = np.cos(ang).astype(np.float32)
    sin = np.sin(ang).astype(np.float32)
    return np.tile(cos, (4, 1)), np.tile(sin, (4, 1))


def build_program(stop="all"):
    nc = bass.Bass("TRN2", target_bir_lowering=False, dynamic_dma_scratch_size=2048)
    P = Prog(nc)

    def din(name, shape):
        return nc.dram_tensor(name, list(shape), F32, kind="ExternalInput")

    x0T = din("x0T", (D, T))
    L0A_WqTp = din("L0A_WqTp", (D, 512))
    L0A_WkTp = din("L0A_WkTp", (D, 512))
    L0A_WkT = din("L0A_WkT", (D, 256))
    L0A_WvT = din("L0A_WvT", (D, D))
    L0A_WoT = din("L0A_WoT", (D, D))
    L0A_g = din("L0A_g", (64, 1))
    F_sc1 = din("F_sc1", (128, 8))
    F_wqs = din("F_wqs", (8, 2, 128, 512))
    F_wk = din("F_wk", (D, D))
    F_wv = din("F_wv", (D, D))
    F_wo = din("F_wo", (D, D))
    F_sc2 = din("F_sc2", (128, 8))
    F_rw = din("F_rw", (D, 8))
    F_gum = din("F_gum", (T, 8))
    F_w1 = din("F_w1", (8, D, D))
    F_w2 = din("F_w2", (8, D, D))
    F_wout = din("F_wout", (8, D, D))
    F_b1 = din("F_b1", (8, D))
    F_b2 = din("F_b2", (8, D))
    F_bout = din("F_bout", (8, D))
    G_n1 = din("G_n1", (128, 8))
    G_wq = din("G_wq", (D, D))
    G_kvd = din("G_kvd", (D, 256))
    G_kup = din("G_kup", (256, 256))
    G_vup = din("G_vup", (256, 256))
    G_wo = din("G_wo", (D, D))
    G_n2 = din("G_n2", (128, 8))
    L_wq = din("L_wq", (D, D))
    L_wk = din("L_wk", (D, 256))
    L_wv = din("L_wv", (D, 256))
    L_wo = din("L_wo", (D, D))
    U1 = din("U1", (D, 256))
    V1 = din("V1", (256, 4096))
    B1F = din("B1F", (128, 32))
    U2 = din("U2", (4096, 256))
    V2 = din("V2", (256, D))
    B2F = din("B2F", (128, 8))
    outT = nc.dram_tensor("outT", [D, T], F32, kind="ExternalOutput")

    x1sp = nc.dram_tensor("x1sp", [D, T], BF16, kind="Internal")
    hsp = nc.dram_tensor("hsp", [D, T], BF16, kind="Internal")
    xmsp = nc.dram_tensor("xmsp", [D, T], BF16, kind="Internal")
    ysp = nc.dram_tensor("ysp", [D, T], BF16, kind="Internal")
    wqsb = nc.dram_tensor("wqsb", [8, 2, 128, 512], BF16, kind="Internal")
    wkb = nc.dram_tensor("wkb", [D, D], BF16, kind="Internal")
    wvb = nc.dram_tensor("wvb", [D, D], BF16, kind="Internal")
    wob = nc.dram_tensor("wob", [D, D], BF16, kind="Internal")
    w1b = nc.dram_tensor("w1b", [8, D, D], BF16, kind="Internal")
    w2b = nc.dram_tensor("w2b", [8, D, D], BF16, kind="Internal")
    woutb = nc.dram_tensor("woutb", [8, D, D], BF16, kind="Internal")

    def inl(name, arr, dtype=BF):
        return nc.inline_tensor(np.ascontiguousarray(arr.astype(dtype)), name)

    selr_np, selt_np = make_sel_consts()
    C_SELR = [inl(f"c_selr{i}", selr_np[i]) for i in range(4)]
    C_SELT = [inl(f"c_selt{i}", selt_np[i]) for i in range(4)]
    C_ONESCOL = inl("c_onescol", np.ones((128, 1), np.float32))
    C_ONESROW_F = inl("c_onesrow_f", np.ones((1, 128), np.float32), np.float32)
    C_ONESROW = inl("c_onesrow", np.ones((1, 128), np.float32))
    pat = np.zeros((1, 128), np.float32)
    pat[0, [0, 32, 64, 96]] = 1.0
    C_SLOTPAT = inl("c_slotpat", pat)
    tri = (np.arange(128)[:, None] <= np.arange(128)[None, :]).astype(np.float32)
    C_TRIU = inl("c_triu", tri)
    C_MGT = inl("c_mgt", 1.0 - tri)
    bd64t = np.zeros((2, 128), np.float32)
    bd64t[0, :64] = 1.0
    bd64t[1, 64:] = 1.0
    C_BD64T_F = inl("c_bd64t_f", bd64t, np.float32)
    C_BD64 = inl("c_bd64", bd64t.T)
    bd32t = np.zeros((4, 128), np.float32)
    for a in range(4):
        bd32t[a, 32 * a:32 * a + 32] = 1.0
    C_BD32T_F = inl("c_bd32t_f", bd32t, np.float32)
    C_BD32 = inl("c_bd32", bd32t.T)
    C_IDENT = inl("c_ident", np.eye(128, dtype=np.float32))
    C_ONESF128 = inl("c_onesf128", np.ones((128, 128), np.float32), np.float32)
    sel8 = np.zeros((8, 1024), np.float32)
    for ex_ in range(8):
        sel8[ex_, 128 * ex_:128 * (ex_ + 1)] = 1.0
    C_SEL8 = inl("c_sel8", sel8)
    cl, sl = rope_table(1e4)
    cg, sg_ = rope_table(1e6)
    C_COSL = inl("c_cosl", cl, np.float32)
    C_SINL = inl("c_sinl", sl, np.float32)
    C_COSG = inl("c_cosg", cg, np.float32)
    C_SING = inl("c_sing", sg_, np.float32)

    sbA = nc.alloc_sbuf_tensor
    ps = nc.alloc_psum_tensor

    XB = sbA("XB", [128, NCH * T], BF16)
    ARA = sbA("ARA", [128, 32768], BF16)
    ARB = sbA("ARB", [128, 33280], BF16)
    LNB = sbA("LNB", [128, 1024], F32)
    WS = sbA("WS", [128, 512], F32)
    STAT = sbA("STAT", [1, 1024], F32)
    XSB = sbA("XSB", [128, 512], BF16)
    XSQ = sbA("XSQ", [128, 512], BF16)
    PT0 = sbA("PT0", [128, 512], BF16)
    PT1 = sbA("PT1", [128, 512], BF16)
    RKBF = sbA("RKBF", [128, 1024], BF16)
    WQS = sbA("WQS", [128, 256], BF16)
    ROPC = sbA("ROPC", [128, 512], F32)
    ROPS = sbA("ROPS", [128, 512], F32)
    CONSTS = sbA("CONSTS", [128, 12], F32)
    G128 = sbA("G128", [128, 1], F32)
    SC1 = sbA("SC1", [128, 8], F32)
    SC2 = sbA("SC2", [128, 8], F32)
    N1W = sbA("N1W", [128, 8], F32)
    N2W = sbA("N2W", [128, 8], F32)
    BB1 = sbA("BB1", [128, 8], F32)
    BB2 = sbA("BB2", [128, 8], F32)
    B1FS = sbA("B1FS", [128, 32], F32)
    B2FS = sbA("B2FS", [128, 8], F32)
    ONESROW = sbA("ONESROW", [1, 128], BF16)
    ONESCOL = sbA("ONESCOL", [128, 1], BF16)
    ONESROW_F = sbA("ONESROW_F", [1, 128], F32)
    SLOTPAT = sbA("SLOTPAT", [1, 128], BF16)
    TRIU = sbA("TRIU", [128, 128], BF16)
    MGT = sbA("MGT", [128, 128], BF16)
    BD64T_F = sbA("BD64T_F", [2, 128], F32)
    BD64 = sbA("BD64", [128, 2], BF16)
    BD32T_F = sbA("BD32T_F", [4, 128], F32)
    BD32 = sbA("BD32", [128, 4], BF16)
    IDENT = sbA("IDENT", [128, 128], BF16)
    BCV = sbA("BCV", [128, 512], F32)
    ONESF128 = sbA("ONESF128", [128, 128], F32)
    GUM = sbA("GUM", [128, 8], F32)
    SEL8 = sbA("SEL8", [8, 1024], BF16)
    YSTG = XSB
    WTOK = sbA("WTOK", [128, 24], F32)
    WTOKB = sbA("WTOKB", [128, 8], BF16)

    P_QF = ps("P_QF", [128, 512], F32)
    P_KF = ps("P_KF", [128, 512], F32)
    P_KV = ps("P_KV", [128, 512], F32)
    P_V = ps("P_V", [128, 512], F32)
    P_SEL = ps("P_SEL", [128, 512], F32)
    P_S = ps("P_S", [128, 512], F32)
    P_Y = ps("P_Y", [128, 512], F32)
    P_W = ps("P_W", [128, 512], F32)

    def carve(arena, col0, cols, dtype=None):
        ap = arena[:, col0:col0 + cols]
        if dtype is not None:
            ap = ap.bitcast(dtype)
        return ap

    WTG = XB[0:8, 0:4096]
    W_QP = carve(ARA, 0, 4096)
    W_KP = carve(ARA, 4096, 4096)
    W_KT = carve(ARA, 8192, 2048)
    W_VT = carve(ARA, 10240, 8192)
    W_OT = carve(ARA, 18432, 8192)
    S2B = carve(ARA, 26624, 2048)
    YB = carve(ARA, 28672, 4096)
    S2M = carve(ARB, 0, 4096, F32)
    XSUM1 = carve(ARB, 4096, 8192, F32)
    SELR = carve(ARB, 17056, 1024)
    SELT = carve(ARB, 18080, 1024)
    SS2 = carve(ARB, 19104, 256, F32)
    PP2 = carve(ARB, 19360, 128)
    PM2 = carve(ARB, 19488, 128)
    QF22 = carve(ARB, 19616, 256)
    SB12 = carve(ARB, 19872, 256)
    KF22 = carve(ARB, 20128, 256)
    QF = carve(ARB, 12288, 512)
    KF = carve(ARB, 12800, 512)
    VTOK = carve(ARB, 13312, 1024)
    KTOKP = carve(ARB, 14336, 288)
    QF2 = carve(ARB, 14624, 256)
    SB1 = carve(ARB, 14880, 256)
    KF2 = carve(ARB, 15136, 256)
    PP = carve(ARB, 15392, 128)
    PM = carve(ARB, 15520, 128)
    SS = carve(ARB, 15648, 256, F32)
    S01M = carve(ARB, 15904, 512, F32)
    S01B = carve(ARB, 16416, 256)
    YSQ = carve(ARB, 16672, 128)
    BSB = carve(ARB, 16800, 256, F32)

    def dma(eng, dst, src, reads=(), writes=(), chan=0):
        return P.op(eng, lambda e, dst=dst, src=src: e.dma_start(out=dst, in_=src),
                    reads=reads, writes=writes, dma=True, chan=chan)

    def mm(out, lhsT, rhs, start, stop, reads, writes, tp=None):
        def fn(e, out=out, lhsT=lhsT, rhs=rhs, start=start, stop=stop, tp=tp):
            kw = {"tile_position": tp} if tp is not None else {}
            return e.matmul(out, lhsT, rhs, start=start, stop=stop, **kw)
        return P.op("tensor", fn, reads=reads, writes=writes)

    def vop(fn, reads, writes, eng="vector"):
        return P.op(eng, fn, reads=reads, writes=writes)

    for i in range(4):
        dma("sync", SELR[:, 256 * i:256 * (i + 1)], C_SELR[i].ap(), writes=[("SELR",)])
        dma("sync", SELT[:, 256 * i:256 * (i + 1)], C_SELT[i].ap(), writes=[("SELT",)])
    for dst, src, key in [
            (ONESROW, C_ONESROW, "ONESROW"), (ONESCOL, C_ONESCOL, "ONESCOL"),
            (ONESROW_F, C_ONESROW_F, "ONESROW_F"), (SLOTPAT, C_SLOTPAT, "SLOTPAT"),
            (TRIU, C_TRIU, "TRIU"), (MGT, C_MGT, "MGT"),
            (BD64T_F, C_BD64T_F, "BD64T_F"), (BD64, C_BD64, "BD64"),
            (BD32T_F, C_BD32T_F, "BD32T_F"), (BD32, C_BD32, "BD32"),
            (IDENT, C_IDENT, "IDENT"), (ONESF128, C_ONESF128, "ONESF128"),
            (SEL8, C_SEL8, "SEL8")]:
        dma("sync", dst[:, :], src.ap(), writes=[(key,)])
    dma("sync", G128[0:64, :], L0A_g.ap(), writes=[("G128",)])
    dma("sync", G128[64:128, :], L0A_g.ap(), writes=[("G128",)])
    dma("sync", SC1[:, :], F_sc1.ap(), writes=[("SC1",)])
    dma("sync", SC2[:, :], F_sc2.ap(), writes=[("SC2",)])
    dma("sync", N1W[:, :], G_n1.ap(), writes=[("N1W",)])
    dma("sync", N2W[:, :], G_n2.ap(), writes=[("N2W",)])
    dma("sync", B1FS[:, :], B1F.ap(), writes=[("B1FS",)])
    dma("sync", B2FS[:, :], B2F.ap(), writes=[("B2FS",)])
    vop(lambda e: e.memset(CONSTS[:, 0:1], 1.0), (), [("CONSTS",)])
    vop(lambda e: e.memset(CONSTS[:, 1:2], 1e-5), (), [("CONSTS",)])
    vop(lambda e: e.memset(CONSTS[:, 2:3], 1e-6), (), [("CONSTS",)])
    vop(lambda e: e.memset(CONSTS[:, 3:4], 0.125), (), [("CONSTS",)])
    vop(lambda e: e.memset(CONSTS[:, 4:5], -44.0), (), [("CONSTS",)])
    vop(lambda e: e.memset(CONSTS[:, 5:6], -8.0), (), [("CONSTS",)])
    vop(lambda e: e.memset(CONSTS[:, 6:7], 1.0 / D), (), [("CONSTS",)])
    vop(lambda e: e.memset(CONSTS[:, 7:8], 1.0 / 64), (), [("CONSTS",)])
    vop(lambda e: e.memset(CONSTS[:, 8:9], 64.0 * 1e-5), (), [("CONSTS",)])

    for g in range(NCH):
        for qq in range(8):
            wk = ("WS",)
            dma("sync", WS[:, :], x0T.ap()[128 * g:128 * g + 128,
                                           512 * qq:512 * qq + 512],
                reads=[wk], writes=[wk])
            vop(lambda e, g=g, qq=qq: e.tensor_copy(
                out=XB[:, T * g + 512 * qq:T * g + 512 * qq + 512], in_=WS[:, :]),
                [wk], [("XB", g, qq)])

    def load_w(dst_fn, src_ap, cols, name, scale_fn=None, scale_imm=None,
               cast_scale=None, nch=NCH):
        nhalf = (cols + 511) // 512
        for g in range(nch):
            for hh in range(nhalf):
                c0 = 512 * hh
                cw = min(512, cols - c0)
                dma("sync", WS[:, 0:cw], src_ap[128 * g:128 * g + 128, c0:c0 + cw],
                    reads=[("WS",)], writes=[("WS",)], chan="w")
                dst = dst_fn(g)[:, c0:c0 + cw]
                if scale_fn is not None:
                    vop(lambda e, dst=dst, cw=cw, g=g: e.tensor_scalar(
                        out=dst, in0=WS[:, 0:cw], scalar1=scale_fn(g),
                        scalar2=scale_imm, op0=ALU.mult,
                        op1=ALU.mult if scale_imm is not None else ALU.bypass),
                        [("WS",), ("G128",), ("SC1",), ("SC2",), ("N1W",), ("N2W",)],
                        [(name, g)])
                elif cast_scale is not None:
                    vop(lambda e, dst=dst, cw=cw: e.tensor_scalar(
                        out=dst, in0=WS[:, 0:cw], scalar1=cast_scale, scalar2=None,
                        op0=ALU.mult), [("WS",)], [(name, g)])
                else:
                    vop(lambda e, dst=dst, cw=cw: e.tensor_copy(
                        out=dst, in_=WS[:, 0:cw]), [("WS",)], [(name, g)])

    PCPEND = []
    PCI = [0]

    def pc_add(pcname, src_ap, dst_dram_ap, rows, cols, ex=None):
        for g in range(rows // 128):
            for hh in range((cols + 511) // 512):
                c0 = 512 * hh
                cw = min(512, cols - c0)
                key = (("PCW", pcname, g, hh) if ex is None
                       else ("PCW", pcname, ex, g, hh))
                for q0 in range(0, cw, 256):
                    qw = min(256, cw - q0)
                    PCPEND.append(
                        (src_ap[128 * g:128 * g + 128, c0 + q0:c0 + q0 + qw],
                         dst_dram_ap[128 * g:128 * g + 128, c0 + q0:c0 + q0 + qw],
                         key, qw))

    def pump(n, w):
        for _ in range(n):
            if not PCPEND:
                return
            src, dst, key, qw = PCPEND.pop(0)
            i = PCI[0]
            PCI[0] += 1
            if w == "p1":
                fs, fk = ((PT0[:, :].bitcast(F32), ("PT", 0)) if i % 2 == 0
                          else (PT1[:, :].bitcast(F32), ("PT", 1)))
            else:
                fs, fk = ((WS[:, 0:256], ("WS",)) if i % 2 == 0
                          else (WS[:, 256:512], ("WSB",)))
            os_, ok = ((XSQ[:, 0:256], ("XSQ",)) if i % 2 == 0
                       else (XSQ[:, 256:512], ("XSQB",)))
            dma("sync", fs[:, 0:qw], src, reads=[fk], writes=[fk], chan="pc")
            P.op("gpsimd", lambda e, fs=fs, os_=os_, qw=qw: e.tensor_copy(
                out=os_[:, 0:qw], in_=fs[:, 0:qw]), reads=[fk, ok], writes=[ok])
            dma("sync", dst, os_[:, 0:qw], reads=[ok], writes=[key],
                chan="pc2")

    def load_w_b16(dst_fn, src_ap, cols, name, pcname, ex=None, nch=NCH):
        nhalf = (cols + 511) // 512
        for g in range(nch):
            for hh in range(nhalf):
                c0 = 512 * hh
                cw = min(512, cols - c0)
                key = (("PCW", pcname, g, hh) if ex is None
                       else ("PCW", pcname, ex, g, hh))
                dma("sync", dst_fn(g)[:, c0:c0 + cw],
                    src_ap[128 * g:128 * g + 128, c0:c0 + cw],
                    reads=[key, (name, g)], writes=[(name, g)],
                    chan="w" if (g + hh) % 2 == 0 else "w2")

    load_w(lambda g: W_QP[:, 512 * g:512 * (g + 1)], L0A_WqTp.ap(), 512, "W_QP",
           cast_scale=0.5)
    load_w(lambda g: W_KP[:, 512 * g:512 * (g + 1)], L0A_WkTp.ap(), 512, "W_KP",
           cast_scale=0.5)
    load_w(lambda g: W_KT[:, 256 * g:256 * (g + 1)], L0A_WkT.ap(), 256, "W_KT",
           cast_scale=0.5)
    load_w(lambda g: W_VT[:, 1024 * g:1024 * (g + 1)], L0A_WvT.ap(), 1024, "W_VT")
    load_w(lambda g: W_OT[:, 1024 * g:1024 * (g + 1)], L0A_WoT.ap(), 1024, "W_OT",
           scale_fn=lambda g: G128[:, 0:1], scale_imm=8.0)

    vop(lambda e: e.memset(S2M[:, :], 0.0), (),
        [("S2M", hh) for hh in range(16)])
    vop(lambda e: e.memset(S01M[:, :], 0.0), (),
        [("S01M", hh) for hh in range(16)])
    P.op("gpsimd", lambda e: e.memset(KTOKP[:, :], 1.0), reads=(),
         writes=[("KTOKP", "ones"), ("KTOKP",)])

    def stats_sumsq(src_f32_ap_fn, src_keys, n=NCH, via_f32=True, also_sum=False):
        for g in range(n):
            ap, keys = src_f32_ap_fn(g)
            if also_sum:
                if via_f32:
                    vop(lambda e, ap=ap: e.tensor_copy(out=XSB[:, :], in_=ap),
                        keys, [("XSB",)])
                    sap = XSB[:, :]
                    skey = [("XSB",)]
                else:
                    sap, skey = ap, keys
                mm(P_QF[0:1, 0:512], ONESCOL[:, :], sap,
                   start=(g == 0), stop=(g == n - 1),
                   reads=skey + [("ONESCOL",)], writes=[("P_QF",)])
            P.op("scalar", lambda e, ap=ap: e.activation(
                out=XSQ[:, :], in_=ap, func=AF.Square),
                reads=keys, writes=[("XSQ",)])
            mm(P_KF[0:1, 0:512], ONESCOL[:, :], XSQ[:, :],
               start=(g == 0), stop=(g == n - 1),
               reads=[("XSQ",), ("ONESCOL",)], writes=[("P_KF",)])

    def bcast_f32(stat_slice, stat_key, pbank, pkey, lnb_off):
        mm(pbank[:, :], ONESROW_F[:, :], stat_slice, start=True, stop=True,
           reads=[stat_key, ("ONESROW_F",)], writes=[pkey])
        P.op("scalar", lambda e: e.copy(out=LNB[:, lnb_off:lnb_off + 512],
                                        in_=pbank[:, :]),
             reads=[pkey], writes=[("LNB", lnb_off)])

    def ln_finish(eps_col, out_writer, src_fn, n=NCH):
        vop(lambda e: e.tensor_scalar(out=STAT[:, 0:512], in0=P_QF[0:1, 0:512],
                                      scalar1=1.0 / D, scalar2=None, op0=ALU.mult),
            [("P_QF",)], [("STAT", "m")])
        vop(lambda e: e.tensor_scalar(out=STAT[:, 512:1024], in0=P_KF[0:1, 0:512],
                                      scalar1=1.0 / D, scalar2=None, op0=ALU.mult),
            [("P_KF",)], [("STAT", "A")])
        vop(lambda e: e.tensor_mul(out=BCV[0:1, 0:512], in0=STAT[:, 0:512],
                                   in1=STAT[:, 0:512]),
            [("STAT", "m")], [("BCV", "a")])
        vop(lambda e: e.tensor_sub(out=STAT[:, 512:1024], in0=STAT[:, 512:1024],
                                   in1=BCV[0:1, 0:512]),
            [("STAT", "A"), ("BCV", "a")], [("STAT", "A")])
        P.op("scalar", lambda e: e.activation(
            out=BCV[0:1, 0:512], in_=STAT[:, 512:1024], func=AF.Sqrt,
            bias=eps_col, scale=1.0),
            reads=[("STAT", "A"), ("CONSTS",)], writes=[("BCV", "a")])
        vop(lambda e: e.reciprocal(out=STAT[:, 512:1024], in_=BCV[0:1, 0:512]),
            [("BCV", "a")], [("STAT", "A")])
        bcast_f32(STAT[0:1, 0:512], ("STAT", "m"), P_KV, ("P_KV",), 0)
        bcast_f32(STAT[0:1, 512:1024], ("STAT", "A"), P_V, ("P_V",), 512)
        for g in range(n):
            ap, keys = src_fn(g)
            vop(lambda e, ap=ap: e.tensor_sub(out=WS[:, :], in0=ap,
                                              in1=LNB[:, 0:512]),
                keys + [("LNB", 0)], [("WS",)])
            out_writer(g)

    def rms_r(eps_col, n=NCH, scale=1.0 / D):
        vop(lambda e: e.tensor_scalar(out=STAT[:, 512:1024], in0=P_KF[0:1, 0:512],
                                      scalar1=scale, scalar2=None, op0=ALU.mult),
            [("P_KF",)], [("STAT", "A")])
        P.op("scalar", lambda e: e.activation(
            out=BCV[0:1, 0:512], in_=STAT[:, 512:1024], func=AF.Sqrt,
            bias=eps_col, scale=1.0),
            reads=[("STAT", "A"), ("CONSTS",)], writes=[("BCV", "a")])
        vop(lambda e: e.reciprocal(out=STAT[:, 512:1024], in_=BCV[0:1, 0:512]),
            [("BCV", "a")], [("STAT", "A")])
        bcast_f32(STAT[0:1, 512:1024], ("STAT", "A"), P_V, ("P_V",), 512)

    for _wb in range(8):
        for _h2 in range(2):
            for _q0 in (0, 256):
                PCPEND.append((F_wqs.ap()[_wb, _h2][:, _q0:_q0 + 256],
                               wqsb.ap()[_wb, _h2][:, _q0:_q0 + 256],
                               ("PCW", "wqs", _wb, _h2), 256))
    pc_add("wk", F_wk.ap(), wkb.ap(), D, D)
    pc_add("wv", F_wv.ap(), wvb.ap(), D, D)
    pc_add("wo", F_wo.ap(), wob.ap(), D, D)

    P.scope("P1_l0attn")
    EPS_RMS0 = 1e-5
    for j in range(NT):
        for g in range(NCH):
            dma("scalar", XSUM1[:, 512 * g:512 * (g + 1)],
                x0T.ap()[128 * g:128 * g + 128, 512 * j:512 * (j + 1)],
                reads=[("XSUM1", g)], writes=[("XSUM1", g)], chan="x0")
        for cc in range(4):
            c = 4 * j + cc
            t0 = 128 * c
            xkeys = [("XB", g, c // 4) for g in range(NCH)]
            P.op("gpsimd", lambda e: e.tensor_copy(out=S2B[:, :], in_=S2M[:, :]),
                 reads=[("S2M", hh) for hh in range(16)], writes=[("S2B",)])
            P.op("gpsimd", lambda e: e.tensor_copy(out=S01B[:, :], in_=S01M[:, :]),
                 reads=[("S01M", hh) for hh in range(16)], writes=[("S01B",)])
            for m in range(4):
                for g in range(NCH):
                    mm(P_QF[:, 128 * m:128 * (m + 1)],
                       W_QP[:, 512 * g + 128 * m:512 * g + 128 * m + 128],
                       XB[:, T * g + t0:T * g + t0 + 128],
                       start=(g == 0), stop=False,
                       reads=[("W_QP", g), xkeys[g]], writes=[("P_QF",)])
                mm(P_QF[:, 128 * m:128 * (m + 1)], SLOTPAT[:, :], ONESROW[:, :],
                   start=False, stop=True,
                   reads=[("SLOTPAT",), ("ONESROW",)], writes=[("P_QF",)])
                for g in range(NCH):
                    mm(P_KF[:, 128 * m:128 * (m + 1)],
                       W_KP[:, 512 * g + 128 * m:512 * g + 128 * m + 128],
                       XB[:, T * g + t0:T * g + t0 + 128],
                       start=(g == 0), stop=(g == NCH - 1),
                       reads=[("W_KP", g), xkeys[g]], writes=[("P_KF",)])
            for g in range(NCH):
                mm(P_KV[:, 0:256], XB[:, T * g + t0:T * g + t0 + 128],
                   W_KT[:, 256 * g:256 * (g + 1)],
                   start=(g == 0), stop=(g == NCH - 1),
                   reads=[("W_KT", g), xkeys[g]], writes=[("P_KV",)])
            for hf in range(2):
                for g in range(NCH):
                    mm(P_V[:, :], XB[:, T * g + t0:T * g + t0 + 128],
                       W_VT[:, 1024 * g + 512 * hf:1024 * g + 512 * hf + 512],
                       start=(g == 0), stop=(g == NCH - 1),
                       reads=[("W_VT", g), xkeys[g]], writes=[("P_V",)])
                vop(lambda e, hf=hf: e.tensor_copy(
                    out=VTOK[:, 512 * hf:512 * (hf + 1)], in_=P_V[:, :]),
                    [("P_V",)], [("VTOK",)])
            vop(lambda e: e.tensor_copy(out=QF[:, :], in_=P_QF[:, :]),
                [("P_QF",)], [("QF",)])
            vop(lambda e: e.tensor_copy(out=KF[:, :], in_=P_KF[:, :]),
                [("P_KF",)], [("KF",)])
            vop(lambda e: e.tensor_copy(
                out=KTOKP[:, :].rearrange("p (h s) -> p h s", s=18)[:, :, 1:17],
                in_=P_KV[:, 0:256].rearrange("p (h s) -> p h s", s=16)),
                [("P_KV",)], [("KTOKP",)])

            def hctx(h):
                par = h % 2
                return dict(
                    h=h, tl=h // 4, b=32 * (h % 4), yb=64 * par, par=par,
                    SSp=(SS, ("SS",)) if par == 0 else (SS2, ("SS2",)),
                    PPp=(PP, ("PP",)) if par == 0 else (PP2, ("PP2",)),
                    PMp=(PM, ("PM",)) if par == 0 else (PM2, ("PM2",)),
                    QF2p=(QF2, ("QF2",)) if par == 0 else (QF22, ("QF22",)),
                    SB1p=(SB1, ("SB1",)) if par == 0 else (SB12, ("SB12",)),
                    KF2p=(KF2, ("KF2",)) if par == 0 else (KF22, ("KF22",)),
                    sB=(P_S, ("P_S",)) if par == 0 else (P_KV, ("P_KV",)),
                    uB=(P_W, ("P_W",)) if par == 0 else (P_V, ("P_V",)),
                    selB=(P_SEL, ("P_SEL",)) if par == 0 else (P_QF, ("P_QF",)),
                    yB=(P_Y, ("P_Y",)) if (h // 2) % 2 == 0 else (P_KF, ("P_KF",)))

            def head_front(h):
                cx = hctx(h)
                tl, b = cx["tl"], cx["b"]
                sB, sbk = cx["sB"]
                SSp, ssk = cx["SSp"]
                PPp, ppk = cx["PPp"]
                PMp, pmk = cx["PMp"]
                selB, selk = cx["selB"]
                SB1p, sb1k = cx["SB1p"]
                QF2p, qf2k = cx["QF2p"]
                KF2p, kf2k = cx["KF2p"]
                qtile = QF[:, 128 * tl:128 * (tl + 1)]
                mm(sB[0:128, 0:128], KF[b:b + 17, 128 * tl:128 * tl + 128],
                   QF[b:b + 17, 128 * tl:128 * tl + 128], start=True, stop=True,
                   reads=[("KF",), ("QF",)], writes=[sbk], tp=(b, 0))
                P.op("scalar", lambda e, sB=sB, SSp=SSp: e.activation(
                    out=SSp[:, :], in_=sB[0:128, 0:128], func=AF.Square,
                    bias=CONSTS[:, 0:1], scale=1.0),
                    reads=[sbk, ("CONSTS",)], writes=[ssk])
                P.op("gpsimd", lambda e, SSp=SSp, PPp=PPp: e.tensor_scalar(
                    out=PPp[:, :], in0=SSp[:, :], scalar1=0.5, scalar2=0.5,
                    op0=ALU.mult, op1=ALU.add), reads=[ssk], writes=[ppk])
                P.op("gpsimd", lambda e, PPp=PPp, PMp=PMp: e.tensor_mul(
                    out=PMp[:, :], in0=PPp[:, :], in1=TRIU[:, :]),
                    reads=[ppk, ("TRIU",)], writes=[pmk])
                for ch in range(2):
                    mm(selB[:, 128 * ch:128 * (ch + 1)],
                       SELR[:, 256 * (h % 4) + 128 * ch:256 * (h % 4) + 128 * (ch + 1)],
                       qtile, start=True, stop=True,
                       reads=[("SELR",), ("QF",)], writes=[selk])
                    mm(selB[:, 256 + 128 * ch:256 + 128 * (ch + 1)],
                       SELT[:, 256 * (h % 4) + 128 * ch:256 * (h % 4) + 128 * (ch + 1)],
                       qtile, start=True, stop=True,
                       reads=[("SELT",), ("QF",)], writes=[selk])
                P.op("scalar", lambda e, selB=selB, SB1p=SB1p: e.copy(
                    out=SB1p[:, :], in_=selB[:, 0:256]),
                    reads=[selk], writes=[sb1k])
                vop(lambda e, selB=selB, SB1p=SB1p, QF2p=QF2p: e.tensor_mul(
                    out=QF2p[:, :], in0=selB[:, 256:512], in1=SB1p[:, :]),
                    [selk, sb1k], [qf2k])
                kslot = KTOKP[:, :].rearrange("p (hh s) -> p hh s", s=18)[:, h, 1:17]
                P.op("gpsimd", lambda e, kslot=kslot, KF2p=KF2p: e.tensor_mul(
                    out=KF2p[:, :].rearrange("p (i jj) -> p i jj", i=16),
                    in0=kslot.unsqueeze(2).broadcast_to((128, 16, 16)),
                    in1=kslot.unsqueeze(1).broadcast_to((128, 16, 16))),
                    reads=[("KTOKP",), ("KTOKP", "ones")], writes=[kf2k])

            def head_back(h, cc):
                cx = hctx(h)
                tl, b, yb = cx["tl"], cx["b"], cx["yb"]
                PMp, pmk = cx["PMp"]
                QF2p, qf2k = cx["QF2p"]
                KF2p, kf2k = cx["KF2p"]
                uB, ubk = cx["uB"]
                yB, ybk = cx["yB"]
                mm(uB[0:17, 0:64], KTOKP[:, 18 * h:18 * h + 17],
                   VTOK[:, 64 * h:64 * h + 64], start=True, stop=True,
                   reads=[("KTOKP",), ("KTOKP", "ones"), ("VTOK",)],
                   writes=[ubk])
                mm(yB[yb:yb + 64, 0:128], VTOK[:, 64 * h:64 * h + 64], PMp[:, :],
                   start=True, stop=False,
                   reads=[("VTOK",), pmk], writes=[ybk], tp=(0, yb))
                mm(yB[yb:yb + 64, 0:128], S01B[b:b + 17, 64 * tl:64 * tl + 64],
                   QF[b:b + 17, 128 * tl:128 * tl + 128], start=False, stop=False,
                   reads=[("S01B",), ("QF",)], writes=[ybk], tp=(b, yb))
                for ch in range(2):
                    mm(uB[:, 64 + 64 * ch:64 + 64 * (ch + 1)],
                       KF2p[:, 128 * ch:128 * (ch + 1)],
                       VTOK[:, 64 * h:64 * h + 64], start=True, stop=True,
                       reads=[kf2k, ("VTOK",)], writes=[ubk])
                for ch in range(2):
                    mm(yB[yb:yb + 64, 0:128],
                       S2B[:, 64 * (2 * h + ch):64 * (2 * h + ch) + 64],
                       QF2p[:, 128 * ch:128 * (ch + 1)],
                       start=False, stop=(ch == 1),
                       reads=[("S2B",), qf2k], writes=[ybk], tp=(0, yb))
                vop(lambda e, b=b, tl=tl, uB=uB: e.scalar_tensor_tensor(
                    out=S01M[b:b + 17, 64 * tl:64 * tl + 64], in0=uB[0:17, 0:64],
                    scalar=1.0, in1=S01M[b:b + 17, 64 * tl:64 * tl + 64],
                    op0=ALU.mult, op1=ALU.add),
                    [ubk], [("S01M", h)])
                for ch in range(2):
                    cix = 64 * (2 * h + ch)
                    vop(lambda e, cix=cix, ch=ch, uB=uB: e.scalar_tensor_tensor(
                        out=S2M[:, cix:cix + 64],
                        in0=uB[:, 64 + 64 * ch:64 + 64 * (ch + 1)],
                        scalar=1.0, in1=S2M[:, cix:cix + 64],
                        op0=ALU.mult, op1=ALU.add),
                        [ubk], [("S2M", h)])

                if h % 2 == 1:
                    gp = h // 2
                    P.op("scalar", lambda e, yB=yB: e.activation(
                        out=YSQ[:, :], in_=yB[:, 0:128], func=AF.Square),
                        reads=[ybk], writes=[("YSQ",)])
                    mm(yB[0:2, 256:384], BD64[:, :], YSQ[:, :], start=True,
                       stop=True, reads=[("YSQ",), ("BD64",)], writes=[ybk])
                    vop(lambda e, yB=yB: e.tensor_scalar(
                        out=BCV[0:2, 0:128], in0=yB[0:2, 256:384],
                        scalar1=64.0 * EPS_RMS0, scalar2=None, op0=ALU.add),
                        [ybk], [("BCV", "a")])
                    P.op("scalar", lambda e: e.activation(
                        out=WS[0:2, 0:128], in_=BCV[0:2, 0:128], func=AF.Sqrt),
                        reads=[("BCV", "a")], writes=[("WS",)])
                    vop(lambda e: e.reciprocal(out=BCV[0:2, 128:256],
                                               in_=WS[0:2, 0:128]),
                        [("WS",)], [("BCV", "a")])
                    mm(yB[:, 384:512], BD64T_F[:, :], BCV[0:2, 128:256],
                       start=True, stop=True,
                       reads=[("BCV", "a"), ("BD64T_F",)], writes=[ybk])
                    P.op("scalar", lambda e, yB=yB: e.copy(out=BSB[:, :],
                                                           in_=yB[:, 384:512]),
                         reads=[ybk], writes=[("BSB",)])
                    vop(lambda e, gp=gp, cc=cc, yB=yB: e.tensor_mul(
                        out=YB[:, 512 * gp + 128 * cc:512 * gp + 128 * cc + 128],
                        in0=yB[:, 0:128], in1=BSB[:, :]),
                        [ybk, ("BSB",)], [("YB", gp, cc)])

            head_front(0)
            head_front(1)
            for h in range(16):
                head_back(h, cc)
                if h + 2 < 16:
                    head_front(h + 2)
                if h % 3 == 2:
                    pump(1, "p1")
            pump(1, "p1")

        for o in range(NCH):
            for g in range(NCH):
                mm(P_W[:, :], W_OT[:, 1024 * g + 128 * o:1024 * g + 128 * o + 128],
                   YB[:, 512 * g:512 * (g + 1)],
                   start=(g == 0), stop=(g == NCH - 1),
                   reads=[("W_OT", g)] + [("YB", g, c4) for c4 in range(4)],
                   writes=[("P_W",)])
            vop(lambda e, o=o: e.scalar_tensor_tensor(
                out=XSUM1[:, 512 * o:512 * (o + 1)], in0=P_W[:, :], scalar=1.0,
                in1=XSUM1[:, 512 * o:512 * (o + 1)], op0=ALU.mult, op1=ALU.add),
                [("P_W",)], [("XSUM1", o)])
        stats_sumsq(lambda g: (XSUM1[:, 512 * g:512 * (g + 1)], [("XSUM1", g)]),
                    None, also_sum=True)
        ln_finish(CONSTS[0:1, 1:2],
                  out_writer=lambda g, j=j: vop(
                      lambda e, g=g: e.tensor_mul(
                          out=XB[:, T * g + 512 * j:T * g + 512 * (j + 1)],
                          in0=WS[:, :], in1=LNB[:, 512:1024]),
                      [("WS",), ("LNB", 512)], [("XB", g, j)]),
                  src_fn=lambda g: (XSUM1[:, 512 * g:512 * (g + 1)],
                                    [("XSUM1", g)]))

    def debug_stream_out(keys_fn):
        for g in range(NCH):
            for q in range(8):
                vop(lambda e, g=g, q=q: e.tensor_copy(
                    out=WS[:, :], in_=XB[:, T * g + 512 * q:T * g + 512 * (q + 1)]),
                    keys_fn(g, q) + [("WS",)], [("WS",)])
                dma("sync", outT.ap()[128 * g:128 * g + 128, 512 * q:512 * (q + 1)],
                    WS[:, :], reads=[("WS",)], writes=[("OUT", g, q)], chan="out")
        P.op("sync", lambda e: e.nop(nofuse=True),
             reads=[("OUT", g, q) for g in range(NCH) for q in range(8)], writes=())

    def debug_hbm_out(src_sp, prefix):
        for g in range(NCH):
            for q in range(8):
                dma("sync", XSB[:, :],
                    src_sp.ap()[128 * g:128 * g + 128, 512 * q:512 * (q + 1)],
                    reads=[(prefix, g, q), ("XSB",)], writes=[("XSB",)], chan="dbg")
                vop(lambda e: e.tensor_copy(out=WS[:, :], in_=XSB[:, :]),
                    [("XSB",), ("WS",)], [("WS",)])
                dma("sync", outT.ap()[128 * g:128 * g + 128, 512 * q:512 * (q + 1)],
                    WS[:, :], reads=[("WS",)], writes=[("OUT", g, q)], chan="out")
        P.op("sync", lambda e: e.nop(nofuse=True),
             reads=[("OUT", g, q) for g in range(NCH) for q in range(8)], writes=())

    if stop == "x1":
        debug_stream_out(lambda g, q: [("XB", g, q)])
        P.finalize()
        return nc

    assert not PCPEND, f"group A pre-casts not drained: {len(PCPEND)}"
    P.scope("P2_rms1")
    P.claim_prefix([("XSQ",)], {"XSQB"})
    RMS1X = ARA[:, :]
    P.claim([("RMS1X", g, j) for g in range(NCH) for j in range(NT)],
            [("W_QP", g) for g in range(NCH)] + [("W_KP", g) for g in range(NCH)]
            + [("W_KT", g) for g in range(NCH)] + [("W_VT", g) for g in range(NCH)]
            + [("W_OT", g) for g in range(NCH)] + [("S2B",)]
            + [("YB", g, c) for g in range(NCH) for c in range(4)])
    for j in range(NT):
        for g in range(NCH):
            dma("scalar", x1sp.ap()[128 * g:128 * g + 128, 512 * j:512 * (j + 1)],
                XB[:, T * g + 512 * j:T * g + 512 * (j + 1)],
                reads=[("XB", g, j)], writes=[("x1sp", g, j)], chan="sp")
        stats_sumsq(lambda g, j=j: (XB[:, T * g + 512 * j:T * g + 512 * (j + 1)],
                                    [("XB", g, j)]), None, via_f32=False)
        rms_r(CONSTS[0:1, 2:3])
        for g in range(NCH):
            vop(lambda e, g=g, j=j: e.scalar_tensor_tensor(
                out=RMS1X[:, T * g + 512 * j:T * g + 512 * (j + 1)],
                in0=XB[:, T * g + 512 * j:T * g + 512 * (j + 1)],
                scalar=SC1[:, g:g + 1], in1=LNB[:, 512:1024],
                op0=ALU.mult, op1=ALU.mult),
                [("XB", g, j), ("SC1",), ("LNB", 512)], [("RMS1X", g, j)])


    BANKS = [P_QF, P_KF, P_KV, P_V, P_SEL, P_S, P_Y, P_W]
    BKEYS = [("P_QF",), ("P_KF",), ("P_KV",), ("P_V",), ("P_SEL",), ("P_S",),
             ("P_Y",), ("P_W",)]

    def _rope_pair(evbank, evkey, odbank, odkey):
        vop(lambda e: e.scalar_tensor_tensor(
            out=XSB[:, :], in0=evbank[:, :], scalar=1.0, in1=ROPC[:, :],
            op0=ALU.mult, op1=ALU.mult), [evkey, ("ROPC",)], [("XSB",)])
        vop(lambda e: e.scalar_tensor_tensor(
            out=XSQ[:, :], in0=odbank[:, :], scalar=1.0, in1=ROPS[:, :],
            op0=ALU.mult, op1=ALU.mult), [odkey, ("ROPS",)], [("XSQ",)])
        vop(lambda e: e.tensor_sub(out=RKBF[:, 0:512], in0=XSB[:, :],
                                   in1=XSQ[:, :]),
            [("XSB",), ("XSQ",)], [("RKBF", 0)])
        vop(lambda e: e.scalar_tensor_tensor(
            out=XSB[:, :], in0=evbank[:, :], scalar=1.0, in1=ROPS[:, :],
            op0=ALU.mult, op1=ALU.mult), [evkey, ("ROPS",)], [("XSB",)])
        vop(lambda e: e.scalar_tensor_tensor(
            out=XSQ[:, :], in0=odbank[:, :], scalar=1.0, in1=ROPC[:, :],
            op0=ALU.mult, op1=ALU.mult), [odkey, ("ROPC",)], [("XSQ",)])
        vop(lambda e: e.tensor_add(out=RKBF[:, 512:1024], in0=XSB[:, :],
                                   in1=XSQ[:, :]),
            [("XSB",), ("XSQ",)], [("RKBF", 1)])

    def _qknorm_rkbf(stat_bank, stat_key, bc_bank, bc_key):
        for part in range(2):
            P.op("scalar", lambda e, part=part: e.activation(
                out=XSQ[:, :], in_=RKBF[:, 512 * part:512 * part + 512],
                func=AF.Square),
                reads=[("RKBF", part)], writes=[("XSQ",)])
            mm(stat_bank[0:4, 0:512], BD32[:, :], XSQ[:, :],
               start=(part == 0), stop=(part == 1),
               reads=[("XSQ",), ("BD32",)], writes=[stat_key])
        vop(lambda e: e.tensor_scalar(
            out=BCV[0:4, 0:512], in0=stat_bank[0:4, 0:512], scalar1=1.0 / 64,
            scalar2=None, op0=ALU.mult), [stat_key], [("BCV", "a")])
        P.op("scalar", lambda e: e.activation(
            out=WS[0:4, 0:512], in_=BCV[0:4, 0:512], func=AF.Sqrt,
            bias=CONSTS[0:4, 2:3], scale=1.0),
            reads=[("BCV", "a"), ("CONSTS",)], writes=[("WS",)])
        vop(lambda e: e.reciprocal(out=BCV[0:4, 0:512], in_=WS[0:4, 0:512]),
            [("WS",)], [("BCV", "a")])
        mm(bc_bank[:, :], BD32T_F[:, :], BCV[0:4, 0:512], start=True, stop=True,
           reads=[("BCV", "a"), ("BD32T_F",)], writes=[bc_key])
        P.op("scalar", lambda e, bc_bank=bc_bank: e.copy(out=BCV[:, :],
                                                         in_=bc_bank[:, :]),
             reads=[bc_key], writes=[("BCV", "a")])
        for part in range(2):
            vop(lambda e, part=part: e.tensor_mul(
                out=RKBF[:, 512 * part:512 * part + 512],
                in0=RKBF[:, 512 * part:512 * part + 512], in1=BCV[:, :]),
                [("RKBF", part), ("BCV", "a")], [("RKBF", part)])

    def rope_proj_pass(src_arena, src_key_fn, nch, wslot_fn, wname, npair,
                       ctab, stab, qknorm, cache_write_fn):
        def tab(j):
            dma("sync", ROPC[:, :], ctab.ap()[:, 512 * j:512 * (j + 1)],
                reads=[("ROPC",)], writes=[("ROPC",)], chan="w")
            dma("sync", ROPS[:, :], stab.ap()[:, 512 * j:512 * (j + 1)],
                reads=[("ROPS",)], writes=[("ROPS",)], chan="w")

        def proj(j, pr):
            for part, pbank, pkey in ((0, P_QF, ("P_QF",)), (1, P_KF, ("P_KF",))):
                for g in range(nch):
                    mm(pbank[:, :], wslot_fn(g, part, pr),
                       src_arena[:, T * g + 512 * j:T * g + 512 * (j + 1)],
                       start=(g == 0), stop=(g == nch - 1),
                       reads=[(wname, g)] + src_key_fn(g, j), writes=[pkey])

        seq = [(j, pr) for j in range(NT) for pr in range(npair)]
        tab(0)
        proj(0, 0)
        for idx, (j, pr) in enumerate(seq):
            _rope_pair(P_QF, ("P_QF",), P_KF, ("P_KF",))
            if idx + 1 < len(seq):
                jn, prn = seq[idx + 1]
                if prn == 0:
                    tab(jn)
                proj(jn, prn)
            if qknorm:
                _qknorm_rkbf(P_S, ("P_S",), P_W, ("P_W",))
            for part in range(2):
                cache_write_fn(pr, part, j,
                               RKBF[:, 512 * part:512 * part + 512],
                               [("RKBF", part)])

    def rope_proj_psum(src_arena, src_key_fn, nch, wslot_fn, wname, npair,
                       ctab, stab, qknorm, cache_write_fn):
        def tab(j):
            dma("sync", ROPC[:, :], ctab.ap()[:, 512 * j:512 * (j + 1)],
                reads=[("ROPC",)], writes=[("ROPC",)], chan="w")
            dma("sync", ROPS[:, :], stab.ap()[:, 512 * j:512 * (j + 1)],
                reads=[("ROPS",)], writes=[("ROPS",)], chan="w")

        def proj(j, pr):
            for part in range(2):
                bi = 2 * pr + part
                for g in range(nch):
                    mm(BANKS[bi][:, :], wslot_fn(g, part, pr),
                       src_arena[:, T * g + 512 * j:T * g + 512 * (j + 1)],
                       start=(g == 0), stop=(g == nch - 1),
                       reads=[(wname, g)] + src_key_fn(g, j),
                       writes=[BKEYS[bi]])

        tab(0)
        for pr in range(npair):
            proj(0, pr)
        for j in range(NT):
            for pr in range(npair):
                _rope_pair(BANKS[2 * pr], BKEYS[2 * pr],
                           BANKS[2 * pr + 1], BKEYS[2 * pr + 1])
                if qknorm:
                    _qknorm_rkbf(BANKS[2 * pr], BKEYS[2 * pr],
                                 BANKS[2 * pr + 1], BKEYS[2 * pr + 1])
                if pr == npair - 1 and j + 1 < NT:
                    tab(j + 1)
                if j + 1 < NT:
                    proj(j + 1, pr)
                for part in range(2):
                    cache_write_fn(pr, part, j,
                                   RKBF[:, 512 * part:512 * part + 512],
                                   [("RKBF", part)])

    def rope_proj_psum_stream(src_arena, src_key_fn, nch, w_stream,
                              npair, ctab, stab, qknorm, cache_write_fn):
        sl = 0
        for j in range(NT):
            dma("sync", ROPC[:, :], ctab.ap()[:, 512 * j:512 * (j + 1)],
                reads=[("ROPC",)], writes=[("ROPC",)], chan="w")
            dma("sync", ROPS[:, :], stab.ap()[:, 512 * j:512 * (j + 1)],
                reads=[("ROPS",)], writes=[("ROPS",)], chan="w")
            for pr in range(npair):
                for part in range(2):
                    bi = 2 * pr + part
                    wb = npair * part + pr
                    for h2 in range(2):
                        par = sl % 2
                        sl += 1
                        PTs, ptk = ((PT0, ("PT", 0)) if par == 0
                                    else (PT1, ("PT", 1)))
                        dma("sync", PTs[:, :], w_stream.ap()[wb, h2],
                            reads=[("PCW", "wqs", wb, h2), ptk], writes=[ptk],
                            chan="w" if par == 0 else "w2")
                        for i in range(4):
                            g = 4 * h2 + i
                            mm(BANKS[bi][:, :], PTs[:, 128 * i:128 * i + 128],
                               src_arena[:, T * g + 512 * j:T * g + 512 * (j + 1)],
                               start=(g == 0), stop=(g == nch - 1),
                               reads=[ptk] + src_key_fn(g, j),
                               writes=[BKEYS[bi]])
            for pr in range(npair):
                _rope_pair(BANKS[2 * pr], BKEYS[2 * pr],
                           BANKS[2 * pr + 1], BKEYS[2 * pr + 1])
                if qknorm:
                    _qknorm_rkbf(BANKS[2 * pr], BKEYS[2 * pr],
                                 BANKS[2 * pr + 1], BKEYS[2 * pr + 1])
                for part in range(2):
                    cache_write_fn(pr, part, j,
                                   RKBF[:, 512 * part:512 * part + 512],
                                   [("RKBF", part)])

    RGQ = [("sync", "rg"), ("scalar", "rg"), ("sync", "rg2"), ("scalar", "rg2")]

    def kc_writer(arena, col0, name, dup, extra_w_fn=None):
        def w(pr, part, j, src_ap, keys):
            qi = 0
            for a in range(4):
                h = 4 * pr + a
                extra = extra_w_fn(h, part, j) if extra_w_fn else []
                if dup:
                    for rep in range(2):
                        r0 = 64 * rep + 32 * part
                        dst = arena[r0:r0 + 32,
                                    col0 + 4096 * h + 512 * j:col0 + 4096 * h + 512 * j + 512]
                        eng, ch = RGQ[qi % 4]
                        qi += 1
                        dma(eng, dst, src_ap[32 * a:32 * a + 32, :],
                            reads=keys, writes=[(name, h, part, j, rep)] + extra,
                            chan=ch)
                else:
                    pair = h // 2
                    r0 = 64 * (h % 2) + 32 * part
                    dst = arena[r0:r0 + 32,
                                col0 + 4096 * pair + 512 * j:col0 + 4096 * pair + 512 * j + 512]
                    eng, ch = RGQ[qi % 4]
                    qi += 1
                    dma(eng, dst, src_ap[32 * a:32 * a + 32, :],
                        reads=keys, writes=[(name, h, part, j)] + extra, chan=ch)
        return w

    def v_pass(src_arena, src_key_fn, nch, w_dram, scale_fn, nkv, vc_col0,
               vc_name, src_b16=None, pcname=None, merged=False):
        banks = [P_QF, P_KF, P_KV, P_V, P_SEL, P_S, P_Y, P_W]
        bkeys = [("P_QF",), ("P_KF",), ("P_KV",), ("P_V",), ("P_SEL",), ("P_S",),
                 ("P_Y",), ("P_W",)]
        width = 64 * nkv
        nhalf = (width + 511) // 512
        ngrp = 8 // nhalf
        si = 0
        for cg in range(NC // ngrp):
            for g in range(nch):
                for hh in range(nhalf):
                    cw = min(512, width - 512 * hh)
                    if src_b16 is not None:
                        slot, skey = ((YSTG, ("XSB",)) if si % 2 == 0
                                      else (XSQ, ("XSQ",)))
                        si += 1
                        dma("sync", slot[:, 0:cw],
                            src_b16[128 * g:128 * g + 128, 512 * hh:512 * hh + cw],
                            reads=[("PCW", pcname, g, hh), skey], writes=[skey],
                            chan="w" if si % 2 == 0 else "w2")
                    else:
                        slot, skey = YSTG, ("XSB",)
                        dma("sync", WS[:, 0:cw],
                            w_dram.ap()[128 * g:128 * g + 128, 512 * hh:512 * hh + cw],
                            reads=[("WS",)], writes=[("WS",)], chan="w")
                        if scale_fn is not None:
                            vop(lambda e, cw=cw, g=g: e.tensor_scalar(
                                out=YSTG[:, 0:cw], in0=WS[:, 0:cw], scalar1=scale_fn(g),
                                scalar2=None, op0=ALU.mult),
                                [("WS",), ("SC1",), ("SC2",), ("N1W",), ("N2W",)],
                                [("XSB",)])
                        else:
                            vop(lambda e, cw=cw: e.tensor_copy(
                                out=YSTG[:, 0:cw], in_=WS[:, 0:cw]),
                                [("WS",)], [("XSB",)])
                    for ci in range(ngrp):
                        c = cg * ngrp + ci
                        bi = ci * nhalf + hh
                        mm(banks[bi][:, 0:cw],
                           src_arena[:, T * g + 128 * c:T * g + 128 * c + 128],
                           slot[:, 0:cw],
                           start=(g == 0), stop=(g == nch - 1),
                           reads=[skey] + src_key_fn(g, c // 4),
                           writes=[bkeys[bi]])
            for ci in range(ngrp):
                c = cg * ngrp + ci
                for hh in range(nhalf):
                    cw = min(512, width - 512 * hh)
                    bi = ci * nhalf + hh
                    if merged:
                        d0 = vc_col0 + 1040 * c + 520 * hh
                        vop(lambda e, d0=d0, bi=bi: e.tensor_copy(
                            out=ARB[:, d0:d0 + 520].rearrange(
                                "p (b x) -> p b x", x=65)[:, :, 0:64],
                            in_=banks[bi][:, 0:512].rearrange(
                                "p (b x) -> p b x", x=64)),
                            [bkeys[bi]], [(vc_name, c)])
                    else:
                        vop(lambda e, c=c, hh=hh, cw=cw, bi=bi: e.tensor_copy(
                            out=ARB[:, vc_col0 + width * c + 512 * hh:
                                    vc_col0 + width * c + 512 * hh + cw],
                            in_=banks[bi][:, 0:cw]),
                            [bkeys[bi]], [(vc_name, c)])

    def attn_pass(qc_arena, kc_arena, kc_col0, kc_name, kc_dup, vc_col0, vc_name,
                  nkv, bias_col, window, tag, merged=False):
        width = 64 * nkv

        def krange(jq):
            lo = 0 if window is None else max(0, 4 * jq - 2)
            return list(range(lo, 4 * jq + 4))

        for jq in range(NT):
            pend = [None]
            for h in range(16):
                kv = h * nkv // 16
                par = h % 2
                pair = h // 2
                if par == 0:
                    PYB, pyk = P_Y, ("P_Y",)
                    y0, y1, drow, dbase = 0, 64, 64, 64
                else:
                    PYB, pyk = P_SEL, ("P_SEL",)
                    y0, y1, drow, dbase = 64, 128, 0, 0
                kts = krange(jq)
                nk = len(kts)

                def kparams(ki):
                    kt = kts[ki]
                    off = 128 * kt - 512 * jq
                    qlo = max(0, off)
                    qhi = 512 if window is None else min(512, off + 384)
                    if ki == 0:
                        qlo = 0
                    return kt, off, qlo, qhi, ki == 0, ki == nk - 1

                def emit_score(ki):
                    kt, off, qlo, qhi, first, last = kparams(ki)
                    if kc_dup:
                        lh = kc_arena[64 * par:64 * par + 64,
                                      kc_col0 + 4096 * kv + 128 * kt:
                                      kc_col0 + 4096 * kv + 128 * kt + 128]
                        lkeys = [(kc_name, kv, pp, kt // 4, par) for pp in range(2)]
                    else:
                        lh = kc_arena[64 * par:64 * par + 64,
                                      kc_col0 + 4096 * (kv // 2) + 128 * kt:
                                      kc_col0 + 4096 * (kv // 2) + 128 * kt + 128]
                        lkeys = [(kc_name, kv, pp, kt // 4) for pp in range(2)]
                    rh = qc_arena[64 * par:64 * par + 64,
                                  4096 * pair + 512 * jq + qlo:
                                  4096 * pair + 512 * jq + qhi]
                    rkeys = [("QC" + tag, h, pp, jq) for pp in range(2)]
                    sb_, sk_ = (P_KV, ("P_KV",)) if ki % 2 == 0 else (P_QF, ("P_QF",))
                    mm(sb_[:, qlo:qhi], lh, rh, start=True, stop=True,
                       reads=lkeys + rkeys, writes=[sk_], tp=(64 * par, 0))

                def emit_tail(ki):
                    kt, off, qlo, qhi, first, last = kparams(ki)
                    sb_, sk_ = (P_KV, ("P_KV",)) if ki % 2 == 0 else (P_QF, ("P_QF",))
                    PT = PT0 if ki % 2 == 0 else PT1
                    ptk = ("PT", ki % 2)
                    P.op("scalar", lambda e, sb_=sb_, PT=PT, qlo=qlo, qhi=qhi: e.activation(
                        out=PT[:, qlo:qhi], in_=sb_[:, qlo:qhi], func=AF.Exp,
                        bias=bias_col, scale=CONSTS[:, 3:4]),
                        reads=[sk_, ("CONSTS",)], writes=[ptk])
                    if off >= 0:
                        vop(lambda e, PT=PT, o=off: e.tensor_mul(
                            out=PT[:, o:o + 128], in0=PT[:, o:o + 128],
                            in1=TRIU[:, :]), [ptk, ("TRIU",)], [ptk])
                    if window is not None and off + 384 <= 512:
                        vop(lambda e, PT=PT, o=qhi - 128: e.tensor_mul(
                            out=PT[:, o:o + 128], in0=PT[:, o:o + 128],
                            in1=MGT[:, :]), [ptk, ("MGT",)], [ptk])
                    if first and qhi < 512:
                        vop(lambda e, PT=PT, qhi=qhi: e.memset(PT[:, qhi:512], 0.0),
                            [ptk], [ptk])
                        qhi = 512
                    if merged:
                        vc0 = vc_col0 + 1040 * kt + 65 * kv
                        if par == 0:
                            mm(PYB[0:65, qlo:qhi], ARB[:, vc0:vc0 + 65],
                               PT[:, qlo:qhi], start=first, stop=last,
                               reads=[(vc_name, kt), ptk], writes=[pyk],
                               tp=(0, 0))
                        else:
                            mm(PYB[y0:y1, qlo:qhi], ARB[:, vc0:vc0 + 64],
                               PT[:, qlo:qhi], start=first, stop=last,
                               reads=[(vc_name, kt), ptk], writes=[pyk],
                               tp=(0, 64))
                            mm(P_S[0:1, qlo:qhi], ONESCOL[:, :], PT[:, qlo:qhi],
                               start=first, stop=last,
                               reads=[("ONESCOL",), ptk], writes=[("P_S",)],
                               tp=(0, 0))
                    else:
                        vsl = ARB[:, vc_col0 + width * kt + 64 * kv:
                                  vc_col0 + width * kt + 64 * kv + 64]
                        mm(PYB[y0:y1, qlo:qhi], vsl, PT[:, qlo:qhi],
                           start=first, stop=last,
                           reads=[(vc_name, kt), ptk], writes=[pyk], tp=(0, 64 * par))
                        mm(P_S[drow:drow + 1, qlo:qhi], ONESCOL[:, :], PT[:, qlo:qhi],
                           start=first, stop=last,
                           reads=[("ONESCOL",), ptk], writes=[("P_S",)], tp=(0, dbase))

                emit_score(0)
                for ki in range(nk):
                    if ki + 1 < nk:
                        emit_score(ki + 1)
                    emit_tail(ki)

                def norm(par=par, pyk=pyk, pair=pair):
                    if par == 0:
                        dsrc, dkey = ((P_Y, pyk) if merged else (P_S, ("P_S",)))
                        vop(lambda e, dsrc=dsrc: e.reciprocal(
                            out=LNB[64:65, 0:512], in_=dsrc[64:65, 0:512]),
                            [dkey], [("LNB", "d0")])
                        mm(P_W[:, :], ONESF128[64:65, :], LNB[64:65, 0:512],
                           start=True, stop=True,
                           reads=[("LNB", "d0"), ("ONESF128",)], writes=[("P_W",)],
                           tp=(64, 0))
                        P.op("scalar", lambda e: e.copy(out=BCV[:, :], in_=P_W[:, :]),
                             reads=[("P_W",)], writes=[("BCV", "a")])
                        vop(lambda e: e.tensor_mul(out=YSTG[0:64, :],
                                                   in0=P_Y[0:64, 0:512],
                                                   in1=BCV[0:64, :]),
                            [pyk, ("BCV", "a")], [("XSB",)])
                    else:
                        vop(lambda e: e.reciprocal(out=LNB[0:1, 512:1024],
                                                   in_=P_S[0:1, 0:512]),
                            [("P_S",)], [("LNB", "d1")])
                        mm(P_V[:, :], ONESROW_F[:, :], LNB[0:1, 512:1024],
                           start=True, stop=True,
                           reads=[("LNB", "d1"), ("ONESROW_F",)], writes=[("P_V",)])
                        P.op("scalar", lambda e: e.copy(out=LNB[:, 0:512],
                                                        in_=P_V[:, :]),
                             reads=[("P_V",)], writes=[("LNB", 0)])
                        vop(lambda e: e.tensor_mul(out=YSTG[64:128, :],
                                                   in0=P_SEL[64:128, 0:512],
                                                   in1=LNB[64:128, 0:512]),
                            [pyk, ("LNB", 0)], [("XSB",)])
                        dma("scalar",
                            ysp.ap()[128 * pair:128 * pair + 128,
                                     512 * jq:512 * (jq + 1)], YSTG[:, :],
                            reads=[("XSB",)],
                            writes=[("ysp", tag, pair, jq)], chan="ysp")

                if pend[0] is not None:
                    pend[0]()
                pend[0] = norm
                pump(6, "att")
            pend[0]()

    P.scope("P3a_k")
    W3C = 20480
    P.claim_prefix([("W3", g) for g in range(NCH)], {"SELR", "SELT", "BSB"})
    P.claim_prefix([("KC", h, p, j) for h in range(16) for p in range(2)
                    for j in range(NT)], {"XB"})
    load_w_b16(lambda g: ARB[:, W3C + 1024 * g:W3C + 1024 * (g + 1)], wkb.ap(),
               1024, "W3", "wk")

    def w3_slot(g, part, pr, npair=4, col0=W3C):
        c = col0 + 1024 * g + 128 * (npair * part + pr)
        return ARB[:, c:c + 128]

    rope_proj_pass(RMS1X, lambda g, j: [("RMS1X", g, j)], NCH, w3_slot, "W3",
                   4, C_COSL, C_SINL, False, kc_writer(XB, 0, "KC", False))
    P.scope("P3b_v")
    P.claim_prefix([("VC3", c) for c in range(NC)],
                   {"QF", "KF", "VTOK", "KTOKP", "QF2", "SB1", "KF2", "PP", "PM",
                    "SS", "S01M", "S01B", "YSQ", "BSB", "SELR", "SELT", "W3",
                    "S2M", "XSUM1"})
    vop(lambda e: e.memset(ARB[:, 0:33280], 1.0), (),
        [("VC3", c) for c in range(NC)])
    v_pass(RMS1X, lambda g, j: [("RMS1X", g, j)], NCH, None, None, 16, 0,
           "VC3", src_b16=wvb.ap(), pcname="wv", merged=True)
    P.scope("P3c_q")
    rope_proj_psum_stream(RMS1X, lambda g, j: [("RMS1X", g, j)], NCH, wqsb,
                          4, C_COSL, C_SINL, False,
                          kc_writer(ARA, 0, "QCl0", False,
                                    extra_w_fn=lambda h, p, j: [("RMS1X", h // 2, j)]))
    for _ex in range(8):
        pc_add("w1", F_w1.ap()[_ex], w1b.ap()[_ex], D, D, ex=_ex)
        pc_add("w2", F_w2.ap()[_ex], w2b.ap()[_ex], D, D, ex=_ex)
        pc_add("wout", F_wout.ap()[_ex], woutb.ap()[_ex], D, D, ex=_ex)
    P.scope("P3d_attn")
    attn_pass(ARA, XB, 0, "KC", False, 0, "VC3", 16, CONSTS[:, 4:5], None, "l0",
              merged=True)
    assert not PCPEND, f"group B pre-casts not drained: {len(PCPEND)}"
    P.scope("P3e_wo")
    P.claim_prefix([("Wl0o", g) for g in range(NCH)]
                   + [("YT", g) for g in range(NCH)]
                   + [("XT", g) for g in range(NCH)]
                   + [("HT", g) for g in range(NCH)], {"VC3"})
    load_w_b16(lambda g: ARB[:, 1024 * g:1024 * (g + 1)], wob.ap(), 1024,
               "Wl0o", "wo")
    for j in range(NT):
        for g in range(NCH):
            dma("sync", ARB[:, 8192 + 512 * g:8192 + 512 * (g + 1)],
                ysp.ap()[128 * g:128 * g + 128, 512 * j:512 * (j + 1)],
                reads=[("ysp", "l0", g, j), ("YT", g)], writes=[("YT", g)],
                chan="xt")
            dma("scalar", ARB[:, 12288 + 512 * g:12288 + 512 * (g + 1)],
                x1sp.ap()[128 * g:128 * g + 128, 512 * j:512 * (j + 1)],
                reads=[("x1sp", g, j), ("XT", g)], writes=[("XT", g)], chan="xt2")
        for o in range(NCH):
            for g in range(NCH):
                mm(P_W[:, :], ARB[:, 1024 * g + 128 * o:1024 * g + 128 * o + 128],
                   ARB[:, 8192 + 512 * g:8192 + 512 * (g + 1)],
                   start=(g == 0), stop=(g == NCH - 1),
                   reads=[("Wl0o", g), ("YT", g)], writes=[("P_W",)])
            vop(lambda e, o=o: e.tensor_add(
                out=ARB[:, 16384 + 512 * o:16384 + 512 * (o + 1)], in0=P_W[:, :],
                in1=ARB[:, 12288 + 512 * o:12288 + 512 * (o + 1)]),
                [("P_W",), ("XT", o)], [("HT", o)])
        for g in range(NCH):
            dma("scalar", hsp.ap()[128 * g:128 * g + 128, 512 * j:512 * (j + 1)],
                ARB[:, 16384 + 512 * g:16384 + 512 * (g + 1)],
                reads=[("HT", g)], writes=[("hsp", g, j)], chan="sp")

    if stop == "h":
        debug_hbm_out(hsp, "hsp")
        P.finalize()
        return nc

    P.scope("P4_router")
    P.claim_prefix([("WS",)], {"WSB"})
    WTG8 = XB[0:8, 28672:32768]
    P.claim_prefix([("HT4", g) for g in range(NCH)]
                   + [("RWS", g) for g in range(NCH)], {"QCl0"})
    P.claim_prefix([("WTG8", j, tc) for j in range(NT) for tc in range(4)],
                   {"KC"})
    P.claim_prefix([("R2X", g, j) for g in range(7) for j in range(NT)], {"KC"})
    P.claim_prefix([("R2X", 7, j) for j in range(NT)], {"VC3"})

    def r2x_slot(g, j):
        if g < 7:
            return XB[:, 4096 * g + 512 * j:4096 * g + 512 * (j + 1)]
        return ARB[:, 24576 + 512 * j:24576 + 512 * (j + 1)]
    load_w(lambda g: ARA[:, 8192 + 8 * g:8192 + 8 * (g + 1)], F_rw.ap(), 8,
           "RWS")
    for j in range(NT):
        for g in range(NCH):
            dma("scalar", ARA[:, 512 * g:512 * (g + 1)],
                hsp.ap()[128 * g:128 * g + 128, 512 * j:512 * (j + 1)],
                reads=[("hsp", g, j), ("HT4", g)], writes=[("HT4", g)], chan="xt")
        stats_sumsq(lambda g: (ARA[:, 512 * g:512 * (g + 1)], [("HT4", g)]),
                    None, via_f32=False)
        rms_r(CONSTS[0:1, 2:3])
        for g in range(NCH):
            vop(lambda e, g=g, j=j: e.scalar_tensor_tensor(
                out=r2x_slot(g, j),
                in0=ARA[:, 512 * g:512 * (g + 1)], scalar=SC2[:, g:g + 1],
                in1=LNB[:, 512:1024], op0=ALU.mult, op1=ALU.mult),
                [("HT4", g), ("SC2",), ("LNB", 512)], [("R2X", g, j)])
        for tc in range(4):
            for g in range(NCH):
                mm(P_S[:, 0:8], r2x_slot(g, j)[:, 128 * tc:128 * tc + 128],
                   ARA[:, 8192 + 8 * g:8192 + 8 * (g + 1)],
                   start=(g == 0), stop=(g == NCH - 1),
                   reads=[("R2X", g, j), ("RWS", g)], writes=[("P_S",)])
            dma("sync", GUM[:, :],
                F_gum.ap()[512 * j + 128 * tc:512 * j + 128 * tc + 128, :],
                reads=[("GUM",)], writes=[("GUM",)], chan="w")
            vop(lambda e: e.tensor_add(out=WTOK[:, 0:8], in0=P_S[:, 0:8],
                                       in1=GUM[:, :]),
                [("P_S",), ("GUM",)], [("WTOK",)])
            vop(lambda e: e.reduce_max(out=WTOK[:, 8:9], in_=WTOK[:, 0:8],
                                       axis=mybir.AxisListType.X, negate=True),
                [("WTOK",)], [("WTOK", "nm")])
            P.op("scalar", lambda e: e.activation(
                out=WTOK[:, 16:24], in_=WTOK[:, 0:8], func=AF.Exp,
                bias=WTOK[:, 8:9], scale=1.0),
                reads=[("WTOK",), ("WTOK", "nm")], writes=[("WTOK", "e")])
            vop(lambda e: e.reduce_sum(out=WTOK[:, 9:10], in_=WTOK[:, 16:24],
                                       axis=mybir.AxisListType.X),
                [("WTOK", "e")], [("WTOK", "s")])
            vop(lambda e: e.reciprocal(out=WTOK[:, 10:11], in_=WTOK[:, 9:10]),
                [("WTOK", "s")], [("WTOK", "rs")])
            vop(lambda e: e.reduce_max(out=WTOK[:, 11:12], in_=WTOK[:, 16:24],
                                       axis=mybir.AxisListType.X),
                [("WTOK", "e")], [("WTOK", "m1")])
            vop(lambda e: e.tensor_scalar(out=WTOK[:, 0:8], in0=WTOK[:, 16:24],
                                          scalar1=WTOK[:, 11:12], scalar2=None,
                                          op0=ALU.is_ge),
                [("WTOK", "e"), ("WTOK", "m1")], [("WTOK",)])
            vop(lambda e: e.tensor_scalar(out=WTOK[:, 0:8], in0=WTOK[:, 0:8],
                                          scalar1=-1.0, scalar2=1.0,
                                          op0=ALU.mult, op1=ALU.add),
                [("WTOK",)], [("WTOK",)])
            vop(lambda e: e.tensor_mul(out=WTOK[:, 0:8], in0=WTOK[:, 0:8],
                                       in1=WTOK[:, 16:24]),
                [("WTOK",), ("WTOK", "e")], [("WTOK",)])
            vop(lambda e: e.reduce_max(out=WTOK[:, 12:13], in_=WTOK[:, 0:8],
                                       axis=mybir.AxisListType.X),
                [("WTOK",)], [("WTOK", "m2")])
            vop(lambda e: e.tensor_scalar(out=WTOK[:, 0:8], in0=WTOK[:, 16:24],
                                          scalar1=WTOK[:, 12:13], scalar2=None,
                                          op0=ALU.is_ge),
                [("WTOK", "e"), ("WTOK", "m2")], [("WTOK",)])
            vop(lambda e: e.tensor_mul(out=WTOK[:, 0:8], in0=WTOK[:, 0:8],
                                       in1=WTOK[:, 16:24]),
                [("WTOK",), ("WTOK", "e")], [("WTOK",)])
            vop(lambda e: e.tensor_scalar(out=WTOK[:, 0:8], in0=WTOK[:, 0:8],
                                          scalar1=WTOK[:, 10:11], scalar2=None,
                                          op0=ALU.mult),
                [("WTOK",), ("WTOK", "rs")], [("WTOK",)])
            vop(lambda e: e.tensor_copy(out=WTOKB[:, :], in_=WTOK[:, 0:8]),
                [("WTOK",)], [("WTOKB",)])
            P.op("tensor", lambda e: e.transpose(
                P_W[0:8, 0:64].bitcast(BF16), WTOKB[:, :], IDENT[:, :]),
                reads=[("WTOKB",), ("IDENT",)], writes=[("P_W",)])
            vop(lambda e, j=j, tc=tc: e.tensor_copy(
                out=WTG8[:, 512 * j + 128 * tc:512 * j + 128 * tc + 128],
                in_=P_W[0:8, 0:64].bitcast(BF16)),
                [("P_W",)], [("WTG8", j, tc)])

    P.scope("P6_moe")
    P.claim_prefix([("XSQ",)], {"XSQB"})
    ACC = ARA
    BOUTT = ARB[0:8, 32256:33280]
    P.claim_prefix([("ACC", g, j) for g in range(NCH) for j in range(NT)],
                   {"HT4", "R2B", "RWS"})
    P.claim_prefix([("BOUTT",)], {"VC3"})
    P.claim_prefix([("W1e", g) for g in range(NCH)]
                   + [("W2e", g) for g in range(NCH)]
                   + [("WOUTe", g) for g in range(NCH)],
                   {"Wl0o", "YT", "XT", "HT", "VC3"})
    for hh in range(2):
        dma("sync", WS[0:8, 0:512], F_bout.ap()[:, 512 * hh:512 * (hh + 1)],
            reads=[("WS",)], writes=[("WS",)], chan="w")
        vop(lambda e, hh=hh: e.tensor_copy(
            out=BOUTT[:, 512 * hh:512 * (hh + 1)], in_=WS[0:8, 0:512]),
            [("WS",)], [("BOUTT",)])
    for j in range(NT):
        for o in range(NCH):
            mm(P_W[:, :], BOUTT[:, 128 * o:128 * (o + 1)],
               WTG8[:, 512 * j:512 * (j + 1)],
               start=True, stop=True,
               reads=[("BOUTT",)] + [("WTG8", j, tc) for tc in range(4)],
               writes=[("P_W",)])
            P.op("scalar", lambda e, o=o, j=j: e.copy(
                out=ACC[:, 4096 * o + 512 * j:4096 * o + 512 * (j + 1)],
                in_=P_W[:, :]),
                reads=[("P_W",)], writes=[("ACC", o, j)])
    P.claim_prefix([("HE", g) for g in range(NCH)], {"VC3", "BOUTT"})
    for ex in range(8):
        load_w_b16(lambda g: ARB[:, 1024 * g:1024 * (g + 1)], w1b.ap()[ex],
                   1024, "W1e", "w1", ex=ex)
        load_w_b16(lambda g: ARB[:, 8192 + 1024 * g:8192 + 1024 * (g + 1)],
                   w2b.ap()[ex], 1024, "W2e", "w2", ex=ex)
        load_w_b16(lambda g: ARB[:, 16384 + 1024 * g:16384 + 1024 * (g + 1)],
                   woutb.ap()[ex], 1024, "WOUTe", "wout", ex=ex)
        for o in range(NCH):
            dma("sync", BB1[:, o:o + 1],
                F_b1.ap()[ex:ex + 1, 128 * o:128 * o + 128],
                reads=[("BB1",)], writes=[("BB1",)], chan="w")
            dma("sync", BB2[:, o:o + 1],
                F_b2.ap()[ex:ex + 1, 128 * o:128 * o + 128],
                reads=[("BB2",)], writes=[("BB2",)], chan="w")
        for j in range(NT):
            mm(P_S[:, :], SEL8[:, 128 * ex:128 * (ex + 1)],
               WTG8[:, 512 * j:512 * (j + 1)],
               start=True, stop=True,
               reads=[("WTG8", j, tc) for tc in range(4)] + [("SEL8",)],
               writes=[("P_S",)])
            P.op("scalar", lambda e: e.copy(out=BCV[:, :], in_=P_S[:, :]),
                 reads=[("P_S",)], writes=[("BCV", "a")])
            for o in range(NCH):
                bh1, kh1 = (P_QF, ("P_QF",)) if o % 2 == 0 else (P_KV, ("P_KV",))
                bh2, kh2 = (P_KF, ("P_KF",)) if o % 2 == 0 else (P_V, ("P_V",))
                sl1, ks1 = (XSB, ("XSB",)) if o % 2 == 0 else (XSQ, ("XSQ",))
                sl2, ks2 = (PT0, ("PT", 0)) if o % 2 == 0 else (PT1, ("PT", 1))
                for g in range(NCH):
                    mm(bh1[:, :],
                       ARB[:, 1024 * g + 128 * o:1024 * g + 128 * o + 128],
                       r2x_slot(g, j),
                       start=(g == 0), stop=(g == NCH - 1),
                       reads=[("W1e", g), ("R2X", g, j)], writes=[kh1])
                for g in range(NCH):
                    mm(bh2[:, :],
                       ARB[:, 8192 + 1024 * g + 128 * o:8192 + 1024 * g + 128 * o + 128],
                       r2x_slot(g, j),
                       start=(g == 0), stop=(g == NCH - 1),
                       reads=[("W2e", g), ("R2X", g, j)], writes=[kh2])
                P.op("scalar", lambda e, o=o, bh1=bh1, sl1=sl1: e.activation(
                    out=sl1[:, :], in_=bh1[:, :], func=AF.Silu,
                    bias=BB1[:, o:o + 1], scale=1.0),
                    reads=[kh1, ("BB1",)], writes=[ks1])
                vop(lambda e, o=o, bh2=bh2, sl1=sl1, sl2=sl2: e.scalar_tensor_tensor(
                    out=sl2[:, :], in0=bh2[:, :], scalar=BB2[:, o:o + 1],
                    in1=sl1[:, :], op0=ALU.add, op1=ALU.mult),
                    [kh2, ("BB2",), ks1], [ks2])
                vop(lambda e, o=o, sl2=sl2: e.tensor_mul(
                    out=ARB[:, 28672 + 512 * o:28672 + 512 * (o + 1)],
                    in0=sl2[:, :], in1=BCV[:, :]),
                    [ks2, ("BCV", "a")], [("HE", o)])
            for o in range(NCH):
                bw, kw = (P_W, ("P_W",)) if o % 2 == 0 else (P_SEL, ("P_SEL",))
                for g in range(NCH):
                    mm(bw[:, :],
                       ARB[:, 16384 + 1024 * g + 128 * o:16384 + 1024 * g + 128 * o + 128],
                       ARB[:, 28672 + 512 * g:28672 + 512 * (g + 1)],
                       start=(g == 0), stop=(g == NCH - 1),
                       reads=[("WOUTe", g), ("HE", g)], writes=[kw])
                vop(lambda e, o=o, j=j, bw=bw: e.scalar_tensor_tensor(
                    out=ACC[:, 4096 * o + 512 * j:4096 * o + 512 * (j + 1)],
                    in0=bw[:, :], scalar=1.0,
                    in1=ACC[:, 4096 * o + 512 * j:4096 * o + 512 * (j + 1)],
                    op0=ALU.mult, op1=ALU.add),
                    [kw], [("ACC", o, j)])

    P.scope("P7_ln")
    XSUM7 = carve(ARB, 24576, 8192, F32)
    P.claim_prefix([("XSUM7", g) for g in range(NCH)], {"R2X", "HE"})
    P.claim_prefix([("XB2", g, j) for g in range(NCH) for j in range(NT)],
                   {"KC", "WTG8", "R2X", "XB"})
    P.claim_prefix([("X1T", g, p) for g in range(NCH) for p in range(2)],
                   {"W1e"})
    P.claim_prefix([("HT7", g, p) for g in range(NCH) for p in range(2)],
                   {"W2e"})

    def x1t(g, j):
        c0 = 4096 * (j % 2) + 512 * g
        return ARB[:, c0:c0 + 512]

    def ht7(g, j):
        c0 = 8192 + 4096 * (j % 2) + 512 * g
        return ARB[:, c0:c0 + 512]

    def p7_load(j):
        for g in range(NCH):
            dma("scalar", x1t(g, j),
                x1sp.ap()[128 * g:128 * g + 128, 512 * j:512 * (j + 1)],
                reads=[("x1sp", g, j), ("X1T", g, j % 2)],
                writes=[("X1T", g, j % 2)], chan="xt")
            dma("scalar", ht7(g, j),
                hsp.ap()[128 * g:128 * g + 128, 512 * j:512 * (j + 1)],
                reads=[("hsp", g, j), ("HT7", g, j % 2)],
                writes=[("HT7", g, j % 2)], chan="xt2")

    p7_load(0)
    for j in range(NT):
        if j + 1 < NT:
            p7_load(j + 1)
        for g in range(NCH):
            vop(lambda e, g=g, j=j: e.tensor_add(
                out=XSUM7[:, 512 * g:512 * (g + 1)],
                in0=ACC[:, 4096 * g + 512 * j:4096 * g + 512 * (j + 1)],
                in1=x1t(g, j)),
                [("ACC", g, j), ("X1T", g, j % 2)], [("XSUM7", g)])
            vop(lambda e, g=g, j=j: e.scalar_tensor_tensor(
                out=XSUM7[:, 512 * g:512 * (g + 1)], in0=ht7(g, j), scalar=1.0,
                in1=XSUM7[:, 512 * g:512 * (g + 1)], op0=ALU.mult, op1=ALU.add),
                [("HT7", g, j % 2)], [("XSUM7", g)])
        stats_sumsq(lambda g: (XSUM7[:, 512 * g:512 * (g + 1)], [("XSUM7", g)]),
                    None, also_sum=True, via_f32=True)
        ln_finish(CONSTS[0:1, 1:2],
                  src_fn=lambda g: (XSUM7[:, 512 * g:512 * (g + 1)],
                                    [("XSUM7", g)]),
                  out_writer=lambda g, j=j: vop(
                      lambda e, g=g: e.tensor_mul(
                          out=XB[:, T * g + 512 * j:T * g + 512 * (j + 1)],
                          in0=WS[:, :], in1=LNB[:, 512:1024]),
                      [("WS",), ("LNB", 512)], [("XB2", g, j)]))

    if stop == "x2":
        debug_stream_out(lambda g, q: [("XB2", g, q)])
        P.finalize()
        return nc

    P.scope("P8_glob")
    N1X = ARA
    P.claim_prefix([("N1X", g, j) for g in range(NCH) for j in range(NT)],
                   {"ACC", "HT4", "R2B", "RWS"})
    for j in range(NT):
        stats_sumsq(lambda g, j=j: (XB[:, T * g + 512 * j:T * g + 512 * (j + 1)],
                                    [("XB2", g, j)]), None, via_f32=False)
        rms_r(CONSTS[0:1, 2:3])
        for g in range(NCH):
            vop(lambda e, g=g, j=j: e.scalar_tensor_tensor(
                out=N1X[:, T * g + 512 * j:T * g + 512 * (j + 1)],
                in0=XB[:, T * g + 512 * j:T * g + 512 * (j + 1)],
                scalar=N1W[:, g:g + 1], in1=LNB[:, 512:1024],
                op0=ALU.mult, op1=ALU.mult),
                [("XB2", g, j), ("N1W",), ("LNB", 512)], [("N1X", g, j)])
    P.claim_prefix([("WKVD", g) for g in range(NCH)] + [("WKUP", g) for g in range(2)],
                   {"XSUM7"})
    P.claim_prefix([("LATB", m, j) for m in range(2) for j in range(NT)],
                   {"XSUM7"})
    load_w(lambda g: ARB[:, 16384 + 256 * g:16384 + 256 * (g + 1)], G_kvd.ap(),
           256, "WKVD")
    for j in range(NT):
        for m2 in range(2):
            lb, lk = (P_QF, ("P_QF",)) if m2 == 0 else (P_KF, ("P_KF",))
            for g in range(NCH):
                mm(lb[:, :],
                   ARB[:, 16384 + 256 * g + 128 * m2:16384 + 256 * g + 128 * m2 + 128],
                   N1X[:, T * g + 512 * j:T * g + 512 * (j + 1)],
                   start=(g == 0), stop=(g == NCH - 1),
                   reads=[("WKVD", g), ("N1X", g, j)], writes=[lk])
            vop(lambda e, m2=m2, j=j, lb=lb: e.tensor_copy(
                out=ARB[:, 24576 + 4096 * m2 + 512 * j:24576 + 4096 * m2 + 512 * (j + 1)],
                in_=lb[:, :]),
                [lk], [("LATB", m2, j)])
    LATB = ARB[:, 24576:32768]
    load_w(lambda g: ARB[:, 18432 + 256 * g:18432 + 256 * (g + 1)], G_kup.ap(),
           256, "WKUP", nch=2)
    P.claim_prefix([("KCg", h, p, j, r) for h in range(4) for p in range(2)
                    for j in range(NT) for r in range(2)],
                   {"W1e", "W2e", "WOUTe", "BOUTT", "VC3"})
    rope_proj_pass(LATB, lambda g, j: [("LATB", g, j)], 2,
                   lambda g, part, pr: ARB[:, 18432 + 256 * g + 128 * part:
                                           18432 + 256 * g + 128 * part + 128],
                   "WKUP", 1, C_COSG, C_SING, True,
                   kc_writer(ARB, 0, "KCg", True))
    P.claim_prefix([("VCg", c) for c in range(NC)], {"WKVD", "WKUP"})
    v_pass(LATB, lambda g, j: [("LATB", g, j)], 2, G_vup, None, 4, 16384, "VCg")
    P.claim_prefix([("W8", g) for g in range(NCH)], {"LATB"})
    load_w(lambda g: ARB[:, 24576 + 1024 * g:24576 + 1024 * (g + 1)], G_wq.ap(),
           1024, "W8")
    rope_proj_psum(N1X, lambda g, j: [("N1X", g, j)], NCH,
                   lambda g, part, pr: ARB[:, 24576 + 1024 * g + 128 * (4 * part + pr):
                                           24576 + 1024 * g + 128 * (4 * part + pr) + 128],
                   "W8", 4, C_COSG, C_SING, True,
                   kc_writer(ARA, 0, "QCg", False,
                             extra_w_fn=lambda h, p, j: [("N1X", h // 2, j)]))
    attn_pass(ARA, ARB, 0, "KCg", True, 16384, "VCg", 4, CONSTS[:, 5:6], None, "g")
    P.claim_prefix([("W8o", g) for g in range(NCH)], {"W8"})
    P.claim_prefix([("YT8", g) for g in range(NCH)], {"VCg"})
    P.claim_prefix([("XMT8", g) for g in range(NCH)], {"VCg"})
    load_w(lambda g: ARB[:, 24576 + 1024 * g:24576 + 1024 * (g + 1)], G_wo.ap(),
           1024, "W8o")
    for j in range(NT):
        for g in range(NCH):
            dma("sync", ARB[:, 16384 + 512 * g:16384 + 512 * (g + 1)],
                ysp.ap()[128 * g:128 * g + 128, 512 * j:512 * (j + 1)],
                reads=[("ysp", "g", g, j), ("YT8", g)], writes=[("YT8", g)],
                chan="xt")
        for o in range(NCH):
            for g in range(NCH):
                mm(P_W[:, :],
                   ARB[:, 24576 + 1024 * g + 128 * o:24576 + 1024 * g + 128 * o + 128],
                   ARB[:, 16384 + 512 * g:16384 + 512 * (g + 1)],
                   start=(g == 0), stop=(g == NCH - 1),
                   reads=[("W8o", g), ("YT8", g)], writes=[("P_W",)])
            vop(lambda e, o=o, j=j: e.tensor_add(
                out=ARB[:, 20480 + 512 * o:20480 + 512 * (o + 1)], in0=P_W[:, :],
                in1=XB[:, T * o + 512 * j:T * o + 512 * (j + 1)]),
                [("P_W",), ("XB2", o, j)], [("XMT8", o)])
        for g in range(NCH):
            dma("scalar", xmsp.ap()[128 * g:128 * g + 128, 512 * j:512 * (j + 1)],
                ARB[:, 20480 + 512 * g:20480 + 512 * (g + 1)],
                reads=[("XMT8", g)], writes=[("xmsp", g, j)], chan="sp")

    if stop == "xm":
        debug_hbm_out(xmsp, "xmsp")
        P.finalize()
        return nc

    P.scope("P9_loc")
    N2X = ARA
    P.claim_prefix([("N2X", g, j) for g in range(NCH) for j in range(NT)],
                   {"QCg", "N1X"})
    P.claim_prefix([("XMT9", g, p) for g in range(NCH) for p in range(2)],
                   {"W8o"})

    def xmt9(g, j):
        c0 = 24576 + 4096 * (j % 2) + 512 * g
        return ARB[:, c0:c0 + 512]

    for g in range(NCH):
        dma("scalar", xmt9(g, 0),
            xmsp.ap()[128 * g:128 * g + 128, 0:512],
            reads=[("xmsp", g, 0), ("XMT9", g, 0)], writes=[("XMT9", g, 0)],
            chan="xt")
    for j in range(NT):
        if j + 1 < NT:
            for g in range(NCH):
                dma("scalar", xmt9(g, j + 1),
                    xmsp.ap()[128 * g:128 * g + 128, 512 * (j + 1):512 * (j + 2)],
                    reads=[("xmsp", g, j + 1), ("XMT9", g, (j + 1) % 2)],
                    writes=[("XMT9", g, (j + 1) % 2)], chan="xt")
        stats_sumsq(lambda g, j=j: (xmt9(g, j), [("XMT9", g, j % 2)]),
                    None, via_f32=False)
        rms_r(CONSTS[0:1, 2:3])
        for g in range(NCH):
            vop(lambda e, g=g, j=j: e.scalar_tensor_tensor(
                out=N2X[:, T * g + 512 * j:T * g + 512 * (j + 1)],
                in0=xmt9(g, j),
                scalar=N2W[:, g:g + 1], in1=LNB[:, 512:1024],
                op0=ALU.mult, op1=ALU.mult),
                [("XMT9", g, j % 2), ("N2W",), ("LNB", 512)], [("N2X", g, j)])
    P.claim_prefix([("WLK", g) for g in range(NCH)], {"YT8", "XMT8", "VCg"})
    load_w(lambda g: ARB[:, 16384 + 256 * g:16384 + 256 * (g + 1)], L_wk.ap(),
           256, "WLK")
    P.claim_prefix([("KCl", h, p, j, r) for h in range(4) for p in range(2)
                    for j in range(NT) for r in range(2)], {"KCg"})
    rope_proj_pass(N2X, lambda g, j: [("N2X", g, j)], NCH,
                   lambda g, part, pr: ARB[:, 16384 + 256 * g + 128 * part:
                                           16384 + 256 * g + 128 * part + 128],
                   "WLK", 1, C_COSL, C_SINL, True,
                   kc_writer(ARB, 0, "KCl", True))
    P.claim_prefix([("VCl", c) for c in range(NC)], {"WLK"})
    v_pass(N2X, lambda g, j: [("N2X", g, j)], NCH, L_wv, None, 4, 16384,
           "VCl")
    P.claim_prefix([("W9", g) for g in range(NCH)], {"XMT9"})
    load_w(lambda g: ARB[:, 24576 + 1024 * g:24576 + 1024 * (g + 1)], L_wq.ap(),
           1024, "W9")
    rope_proj_psum(N2X, lambda g, j: [("N2X", g, j)], NCH,
                   lambda g, part, pr: ARB[:, 24576 + 1024 * g + 128 * (4 * part + pr):
                                           24576 + 1024 * g + 128 * (4 * part + pr) + 128],
                   "W9", 4, C_COSL, C_SINL, True,
                   kc_writer(ARA, 0, "QCl", False,
                             extra_w_fn=lambda h, p, j: [("N2X", h // 2, j)]))
    attn_pass(ARA, ARB, 0, "KCl", True, 16384, "VCl", 4, CONSTS[:, 5:6], 256, "l")
    P.claim_prefix([("W9o", g) for g in range(NCH)], {"W9"})
    P.claim_prefix([("YT9", g) for g in range(NCH)], {"VCl"})
    P.claim_prefix([("XMT", g) for g in range(NCH)], {"VCl"})
    P.claim_prefix([("XSUM9", g) for g in range(NCH)], {"KCl"})
    XSUM9 = carve(ARB, 0, 8192, F32)
    load_w(lambda g: ARB[:, 24576 + 1024 * g:24576 + 1024 * (g + 1)], L_wo.ap(),
           1024, "W9o")
    for j in range(NT):
        for g in range(NCH):
            dma("sync", ARB[:, 16384 + 512 * g:16384 + 512 * (g + 1)],
                ysp.ap()[128 * g:128 * g + 128, 512 * j:512 * (j + 1)],
                reads=[("ysp", "l", g, j), ("YT9", g)], writes=[("YT9", g)],
                chan="xt")
            dma("scalar", ARB[:, 20480 + 512 * g:20480 + 512 * (g + 1)],
                xmsp.ap()[128 * g:128 * g + 128, 512 * j:512 * (j + 1)],
                reads=[("xmsp", g, j), ("XMT", g)], writes=[("XMT", g)],
                chan="xt2")
        for o in range(NCH):
            for g in range(NCH):
                mm(P_W[:, :],
                   ARB[:, 24576 + 1024 * g + 128 * o:24576 + 1024 * g + 128 * o + 128],
                   ARB[:, 16384 + 512 * g:16384 + 512 * (g + 1)],
                   start=(g == 0), stop=(g == NCH - 1),
                   reads=[("W9o", g), ("YT9", g)], writes=[("P_W",)])
            vop(lambda e, o=o, j=j: e.tensor_add(
                out=XSUM9[:, 512 * o:512 * (o + 1)], in0=P_W[:, :],
                in1=XB[:, T * o + 512 * j:T * o + 512 * (j + 1)]),
                [("P_W",), ("XB2", o, j)], [("XSUM9", o)])
            vop(lambda e, o=o: e.scalar_tensor_tensor(
                out=XSUM9[:, 512 * o:512 * (o + 1)],
                in0=ARB[:, 20480 + 512 * o:20480 + 512 * (o + 1)], scalar=1.0,
                in1=XSUM9[:, 512 * o:512 * (o + 1)], op0=ALU.mult, op1=ALU.add),
                [("XMT", o)], [("XSUM9", o)])
        stats_sumsq(lambda g: (XSUM9[:, 512 * g:512 * (g + 1)], [("XSUM9", g)]),
                    None, also_sum=True, via_f32=True)
        ln_finish(CONSTS[0:1, 1:2],
                  src_fn=lambda g: (XSUM9[:, 512 * g:512 * (g + 1)],
                                    [("XSUM9", g)]),
                  out_writer=lambda g, j=j: vop(
                      lambda e, g=g: e.tensor_mul(
                          out=XB[:, T * g + 512 * j:T * g + 512 * (j + 1)],
                          in0=WS[:, :], in1=LNB[:, 512:1024]),
                      [("WS",), ("LNB", 512)], [("XB3", g, j)]))

    if stop == "x3":
        debug_stream_out(lambda g, q: [("XB3", g, q)])
        P.finalize()
        return nc

    P.scope("P10_ffn")
    P.claim_prefix([("U2A", g) for g in range(32)] + [("V1A", g) for g in range(2)]
                   + [("U1A", g) for g in range(NCH)] + [("V2A", g) for g in range(2)]
                   + [("XSUM10", g) for g in range(NCH)],
                   {"N2X", "QCl"})
    U2A = carve(ARA, 0, 8192)
    V1A = carve(ARA, 8192, 8192)
    U1A = carve(ARA, 16384, 2048)
    V2A = carve(ARA, 18432, 2048)
    XSUM10 = carve(ARA, 20480, 8192, F32)
    load_w(lambda g: U2A[:, 256 * g:256 * (g + 1)], U2.ap(), 256, "U2A", nch=32)
    load_w(lambda g: V1A[:, 4096 * g:4096 * (g + 1)], V1.ap(), 4096, "V1A", nch=2)
    load_w(lambda g: U1A[:, 256 * g:256 * (g + 1)], U1.ap(), 256, "U1A")
    load_w(lambda g: V2A[:, 1024 * g:1024 * (g + 1)], V2.ap(), 1024, "V2A", nch=2)
    for j in range(NT):
        for m2, dstb, dkey in ((0, XSB, ("XSB",)), (1, XSQ, ("XSQ",))):
            for g in range(NCH):
                mm(P_QF[:, :],
                   U1A[:, 256 * g + 128 * m2:256 * g + 128 * m2 + 128],
                   XB[:, T * g + 512 * j:T * g + 512 * (j + 1)],
                   start=(g == 0), stop=(g == NCH - 1),
                   reads=[("U1A", g), ("XB3", g, j)], writes=[("P_QF",)])
            vop(lambda e, dstb=dstb: e.tensor_copy(out=dstb[:, :], in_=P_QF[:, :]),
                [("P_QF",)], [dkey])
        def v1mm(ff):
            hb, hk = (P_KV, ("P_KV",)) if ff % 2 == 0 else (P_QF, ("P_QF",))
            mm(hb[:, :], V1A[:, 128 * ff:128 * ff + 128], XSB[:, :],
               start=True, stop=False,
               reads=[("V1A", 0), ("XSB",)], writes=[hk])
            mm(hb[:, :], V1A[:, 4096 + 128 * ff:4096 + 128 * ff + 128], XSQ[:, :],
               start=False, stop=True,
               reads=[("V1A", 1), ("XSQ",)], writes=[hk])

        v1mm(0)
        for ff in range(32):
            hb, hk = (P_KV, ("P_KV",)) if ff % 2 == 0 else (P_QF, ("P_QF",))
            pt, ptk = (PT0, ("PT", 0)) if ff % 2 == 0 else (PT1, ("PT", 1))
            if ff + 1 < 32:
                v1mm(ff + 1)
            P.op("scalar", lambda e, ff=ff, hb=hb, pt=pt: e.activation(
                out=pt[:, :], in_=hb[:, :], func=AF.Gelu_apprx_tanh,
                bias=B1FS[:, ff:ff + 1], scale=1.0),
                reads=[hk, ("B1FS",)], writes=[ptk])
            mm(P_S[:, :], U2A[:, 256 * ff:256 * ff + 128], pt[:, :],
               start=(ff == 0), stop=(ff == 31),
               reads=[("U2A", ff), ptk], writes=[("P_S",)])
            mm(P_Y[:, :], U2A[:, 256 * ff + 128:256 * ff + 256], pt[:, :],
               start=(ff == 0), stop=(ff == 31),
               reads=[("U2A", ff), ptk], writes=[("P_Y",)])
        vop(lambda e: e.tensor_copy(out=PT1[:, :], in_=P_S[:, :]),
            [("P_S",)], [("PT", 1)])
        vop(lambda e: e.tensor_copy(out=XSB[:, :], in_=P_Y[:, :]),
            [("P_Y",)], [("XSB",)])
        for o in range(NCH):
            mm(P_W[:, :], V2A[:, 128 * o:128 * o + 128], PT1[:, :],
               start=True, stop=False,
               reads=[("V2A", 0), ("PT", 1)], writes=[("P_W",)])
            mm(P_W[:, :], V2A[:, 1024 + 128 * o:1024 + 128 * o + 128], XSB[:, :],
               start=False, stop=True,
               reads=[("V2A", 1), ("XSB",)], writes=[("P_W",)])
            vop(lambda e, o=o, j=j: e.scalar_tensor_tensor(
                out=XSUM10[:, 512 * o:512 * (o + 1)], in0=P_W[:, :],
                scalar=B2FS[:, o:o + 1],
                in1=XB[:, T * o + 512 * j:T * o + 512 * (j + 1)],
                op0=ALU.add, op1=ALU.add),
                [("P_W",), ("B2FS",), ("XB3", o, j)], [("XSUM10", o)])
        stats_sumsq(lambda g: (XSUM10[:, 512 * g:512 * (g + 1)], [("XSUM10", g)]),
                    None, also_sum=True, via_f32=True)
        ln_finish(CONSTS[0:1, 1:2],
                  src_fn=lambda g: (XSUM10[:, 512 * g:512 * (g + 1)],
                                    [("XSUM10", g)]),
                  out_writer=lambda g, j=j: (
                      vop(lambda e: e.tensor_mul(out=BCV[:, :], in0=WS[:, :],
                                                 in1=LNB[:, 512:1024]),
                          [("WS",), ("LNB", 512)], [("BCV", "a")]),
                      dma("sync", outT.ap()[128 * g:128 * g + 128,
                                            512 * j:512 * (j + 1)], BCV[:, :],
                          reads=[("BCV", "a")], writes=[("OUT", g, j)],
                          chan="out")))
    P.op("sync", lambda e: e.nop(nofuse=True),
         reads=[("OUT", g, j) for g in range(NCH) for j in range(NT)], writes=())

    P.finalize()
    return nc


def _prep_core_inputs(core, I):
    m = {}
    m["x0T"] = np.ascontiguousarray(I["L0_attn_hidden_states"][core].T)
    m["L0A_WqTp"] = pad_qk_feat_weight(np.ascontiguousarray(I["L0_attn_Wq"].T))
    m["L0A_WkTp"] = pad_qk_feat_weight(np.ascontiguousarray(I["L0_attn_Wk"].T))
    m["L0A_WkT"] = np.ascontiguousarray(I["L0_attn_Wk"].T)
    m["L0A_WvT"] = np.ascontiguousarray(I["L0_attn_Wv"].T)
    m["L0A_WoT"] = np.ascontiguousarray(I["L0_attn_Wo"].T)
    m["L0A_g"] = np.ascontiguousarray(I["L0_attn_g_weight"].reshape(64, 1))
    ph = perm_halfsplit()
    pi = perm_interleave()
    pi4 = np.concatenate([pi[:128] - 0, pi[512:640]])
    p4 = np.empty(256, np.int64)
    for a in range(4):
        for i in range(32):
            p4[32 * a + i] = 64 * a + 2 * i
            p4[128 + 32 * a + i] = 64 * a + 2 * i + 1
    m["F_sc1"] = np.ascontiguousarray(I["L0_ffn_scale1"].reshape(8, 128).T)
    wqp = I["L0_ffn_wq"][:, ph]
    blk = wqp.reshape(8, 128, 8, 128)
    S = blk.transpose(2, 0, 1, 3).reshape(8, 2, 4, 128, 128)
    m["F_wqs"] = np.ascontiguousarray(
        S.transpose(0, 1, 3, 2, 4).reshape(8, 2, 128, 512))
    m["F_wk"] = np.ascontiguousarray(I["L0_ffn_wk"][:, ph])
    m["F_wv"] = np.ascontiguousarray(I["L0_ffn_wv"])
    m["F_wo"] = np.ascontiguousarray(I["L0_ffn_wo"])
    m["F_sc2"] = np.ascontiguousarray(I["L0_ffn_scale2"].reshape(8, 128).T)
    m["F_rw"] = np.ascontiguousarray(I["L0_ffn_router_w"])
    m["F_gum"] = _gumbel_noise()[core]
    m["F_w1"] = np.ascontiguousarray(I["L0_ffn_w1"])
    m["F_w2"] = np.ascontiguousarray(I["L0_ffn_w2"])
    m["F_wout"] = np.ascontiguousarray(I["L0_ffn_wout"])
    m["F_b1"] = np.ascontiguousarray(I["L0_ffn_b1"])
    m["F_b2"] = np.ascontiguousarray(I["L0_ffn_b2"])
    m["F_bout"] = np.ascontiguousarray(I["L0_ffn_bout"])
    m["G_n1"] = np.ascontiguousarray(I["L1_attn_norm1_w"].reshape(8, 128).T)
    m["G_wq"] = np.ascontiguousarray(I["L1_attn_g_wq"][:, pi])
    m["G_kvd"] = np.ascontiguousarray(I["L1_attn_g_kv_down"])
    m["G_kup"] = np.ascontiguousarray(I["L1_attn_g_k_up"][:, p4])
    m["G_vup"] = np.ascontiguousarray(I["L1_attn_g_v_up"])
    m["G_wo"] = np.ascontiguousarray(I["L1_attn_g_wo"])
    m["G_n2"] = np.ascontiguousarray(I["L1_attn_norm2_w"].reshape(8, 128).T)
    m["L_wq"] = np.ascontiguousarray(I["L1_attn_l_wq"][:, pi])
    m["L_wk"] = np.ascontiguousarray(I["L1_attn_l_wk"][:, p4])
    m["L_wv"] = np.ascontiguousarray(I["L1_attn_l_wv"])
    m["L_wo"] = np.ascontiguousarray(I["L1_attn_l_wo"])
    m["U1"] = np.ascontiguousarray(I["L1_ffn_U1"])
    m["V1"] = np.ascontiguousarray(I["L1_ffn_V1"])
    m["B1F"] = np.ascontiguousarray(I["L1_ffn_b1"].reshape(32, 128).T)
    m["U2"] = np.ascontiguousarray(I["L1_ffn_U2"])
    m["V2"] = np.ascontiguousarray(I["L1_ffn_V2"])
    m["B2F"] = np.ascontiguousarray(I["L1_ffn_b2"].reshape(8, 128).T)
    return {k: np.asarray(v, dtype=np.float32) for k, v in m.items()}


_GUMBEL = None


def _gumbel_noise():
    global _GUMBEL
    if _GUMBEL is None:
        import jax
        cpus = jax.devices("cpu")
        with jax.default_device(cpus[0]):
            g = jax.random.gumbel(jax.random.key(42), (8, T, 8)) * 0.05
        _GUMBEL = np.asarray(g, dtype=np.float32)
    return _GUMBEL


def kernel(**inputs):
    I = {k: np.asarray(v) for k, v in inputs.items()}
    nc = build_program(stop=os.environ.get("BASS_MK_STOP", "all"))
    in_maps = [_prep_core_inputs(c, I) for c in range(8)]
    res = run_bass_kernel_spmd(nc, in_maps, core_ids=list(range(8)))
    return np.stack([np.ascontiguousarray(r["outT"].T) for r in res.results])
